# version 03
# baseline (speedup 1.0000x reference)
import sys
sys.path.insert(0, '/opt/trn_rl_repo')
import numpy as np
import concourse.bass as bass
import concourse.mybir as mybir
from concourse.bass_utils import run_bass_kernel_spmd

F32 = mybir.dt.float32
BF16 = mybir.dt.bfloat16
AF = mybir.ActivationFunctionType
ALU = mybir.AluOpType

D, DI, NS, DTR, KC, E, DFF = 1024, 2048, 16, 64, 4, 8, 2048
T = 4096
TC = 512
NCH = T // TC
KB, IB, FB = 8, 16, 16
TB = 2 * TC
W1 = TB + 1
TF = 256
EPS = 1e-5
ENGS = ("tensor", "vector", "scalar", "gpsimd", "sync")


class P:
    def __init__(self, nc, block):
        self.nc, self.block = nc, block
        self.q = {e: [] for e in ENGS}
        self.sem, self.cnt = {}, {}

    def new_sem(self, name):
        self.sem[name] = self.nc.alloc_semaphore(name=name)
        self.cnt[name] = 0

    def flush(self):
        for e in ENGS:
            fns = self.q[e]
            if fns:
                def runner(ex, fns=tuple(fns)):
                    for f in fns:
                        f(ex)
                getattr(self.block, e)(runner)
                self.q[e] = []

    def op(self, eng, sname, fn):
        by = 16 if sname.startswith("q") else 1
        self.cnt[sname] += by
        s = self.sem[sname]

        def g(ex, fn=fn, s=s, by=by):
            fn(ex).then_inc(s, by)
        self.q[eng].append(g)
        return self.cnt[sname]

    def val(self, name):
        return self.cnt[name]

    def wait(self, eng, sname, val=None):
        v = self.cnt[sname] if val is None else val
        if v > 0:
            s = self.sem[sname]
            self.q[eng].append(lambda ex, s=s, v=v: ex.wait_ge(s, v))


def build(debug=False):
    nc = bass.Bass("TRN2", target_bir_lowering=False)
    dr, sb, ps = {}, {}, {}

    def din(name, shape, dt=F32):
        dr[name] = nc.dram_tensor(name, shape, dt, kind="ExternalInput").ap()

    def dscr(name, shape, dt):
        dr[name] = nc.dram_tensor(name, shape, dt, kind="Internal").ap()

    din("xTm", [D, T]); din("xTp", [D, T]); din("maskv", [128, 1])
    din("W_in_T", [D, 2 * DI]); din("b_in", [128, 32])
    din("lmg", [128, KB]); din("lmb", [128, KB])
    din("leg", [128, KB]); din("leb", [128, KB])
    din("convw", [128, IB * KC]); din("convb", [128, IB])
    din("W_x_T", [128, IB * 96])
    din("W_dt_T", [DTR, DI]); din("dtb", [128, IB])
    din("A_log", [128, IB * NS]); din("Dsk", [128, IB])
    din("W_out_T", [DI, D])
    din("router_T", [128, KB * E])
    din("w1_T", [E, D, DFF]); din("b1", [128, E * FB])
    din("w2_T", [E, DFF, D]); din("b2r", [8, D])
    din("W_r_T", [D, D]); din("W_k_T", [D, D]); din("W_v_T", [D, D])
    din("mur", [128, KB]); din("muk", [128, KB])
    dr["outT"] = nc.dram_tensor("outT", [D, T], F32, kind="ExternalOutput").ap()

    dscr("sWin", [32, 128, KB * 128], BF16)
    dscr("sWout", [KB, 128, IB * 128], BF16)
    dscr("sW1", [E, FB, 128, KB * 128], BF16)
    dscr("sW2", [KB, E, 128, FB * 128], BF16)
    dscr("sWr", [2, KB, 128, KB * 128], BF16)
    dscr("sWk", [2, KB, 128, KB * 128], BF16)
    dscr("sWv", [KB, 128, KB * 128], BF16)
    scrkind = "ExternalOutput" if debug else "Internal"
    dr["s_yg"] = nc.dram_tensor("s_yg", [IB, 128, 2 * T], BF16, kind=scrkind).ap()
    dr["s_sz"] = nc.dram_tensor("s_sz", [IB, 128, 2 * T], BF16, kind=scrkind).ap()
    dr["s_xcD"] = nc.dram_tensor("s_xcD", [IB, 128, 2 * T], BF16, kind=scrkind).ap()
    if debug:
        dr["s_dbgA"] = nc.dram_tensor("s_dbgA", [128, 2 * T], F32, kind="ExternalOutput").ap()
        dr["s_dbgP"] = nc.dram_tensor("s_dbgP", [128, 2 * T], F32, kind="ExternalOutput").ap()
        dr["s_dbgD"] = nc.dram_tensor("s_dbgD", [128, 2 * T], F32, kind="ExternalOutput").ap()
        dr["s_dbgU"] = nc.dram_tensor("s_dbgU", [128, 2 * T], F32, kind="ExternalOutput").ap()
        dr["s_dbgH"] = nc.dram_tensor("s_dbgH", [128, 2 * T], F32, kind="ExternalOutput").ap()
        for nm in ["s_dbgHA", "s_dbgXM", "s_dbgAC", "s_dbgX2", "s_dbgXS"]:
            dr[nm] = nc.dram_tensor(nm, [KB, 128, 2 * T], BF16, kind="ExternalOutput").ap()
        dr["s_dbgW"] = nc.dram_tensor("s_dbgW", [1, 8, 2 * T], F32, kind="ExternalOutput").ap()
        dr["s_dbgLG"] = nc.dram_tensor("s_dbgLG", [128, (2 * T) // 128 * 8], F32,
                                       kind="ExternalOutput").ap()
        dr["s_dbgRT"] = nc.dram_tensor("s_dbgRT", [128, KB * E], BF16,
                                       kind="ExternalOutput").ap()
        dr["s_dbgBR"] = nc.dram_tensor("s_dbgBR", [1, 8], BF16,
                                       kind="ExternalOutput").ap()
        dr["s_dbgT2"] = nc.dram_tensor("s_dbgT2", [128, (2 * T) // 128 * 24], F32,
                                       kind="ExternalOutput").ap()

    def sbuf(name, shape, dt):
        sb[name] = nc.alloc_sbuf_tensor(name, shape, dt).ap()

    sbuf("c_lmb", [128, KB], F32); sbuf("c_leg", [128, KB], F32)
    sbuf("c_leb", [128, KB], F32)
    sbuf("c_convw", [128, IB * KC], F32); sbuf("c_convb", [128, IB], F32)
    sbuf("c_dtb", [128, IB], F32); sbuf("c_A", [128, IB * NS], F32)
    sbuf("c_D", [128, IB], F32)
    sbuf("c_mur", [128, KB], F32); sbuf("c_muk", [128, KB], F32)
    sbuf("c_mask", [128, 1], F32)
    sbuf("c_Wx", [128, IB * 96], BF16)
    sbuf("c_Wdt", [DTR, DI], BF16)
    sbuf("c_rout", [128, KB * E], BF16)
    sbuf("c_ones", [128, 1], BF16); sbuf("c_ones32", [128, 1], F32)
    sbuf("c_eps", [128, 1], F32); sbuf("c_neg1", [128, 1], F32)
    sbuf("c_nhalf", [128, 1], F32)
    sbuf("c_onesr", [1, 128], BF16); sbuf("c_onesr32", [1, 128], F32)
    sbuf("c_sin", [1, 2 * DI], BF16)
    sbuf("c_swr", [1, 2 * D], BF16); sbuf("c_swk", [1, 2 * D], BF16)
    sbuf("c_bein", [128, 32], F32)
    sbuf("c_bew1", [128, E * FB], F32)
    sbuf("c_b2", [8, D], BF16)
    sbuf("c_ber", [1, 8], BF16)
    sbuf("st_h", [128, IB * NS], F32)
    sbuf("st_hal", [128, IB * (KC - 1)], F32)
    sbuf("st_x1p", [128, KB], BF16)
    sbuf("st_p2", [1, 1], F32)
    sbuf("wA", [128, IB * 128], BF16)
    sbuf("wB", [128, IB * 128], BF16)
    sbuf("k_yg", [128, 2 * TC], BF16)
    sbuf("k_x", [128, TB], F32)
    sbuf("k_rows", [1, 8 * TC], F32)
    sbuf("k_rb", [1, TB + 2], BF16)
    sbuf("k_w", [8, 2 * TC], F32)
    sbuf("k_wb", [8, 2 * TC], BF16)
    sbuf("k_tm", [128, 8], F32)
    sbuf("k_tc", [128, 8], F32)

    ARENA = 120 * 1024
    arena_t = nc.alloc_sbuf_tensor("arena", [128, ARENA], mybir.dt.uint8)
    arena_f32 = arena_t.ap().bitcast(F32)
    arena_bf = arena_t.ap().bitcast(BF16)
    _off = {"f": 0, "k": 0}

    def view(group, name, cols, dt):
        esz = mybir.dt.size(dt)
        off = (_off[group] + 63) // 64 * 64
        base = arena_f32 if dt == F32 else arena_bf
        sb[name] = base[:, off // esz: off // esz + cols]
        _off[group] = off + cols * esz
        assert _off[group] <= ARENA, (name, _off[group])

    view("f", "f_xk", 2 * TC, F32)
    view("f", "f_rows", 8 * TC, F32)
    view("f", "f_rb", TC, BF16)
    view("f", "f_xr1", KB * TC, BF16)
    view("f", "f_xi", TC + KC - 1, F32)
    view("f", "f_cva", TC, F32)
    view("f", "f_cvb", TC, F32)
    view("f", "f_xc", IB * TC, BF16)
    view("f", "f_proj", TC, F32)
    view("f", "f_dtr", TC, BF16)
    view("f", "f_dt", 2 * TC, F32)
    view("f", "f_dtx", 2 * TC, BF16)
    view("f", "f_Br", NS * TC, BF16)
    view("f", "f_Cr", NS * TC, BF16)
    view("f", "f_BC", TC, BF16)
    view("f", "f_dA", 2 * (TC + 2), F32)
    view("f", "f_u", TC + 2, BF16)
    view("f", "f_h", TC + 2, BF16)
    view("f", "f_ya", TC, F32)
    view("f", "f_t0", TC, BF16)
    view("f", "f_yg", TC, BF16)
    view("k", "k_ha", KB * TB, BF16)
    view("k", "k_xm", KB * TB, BF16)
    view("k", "k_x2a", KB * TB, BF16)
    view("k", "k_acc", KB * TB, BF16)
    view("k", "k_xs1", KB * W1, BF16)
    view("k", "k_h1", FB * TC, BF16)
    view("k", "k_r", KB * TF, BF16)
    view("k", "k_ka", KB * TF, BF16)
    view("k", "k_out", KB * TF, F32)
    view("k", "k_wr", TC, BF16)
    view("k", "k_dl", TC, F32)
    view("k", "k_t0", TC, BF16)
    view("k", "k_t1", TC, F32)
    view("k", "k_w2s", TF, F32)
    sb["f_rb"] = sb["f_rb"][0:1, :]
    sb["u_f32"] = sb["f_xc"].bitcast(F32)[:, 0:2 * DI]
    sb["u_bf"] = sb["f_Br"][:, 0:2 * DI]

    import ml_dtypes
    oh32 = np.zeros((32, 32 * 128), np.float32)
    for r in range(32):
        oh32[r, r * 128:(r + 1) * 128] = 1.0
    dr["i_oh32"] = nc.inline_tensor(oh32.astype(ml_dtypes.bfloat16), name="i_oh32").ap()
    dr["i_id"] = nc.inline_tensor(np.eye(128, dtype=ml_dtypes.bfloat16), name="i_id").ap()
    oh8 = np.zeros((8, 8 * 128), np.float32)
    for r in range(8):
        oh8[r, r * 128:(r + 1) * 128] = 1.0
    dr["i_oh8"] = nc.inline_tensor(oh8.astype(ml_dtypes.bfloat16), name="i_oh8").ap()
    sbuf("c_oh32", [32, 32 * 128], BF16)
    sbuf("c_id", [128, 128], BF16)
    sbuf("c_oh8", [8, 8 * 128], BF16)

    def psum(name, shape):
        ps[name] = nc.alloc_psum_tensor(name, shape, F32).ap()

    psum("pA", [128, 512]); psum("pB", [128, 512]); psum("pR", [128, 512])
    psum("pT", [128, 512]); psum("pU", [128, 512]); psum("pV", [128, 512])
    psum("pW", [128, 512]); psum("pX", [128, 512])

    with nc.Block() as block:
        p = P(nc, block)
        for s in ["qs", "qg", "qc", "T", "V", "S"]:
            p.new_sem(s)
        _setup(p, dr, sb, ps)
        for ch in range(NCH):
            _front(p, dr, sb, ps, ch, True)
            _scan(p, dr, sb, ps, ch, True)
        _back(p, dr, sb, ps, NCH - 2, True)
        for ch in range(0, NCH, 2):
            _front(p, dr, sb, ps, ch, False)
            _scan(p, dr, sb, ps, ch, False)
            _front(p, dr, sb, ps, ch + 1, False)
            _scan(p, dr, sb, ps, ch + 1, False)
            _back(p, dr, sb, ps, ch, False)
        p.wait("sync", "qs"); p.wait("sync", "qg"); p.wait("sync", "qc")
        p.flush()
    return nc


def _setup(p, dr, sb, ps):
    for dst, src in [("c_lmb", "lmb"), ("c_leg", "leg"), ("c_leb", "leb"),
                     ("c_convw", "convw"), ("c_convb", "convb"),
                     ("c_dtb", "dtb"), ("c_D", "Dsk"), ("c_mur", "mur"),
                     ("c_muk", "muk"), ("c_mask", "maskv"),
                     ("c_bein", "b_in"), ("c_bew1", "b1")]:
        p.op("sync", "qs", lambda s, d=sb[dst], x=dr[src]: s.dma_start(out=d, in_=x))
    for dst, src in [("c_Wx", "W_x_T"), ("c_Wdt", "W_dt_T"),
                     ("c_rout", "router_T"), ("c_b2", "b2r")]:
        p.op("gpsimd", "qg", lambda g, d=sb[dst], x=dr[src]: g.dma_start(out=d, in_=x))
    p.op("sync", "qs", lambda s: s.dma_start(out=sb["c_A"], in_=dr["A_log"]))
    p.wait("scalar", "qs")
    p.op("scalar", "S", lambda s: s.activation(sb["c_A"], sb["c_A"], AF.Exp))
    p.op("vector", "V", lambda v: v.memset(sb["c_ones32"], 1.0))
    p.op("vector", "V", lambda v: v.memset(sb["c_eps"], EPS))
    p.op("vector", "V", lambda v: v.memset(sb["c_neg1"], -1.0))
    p.op("vector", "V", lambda v: v.memset(sb["c_nhalf"], -0.5))
    p.op("vector", "V", lambda v: v.memset(sb["c_onesr32"], 1.0))
    p.op("vector", "V", lambda v: v.tensor_copy(sb["c_ones"], sb["c_ones32"]))
    p.op("vector", "V", lambda v: v.tensor_copy(sb["c_onesr"], sb["c_onesr32"]))
    for dst, srcn in [("c_oh32", "i_oh32"), ("c_id", "i_id"), ("c_oh8", "i_oh8")]:
        p.op("sync", "qs", lambda s, d=sb[dst], x=dr[srcn]: s.dma_start(out=d, in_=x))
    for z in ["st_h", "st_hal", "st_x1p", "st_p2"]:
        p.op("vector", "V", lambda v, z=z: v.memset(sb[z], 0.0))
    p.op("vector", "V", lambda v: v.memset(sb["f_dA"][:, 0:2], 0.0))
    p.op("vector", "V", lambda v: v.memset(sb["f_dA"][:, TC + 2:TC + 4], 0.0))
    p.op("vector", "V", lambda v: v.memset(sb["f_u"][:, 0:1], 0.0))
    wv = p.val("V")
    p.wait("scalar", "V", wv)
    p.op("scalar", "S", lambda s: s.activation(sb["c_A"], sb["c_A"], AF.Copy,
                                               scale=sb["c_neg1"][:, 0:1]))
    p.flush()

    p.wait("vector", "qg")
    p.wait("vector", "qs")
    for k in range(KB):
        p.op("vector", "V", lambda v, k=k:
             v.tensor_scalar(sb["c_rout"][:, k * E:(k + 1) * E],
                             sb["c_rout"][:, k * E:(k + 1) * E],
                             sb["c_leg"][:, k:k + 1], None, ALU.mult))
    p.op("sync", "qs", lambda s: s.dma_start(out=sb["u_f32"][:, 0:KB * E],
                                             in_=dr["router_T"]))
    p.wait("tensor", "qs")
    p.wait("tensor", "V")
    for k in range(KB):
        p.op("tensor", "T", lambda t, k=k:
             t.matmul(ps["pV"][0:1, 0:8], sb["c_leb"][:, k:k + 1],
                      sb["u_f32"][:, k * E:(k + 1) * E],
                      start=(k == 0), stop=(k == KB - 1)))
    wt = p.val("T")
    p.wait("vector", "T", wt)
    p.op("vector", "V", lambda v: v.tensor_copy(sb["c_ber"], ps["pV"][0:1, 0:8]))
    p.flush()

    FR = lambda i: sb["f_rows"][0:1, i * TC:(i + 1) * TC]
    KRr = lambda i: sb["k_rows"][0:1, i * TC:(i + 1) * TC]

    def family(src3, kb, M, nmb, gain, lnb, sumdst, befdst, store_fn):
        nseg = (M + 511) // 512
        for k in range(kb):
            fb, bb = sb["u_f32"][:, 0:M], sb["u_bf"][:, 0:M]
            p.wait("sync", "V"); p.wait("sync", "T")
            p.op("sync", "qs", lambda s, a=src3(k), fb=fb: s.dma_start(out=fb, in_=a))
            p.wait("vector", "qs")
            if gain is not None:
                p.op("vector", "V", lambda v, fb=fb, bb=bb, g=gain, k=k:
                     v.tensor_scalar(bb, fb, g[:, k:k + 1], None, ALU.mult))
            else:
                p.op("vector", "V", lambda v, fb=fb, bb=bb: v.tensor_copy(bb, fb))
            wv = p.val("V")
            p.wait("tensor", "V", wv)
            if lnb is not None:
                for r in range(nseg):
                    so, sw = r * 512, min(512, M - r * 512)
                    if k == 0:
                        p.op("vector", "V", lambda v, r=r, sw=sw:
                             v.memset(KRr(r)[:, 0:sw], 0.0))
                    wvz = p.val("V")
                    p.wait("tensor", "V", wvz)
                    p.op("tensor", "T", lambda t, fb=fb, so=so, sw=sw, lb=lnb, k=k:
                         t.matmul(ps["pV"][0:1, 0:sw], lb[:, k:k + 1],
                                  fb[:, so:so + sw], start=True, stop=True))
                    wt = p.val("T")
                    p.wait("vector", "T", wt)
                    p.op("vector", "V", lambda v, r=r, sw=sw:
                         v.tensor_tensor(KRr(r)[:, 0:sw], KRr(r)[:, 0:sw],
                                         ps["pV"][0:1, 0:sw], ALU.add))
            if sumdst is not None:
                for r in range(nseg):
                    so, sw = r * 512, min(512, M - r * 512)
                    if k == 0:
                        p.op("vector", "V", lambda v, r=r, sw=sw:
                             v.memset(FR(r)[:, 0:sw], 0.0))
                    wvz = p.val("V")
                    p.wait("tensor", "V", wvz)
                    p.op("tensor", "T", lambda t, bb=bb, so=so, sw=sw:
                         t.matmul(ps["pB"][0:1, 0:sw], sb["c_ones"],
                                  bb[:, so:so + sw], start=True, stop=True))
                    wt = p.val("T")
                    p.wait("vector", "T", wt)
                    p.op("vector", "V", lambda v, r=r, sw=sw:
                         v.tensor_tensor(FR(r)[:, 0:sw], FR(r)[:, 0:sw],
                                         ps["pB"][0:1, 0:sw], ALU.add))
                    if k == kb - 1:
                        wvv = p.val("V")
                        p.wait("scalar", "V", wvv)
                        p.op("scalar", "S", lambda s, so=so, sw=sw, r=r:
                             sumdst(s, so, sw, r))
            wv2 = p.val("V")
            p.wait("sync", "V", wv2)
            for mb in range(nmb):
                dsl, c0 = store_fn(k, mb)
                p.op("sync", "qs", lambda s, d=dsl, bb=bb, c0=c0:
                     s.dma_start(out=d, in_=bb[:, c0:c0 + 128]))
            p.wait("vector", "qs")
            p.wait("vector", "T")
            p.flush()
        if lnb is not None:
            wv = p.val("V")
            p.wait("tensor", "V", wv)
            nmb2 = M // 128
            for m in range(nmb2):
                r, c = divmod(m * 128, TC)
                p.op("tensor", "T", lambda t, r=r, c=c, m=m:
                     t.matmul(ps["pV"][:, m:m + 1],
                              KRr(r)[:, c:c + 128], sb["c_onesr32"][0:1, 0:1],
                              start=True, stop=True))
            wt = p.val("T")
            p.wait("vector", "T", wt)
            p.op("vector", "V", lambda v, nmb2=nmb2, befdst=befdst:
                 v.tensor_tensor(befdst[:, 0:nmb2], befdst[:, 0:nmb2],
                                 ps["pV"][:, 0:nmb2], ALU.add))
            p.flush()

    lmg = p.nc.alloc_sbuf_tensor("tmp_lmg", [128, KB], F32).ap()
    p.op("sync", "qs", lambda s: s.dma_start(out=lmg, in_=dr["lmg"]))
    p.wait("vector", "qs")
    p.flush()

    def sum_in(s, so, sw, r):
        return s.activation(sb["c_sin"][0:1, so:so + sw], FR(r)[:, 0:sw], AF.Copy)

    family(lambda k: dr["W_in_T"][k * 128:(k + 1) * 128, :], KB, 2 * DI, 32,
           lmg, sb["c_lmb"], sum_in, sb["c_bein"],
           lambda k, mb: (dr["sWin"][mb, :, k * 128:(k + 1) * 128], mb * 128))
    family(lambda k: dr["W_out_T"][k * 128:(k + 1) * 128, :], IB, D, KB,
           None, None, None, None,
           lambda k, mb: (dr["sWout"][mb, :, k * 128:(k + 1) * 128], mb * 128))
    for e in range(E):
        family(lambda k, e=e: dr["w1_T"][e, k * 128:(k + 1) * 128, :], KB, DFF, FB,
               sb["c_leg"], sb["c_leb"], None,
               sb["c_bew1"][:, e * FB:(e + 1) * FB],
               lambda k, mb, e=e: (dr["sW1"][e, mb, :, k * 128:(k + 1) * 128], mb * 128))
    for e in range(E):
        family(lambda k, e=e: dr["w2_T"][e, k * 128:(k + 1) * 128, :], FB, D, KB,
               None, None, None, None,
               lambda k, mb, e=e: (dr["sW2"][mb, e, :, k * 128:(k + 1) * 128], mb * 128))

    for wsrc, scr, mucol, sumr in [("W_r_T", "sWr", "c_mur", "c_swr"),
                                   ("W_k_T", "sWk", "c_muk", "c_swk")]:
        for var in range(2):
            for k in range(KB):
                fb, bb = sb["u_f32"][:, 0:D], sb["u_bf"][:, 0:D]
                tmp = sb["u_f32"][:, D:2 * D]
                p.wait("sync", "V"); p.wait("sync", "T")
                p.op("sync", "qs", lambda s, k=k, fb=fb, w=wsrc:
                     s.dma_start(out=fb, in_=dr[w][k * 128:(k + 1) * 128, :]))
                p.wait("vector", "qs")
                if var == 0:
                    p.op("vector", "V", lambda v, fb=fb, bb=bb, k=k, mc=mucol:
                         v.tensor_scalar(bb, fb, sb[mc][:, k:k + 1], None, ALU.mult))
                else:
                    p.op("vector", "V", lambda v, fb=fb, k=k, mc=mucol, tmp=tmp:
                         v.tensor_scalar(tmp, fb, sb[mc][:, k:k + 1], None, ALU.mult))
                    p.op("vector", "V", lambda v, fb=fb, bb=bb, tmp=tmp:
                         v.tensor_tensor(bb, fb, tmp, ALU.subtract))
                wv = p.val("V")
                p.wait("tensor", "V", wv)
                for r in range(2):
                    so = r * 512
                    if k == 0:
                        p.op("vector", "V", lambda v, r=r: v.memset(FR(r), 0.0))
                    wvz = p.val("V")
                    p.wait("tensor", "V", wvz)
                    p.op("tensor", "T", lambda t, bb=bb, so=so:
                         t.matmul(ps["pB"][0:1, 0:512], sb["c_ones"],
                                  bb[:, so:so + 512], start=True, stop=True))
                    wt = p.val("T")
                    p.wait("vector", "T", wt)
                    p.op("vector", "V", lambda v, r=r:
                         v.tensor_tensor(FR(r), FR(r), ps["pB"][0:1, 0:512], ALU.add))
                    if k == KB - 1:
                        wvv = p.val("V")
                        p.wait("scalar", "V", wvv)
                        p.op("scalar", "S", lambda s, so=so, r=r, sumr=sumr, var=var:
                             s.activation(sb[sumr][0:1, var * D + so:var * D + so + 512],
                                          FR(r), AF.Copy))
                wv2 = p.val("V")
                p.wait("sync", "V", wv2)
                for mb in range(KB):
                    p.op("sync", "qs", lambda s, scr=scr, var=var, mb=mb, k=k, bb=bb:
                         s.dma_start(out=dr[scr][var, mb, :, k * 128:(k + 1) * 128],
                                     in_=bb[:, mb * 128:(mb + 1) * 128]))
                p.wait("vector", "qs")
                p.wait("vector", "T")
                p.flush()
    family(lambda k: dr["W_v_T"][k * 128:(k + 1) * 128, :], KB, D, KB,
           None, None, None, None,
           lambda k, mb: (dr["sWv"][mb, :, k * 128:(k + 1) * 128], mb * 128))

    for e in ENGS:
        p.wait(e, "qs"); p.wait(e, "qg"); p.wait(e, "V"); p.wait(e, "S"); p.wait(e, "T")
    p.flush()


def _front(p, dr, sb, ps, ch, prefix):
    t0 = ch * TC
    half = 0 if prefix else 1
    xsrc = dr["xTp"] if prefix else dr["xTm"]
    do_z = (not prefix) or ch >= NCH - 2
    FR = lambda i: sb["f_rows"][0:1, i * TC:(i + 1) * TC]

    for k in range(KB):
        par = k % 2
        xk = sb["f_xk"][:, par * TC:(par + 1) * TC]
        p.wait("sync", "V"); p.wait("sync", "T"); p.wait("sync", "S")
        p.op("sync", "qs", lambda s, xk=xk, k=k:
             s.dma_start(out=xk, in_=xsrc[k * 128:(k + 1) * 128, t0:t0 + TC]))
        p.wait("scalar", "qs")
        p.wait("scalar", "T")
        p.op("scalar", "S", lambda s, xk=xk: s.activation(sb["f_cva"], xk, AF.Square))
        ws = p.val("S")
        p.wait("tensor", "qs")
        p.wait("tensor", "S", ws)
        p.op("tensor", "T", lambda t, xk=xk, k=k:
             t.matmul(ps["pA"][0:1, 0:TC], sb["c_ones32"], xk,
                      start=(k == 0), stop=(k == KB - 1)))
        p.op("tensor", "T", lambda t, k=k:
             t.matmul(ps["pA"][32:33, 0:TC], sb["c_ones32"], sb["f_cva"],
                      start=(k == 0), stop=(k == KB - 1)))
    wt = p.val("T")
    p.wait("vector", "T", wt)
    p.op("vector", "V", lambda v: v.tensor_scalar(FR(0), ps["pA"][0:1, 0:TC],
                                                  1.0 / D, None, ALU.mult))
    p.op("vector", "V", lambda v: v.tensor_scalar(FR(1), ps["pA"][32:33, 0:TC],
                                                  1.0 / D, None, ALU.mult))
    p.op("vector", "V", lambda v: v.tensor_tensor(FR(2), FR(0), FR(0), ALU.mult))
    p.op("vector", "V", lambda v: v.tensor_tensor(FR(2), FR(1), FR(2), ALU.subtract))
    wv = p.val("V")
    p.wait("scalar", "V", wv)
    p.op("scalar", "S", lambda s: s.activation(FR(3), FR(2), AF.Ln, bias=sb["c_eps"][0:1, 0:1]))
    p.op("scalar", "S", lambda s: s.activation(FR(3), FR(3), AF.Exp, scale=sb["c_nhalf"][0:1, 0:1]))
    ws = p.val("S")
    p.wait("vector", "S", ws)
    p.op("vector", "V", lambda v: v.tensor_tensor(FR(4), FR(0), FR(3), ALU.mult))
    p.op("vector", "V", lambda v: v.tensor_scalar(FR(4), FR(4), -1.0, None, ALU.mult))
    p.op("vector", "V", lambda v: v.tensor_copy(sb["f_rb"], FR(4)))
    wv = p.val("V")
    p.wait("tensor", "V", wv)
    p.op("tensor", "T", lambda t: t.matmul(ps["pR"][:, 0:TC], sb["c_onesr32"], FR(3),
                                           start=True, stop=True))
    wt = p.val("T")
    for k in range(KB):
        par = k % 2
        xk = sb["f_xk"][:, par * TC:(par + 1) * TC]
        p.wait("sync", "V")
        p.op("sync", "qs", lambda s, xk=xk, k=k:
             s.dma_start(out=xk, in_=xsrc[k * 128:(k + 1) * 128, t0:t0 + TC]))
        p.wait("vector", "qs")
        p.wait("vector", "T", wt)
        p.op("vector", "V", lambda v, xk=xk, k=k:
             v.tensor_tensor(sb["f_xr1"][:, k * TC:(k + 1) * TC], xk,
                             ps["pR"][:, 0:TC], ALU.mult))
    p.flush()

    mlist = list(range(IB)) + (list(range(IB, 32)) if do_z else [])
    for i, m in enumerate(mlist):
        wb = sb["wA"] if i % 2 == 0 else sb["wB"]
        p.wait("scalar", "T")
        p.wait("scalar", "V")
        p.op("scalar", "qc", lambda s, wb=wb, m=m:
             s.dma_start(out=wb[:, 0:KB * 128], in_=dr["sWin"][m]))
        wq = p.val("qc")
        p.wait("tensor", "qc", wq)
        p.wait("tensor", "V")
        p.wait("tensor", "S")
        pp = ps["pA"] if i % 2 == 0 else ps["pB"]
        for k in range(KB):
            p.op("tensor", "T", lambda t, wb=wb, k=k, pp=pp:
                 t.matmul(pp[:, 0:TC], wb[:, k * 128:(k + 1) * 128],
                          sb["f_xr1"][:, k * TC:(k + 1) * TC],
                          start=(k == 0), stop=False))
        p.op("tensor", "T", lambda t, m=m, pp=pp:
             t.matmul(pp[:, 0:TC], sb["c_sin"][0:1, m * 128:(m + 1) * 128],
                      sb["f_rb"], start=False, stop=True))
        wt = p.val("T")
        if m < IB:
            p.wait("scalar", "T", wt)
            p.wait("scalar", "V")
            p.op("scalar", "S", lambda s, pp=pp, m=m:
                 s.activation(sb["f_xi"][:, KC - 1:], pp[:, 0:TC], AF.Identity,
                              bias=sb["c_bein"][:, m:m + 1]))
            ws = p.val("S")
            p.wait("vector", "S", ws)
            p.op("vector", "V", lambda v, m=m:
                 v.tensor_copy(sb["f_xi"][:, 0:KC - 1],
                               sb["st_hal"][:, m * (KC - 1):(m + 1) * (KC - 1)]))
            p.op("vector", "V", lambda v, m=m:
                 v.tensor_scalar(sb["f_cva"], sb["f_xi"][:, 0:TC],
                                 sb["c_convw"][:, m * KC:m * KC + 1],
                                 sb["c_convb"][:, m:m + 1], ALU.mult, ALU.add))
        if m < IB:
            for kk in range(1, KC):
                srcb, dstb = ((sb["f_cva"], sb["f_cvb"]) if kk % 2 == 1
                              else (sb["f_cvb"], sb["f_cva"]))
                p.op("vector", "V", lambda v, m=m, kk=kk, srcb=srcb, dstb=dstb:
                     v.scalar_tensor_tensor(dstb, sb["f_xi"][:, kk:kk + TC],
                                            sb["c_convw"][:, m * KC + kk:m * KC + kk + 1],
                                            srcb, ALU.mult, ALU.add))
            p.op("vector", "V", lambda v, m=m:
                 v.tensor_copy(sb["st_hal"][:, m * (KC - 1):(m + 1) * (KC - 1)],
                               sb["f_xi"][:, TC:TC + KC - 1]))
            wv = p.val("V")
            p.wait("scalar", "V", wv)
            fin = sb["f_cvb"] if KC % 2 == 0 else sb["f_cva"]
            p.op("scalar", "S", lambda s, m=m, fin=fin:
                 s.activation(sb["f_xc"][:, m * TC:(m + 1) * TC], fin, AF.Silu))
        else:
            mb = m - IB
            p.wait("scalar", "T", wt)
            p.wait("scalar", "qs")
            p.op("scalar", "S", lambda s, pp=pp, m=m:
                 s.activation(sb["f_t0"], pp[:, 0:TC], AF.Silu,
                              bias=sb["c_bein"][:, m:m + 1]))
            ws = p.val("S")
            p.wait("sync", "S", ws)
            p.op("sync", "qs", lambda s, mb=mb:
                 s.dma_start(out=dr["s_sz"][mb, :, half * T + t0:half * T + t0 + TC],
                             in_=sb["f_t0"]))
        p.flush()

    ws = p.val("S")
    p.wait("tensor", "S", ws)
    for k in range(IB):
        p.op("tensor", "T", lambda t, k=k:
             t.matmul(ps["pR"][0:96, 0:TC], sb["c_Wx"][:, k * 96:(k + 1) * 96],
                      sb["f_xc"][:, k * TC:(k + 1) * TC],
                      start=(k == 0), stop=(k == IB - 1)))
    wt = p.val("T")
    p.wait("scalar", "T", wt)
    p.wait("scalar", "V")
    p.op("scalar", "S", lambda s: s.activation(sb["f_proj"][0:96, :],
                                               ps["pR"][0:96, 0:TC], AF.Copy))
    p.op("scalar", "S", lambda s: s.activation(sb["f_dtr"][0:DTR, :],
                                               ps["pR"][0:DTR, 0:TC], AF.Copy))
    p.op("scalar", "S", lambda s: s.activation(sb["f_BC"][0:32, :],
                                               ps["pR"][64:96, 0:TC], AF.Copy))
    ws = p.val("S")
    if "s_dbgP" in dr:
        p.wait("sync", "S", ws)
        p.op("sync", "qs", lambda s: s.dma_start(
            out=dr["s_dbgP"][0:96, half * T + t0:half * T + t0 + TC], in_=sb["f_proj"][0:96, :]))
    for m in range(IB):
        p.wait("vector", "qs")
        p.wait("vector", "S", ws)
        p.op("vector", "V", lambda v, m=m:
             v.tensor_scalar(sb["f_t0"], sb["f_xc"][:, m * TC:(m + 1) * TC],
                             sb["c_D"][:, m:m + 1], None, ALU.mult))
        wv = p.val("V")
        p.wait("sync", "V", wv)
        p.op("sync", "qs", lambda s, m=m:
             s.dma_start(out=dr["s_xcD"][m, :, half * T + t0:half * T + t0 + TC],
                         in_=sb["f_t0"]))
    p.wait("tensor", "S", ws)
    p.wait("scalar", "V")
    for n in range(2 * NS):
        dst = sb["f_Br"] if n < NS else sb["f_Cr"]
        nn = n % NS
        col = n if n < NS else 16 + nn
        p.op("tensor", "T", lambda t, col=col:
             t.matmul(ps["pV"][:, 0:TC], sb["c_oh32"][:, col * 128:(col + 1) * 128],
                      sb["f_BC"][0:32, :], start=True, stop=True))
        wt = p.val("T")
        p.wait("scalar", "T", wt)
        p.op("scalar", "S", lambda s, dst=dst, nn=nn:
             s.activation(dst[:, nn * TC:(nn + 1) * TC], ps["pV"][:, 0:TC], AF.Copy))
        ws2 = p.val("S")
        p.wait("tensor", "S", ws2)
    p.flush()


def _scan(p, dr, sb, ps, ch, prefix):
    t0 = ch * TC
    half = 0 if prefix else 1
    do_y = (not prefix) or ch >= NCH - 2

    if True:
        p.op("vector", "V", lambda v: v.memset(sb["f_dA"][:, 0:2], 0.0))
        p.op("vector", "V", lambda v: v.memset(sb["f_dA"][:, TC + 2:TC + 4], 0.0))
        p.op("vector", "V", lambda v: v.memset(sb["f_u"][:, 0:1], 0.0))
        p.wait("vector", "V")
    for m in range(IB):
        par = m % 2
        dt = sb["f_dt"][:, par * TC:(par + 1) * TC]
        dtx = sb["f_dtx"][:, par * TC:(par + 1) * TC]
        p.wait("tensor", "S")
        p.wait("tensor", "V")
        pp = ps["pA"] if par == 0 else ps["pB"]
        p.op("tensor", "T", lambda t, m=m, pp=pp:
             t.matmul(pp[:, 0:TC], sb["c_Wdt"][:, m * 128:(m + 1) * 128],
                      sb["f_dtr"][0:DTR, :], start=True, stop=True))
        wt = p.val("T")
        p.wait("scalar", "T", wt)
        p.wait("scalar", "V")
        p.op("scalar", "S", lambda s, m=m, pp=pp:
             s.activation(sb["f_cva"], pp[:, 0:TC], AF.Abs, bias=sb["c_dtb"][:, m:m + 1]))
        p.op("scalar", "S", lambda s: s.activation(sb["f_cva"], sb["f_cva"], AF.Exp,
                                                   scale=sb["c_neg1"][:, 0:1]))
        p.op("scalar", "S", lambda s: s.activation(sb["f_cva"], sb["f_cva"], AF.Ln,
                                                   bias=1.0))
        p.op("scalar", "S", lambda s, m=m, pp=pp:
             s.activation(sb["f_cvb"], pp[:, 0:TC], AF.Relu, bias=sb["c_dtb"][:, m:m + 1]))
        ws = p.val("S")
        p.wait("vector", "S", ws)
        p.op("vector", "V", lambda v, dt=dt: v.tensor_tensor(dt, sb["f_cva"],
                                                             sb["f_cvb"], ALU.add))
        mask = sb["c_mask"][:, 0:1] if prefix else 1.0
        p.op("vector", "V", lambda v, dt=dt, dtx=dtx, m=m, mask=mask:
             v.scalar_tensor_tensor(dtx, dt, mask, sb["f_xc"][:, m * TC:(m + 1) * TC],
                                    ALU.mult, ALU.mult))
        if do_y:
            p.op("vector", "V", lambda v: v.memset(sb["f_ya"], 0.0))
        wv_dt = p.val("V")
        if m == 0 and "s_dbgD" in dr:
            p.wait("sync", "V", wv_dt)
            p.op("sync", "qs", lambda s, dt=dt: s.dma_start(
                out=dr["s_dbgD"][:, half * T + t0:half * T + t0 + TC], in_=dt))
        p.wait("scalar", "V", wv_dt)
        for n in range(NS):
            npar = n % 2
            dA = sb["f_dA"][:, npar * (TC + 2):(npar + 1) * (TC + 2)]
            if n >= 2:
                p.wait("scalar", "V")
            p.op("scalar", "S", lambda s, dA=dA, dt=dt, m=m, n=n:
                 s.activation(dA[:, 2:], dt, AF.Exp,
                              scale=sb["c_A"][:, m * NS + n:m * NS + n + 1]))
            ws = p.val("S")
            p.op("vector", "V", lambda v, dtx=dtx, n=n:
                 v.tensor_tensor(sb["f_u"][:, 2:], dtx,
                                 sb["f_Br"][:, n * TC:(n + 1) * TC], ALU.mult))
            p.op("vector", "V", lambda v, m=m, n=n:
                 v.tensor_copy(sb["f_u"][:, 1:2],
                               sb["st_h"][:, m * NS + n:m * NS + n + 1]))
            p.wait("vector", "S", ws)
            p.wait("vector", "V")
            p.op("vector", "V", lambda v, dA=dA:
                 v.tensor_tensor_scan(sb["f_h"], dA, sb["f_u"], 0.0,
                                      ALU.mult, ALU.add))
            p.wait("vector", "V")
            p.op("vector", "V", lambda v, m=m, n=n:
                 v.tensor_copy(sb["st_h"][:, m * NS + n:m * NS + n + 1],
                               sb["f_h"][:, TC + 1:TC + 2]))
            if m == 0 and n == 0 and "s_dbgU" in dr:
                wvx = p.val("V")
                p.wait("gpsimd", "V", wvx)
                p.op("gpsimd", "qg", lambda g, dA=dA: g.dma_start(
                    out=dr["s_dbgA"][:, half * T + t0:half * T + t0 + TC], in_=dA[:, 2:]))
                p.op("gpsimd", "qg", lambda g: g.dma_start(
                    out=dr["s_dbgU"][:, half * T + t0:half * T + t0 + TC], in_=sb["f_u"][:, 2:]))
                p.op("gpsimd", "qg", lambda g: g.dma_start(
                    out=dr["s_dbgH"][:, half * T + t0:half * T + t0 + TC], in_=sb["f_h"][:, 2:]))
                p.wait("vector", "qg")
            if do_y:
                p.op("vector", "V", lambda v, n=n:
                     v.tensor_tensor(sb["f_t0"], sb["f_h"][:, 2:],
                                     sb["f_Cr"][:, n * TC:(n + 1) * TC], ALU.mult))
                p.op("vector", "V", lambda v:
                     v.tensor_tensor(sb["f_ya"], sb["f_ya"], sb["f_t0"], ALU.add))
        if do_y:
            p.wait("vector", "qs")
            p.wait("gpsimd", "qs")
            p.wait("gpsimd", "V")
            p.op("gpsimd", "qg", lambda g, m=m:
                 g.dma_start(out=sb["f_t0"],
                             in_=dr["s_xcD"][m, :, half * T + t0:half * T + t0 + TC]))
            p.op("gpsimd", "qg", lambda g, m=m:
                 g.dma_start(out=sb["f_yg"],
                             in_=dr["s_sz"][m, :, half * T + t0:half * T + t0 + TC]))
            p.wait("vector", "qg")
            p.op("vector", "V", lambda v:
                 v.tensor_tensor(sb["f_t0"], sb["f_ya"], sb["f_t0"], ALU.add))
            p.op("vector", "V", lambda v:
                 v.tensor_tensor(sb["f_yg"], sb["f_t0"], sb["f_yg"], ALU.mult))
            wv = p.val("V")
            p.wait("sync", "V", wv)
            p.op("sync", "qs", lambda s, m=m:
                 s.dma_start(out=dr["s_yg"][m, :, half * T + t0:half * T + t0 + TC],
                             in_=sb["f_yg"]))
        p.flush()


def _back(p, dr, sb, ps, ch, prefix):
    t0 = ch * TC
    half = 0 if prefix else 1
    xsrc = dr["xTp"] if prefix else dr["xTm"]
    KRr = lambda i: sb["k_rows"][0:1, i * TC:(i + 1) * TC]

    def dbg(key, srcfn, nblk=KB):
        if key not in dr:
            return
        wv = p.val("V")
        p.wait("sync", "V", wv)
        for m in range(nblk):
            p.op("sync", "qs", lambda s, m=m: s.dma_start(
                out=dr[key][m, :, half * T + t0:half * T + t0 + TB], in_=srcfn(m)))
        p.wait("vector", "qs")
        p.wait("scalar", "qs")
        p.wait("tensor", "qs")

    for m in range(KB):
        wb = sb["wA"] if m % 2 == 0 else sb["wB"]
        p.wait("scalar", "T")
        p.op("scalar", "qc", lambda s, wb=wb, m=m: s.dma_start(out=wb, in_=dr["sWout"][m]))
        wq = p.val("qc")
        for hh in range(2):
            pp = ps["pT"] if hh == 0 else ps["pU"]
            for k in range(IB):
                par = k % 2
                yg = sb["k_yg"][:, par * TC:(par + 1) * TC]
                p.wait("sync", "T")
                p.op("sync", "qs", lambda s, yg=yg, k=k, hh=hh:
                     s.dma_start(out=yg, in_=dr["s_yg"][k, :, half * T + t0 + hh * TC:
                                                        half * T + t0 + (hh + 1) * TC]))
                p.wait("tensor", "qs")
                if k == 0 and hh == 0:
                    p.wait("tensor", "qc", wq)
                    p.wait("tensor", "V")
                p.op("tensor", "T", lambda t, wb=wb, k=k, yg=yg, pp=pp:
                     t.matmul(pp[:, 0:TC], wb[:, k * 128:(k + 1) * 128], yg,
                              start=(k == 0), stop=(k == IB - 1)))
        wt = p.val("T")
        p.wait("sync", "V")
        p.op("sync", "qs", lambda s, m=m:
             s.dma_start(out=sb["k_x"], in_=xsrc[m * 128:(m + 1) * 128, t0:t0 + TB]))
        p.wait("vector", "qs")
        p.wait("vector", "T", wt)
        p.op("vector", "V", lambda v, m=m:
             v.tensor_tensor(sb["k_ha"][:, m * TB:m * TB + TC],
                             sb["k_x"][:, 0:TC], ps["pT"][:, 0:TC], ALU.add))
        p.op("vector", "V", lambda v, m=m:
             v.tensor_tensor(sb["k_ha"][:, m * TB + TC:(m + 1) * TB],
                             sb["k_x"][:, TC:TB], ps["pU"][:, 0:TC], ALU.add))
        p.flush()
    dbg("s_dbgHA", lambda m: sb["k_ha"][:, m * TB:(m + 1) * TB])

    def stats(srcfn):
        for m in range(KB):
            p.wait("scalar", "T")
            p.wait("scalar", "V")
            p.op("scalar", "S", lambda s, m=m:
                 s.activation(sb["k_dl"], srcfn(m), AF.Square))
            ws = p.val("S")
            p.wait("tensor", "S", ws)
            p.op("tensor", "T", lambda t, m=m:
                 t.matmul(ps["pA"][0:1, 0:TC], sb["c_ones"], srcfn(m),
                          start=(m == 0), stop=(m == KB - 1)))
            p.op("tensor", "T", lambda t, m=m:
                 t.matmul(ps["pA"][32:33, 0:TC], sb["c_ones32"], sb["k_dl"],
                          start=(m == 0), stop=(m == KB - 1)))

    def lnrows(o):
        wt = p.val("T")
        p.wait("vector", "T", wt)
        p.op("vector", "V", lambda v: v.tensor_scalar(KRr(o), ps["pA"][0:1, 0:TC],
                                                      1.0 / D, None, ALU.mult))
        p.op("vector", "V", lambda v: v.tensor_scalar(KRr(o + 1), ps["pA"][32:33, 0:TC],
                                                      1.0 / D, None, ALU.mult))
        p.op("vector", "V", lambda v: v.tensor_tensor(KRr(o + 2), KRr(o), KRr(o), ALU.mult))
        p.op("vector", "V", lambda v: v.tensor_tensor(KRr(o + 1), KRr(o + 1), KRr(o + 2),
                                                      ALU.subtract))
        wv = p.val("V")
        p.wait("scalar", "V", wv)
        p.op("scalar", "S", lambda s: s.activation(KRr(o + 1), KRr(o + 1), AF.Ln, bias=sb["c_eps"][0:1, 0:1]))
        p.op("scalar", "S", lambda s: s.activation(KRr(o + 1), KRr(o + 1), AF.Exp, scale=sb["c_nhalf"][0:1, 0:1]))
        ws = p.val("S")
        p.wait("vector", "S", ws)
        p.op("vector", "V", lambda v: v.tensor_tensor(KRr(o + 2), KRr(o), KRr(o + 1), ALU.mult))
        p.op("vector", "V", lambda v: v.tensor_scalar(KRr(o + 2), KRr(o + 2), -1.0,
                                                      None, ALU.mult))

    for hh in range(2):
        stats(lambda m, hh=hh: sb["k_ha"][:, m * TB + hh * TC:m * TB + (hh + 1) * TC])
        lnrows(hh * 4)
        wv = p.val("V")
        p.wait("tensor", "V", wv)
        p.op("tensor", "T", lambda t, o=hh * 4:
             t.matmul(ps["pR"][:, 0:TC], sb["c_onesr32"], KRr(o), start=True, stop=True))
        p.op("tensor", "T", lambda t, o=hh * 4:
             t.matmul(ps["pV"][:, 0:TC], sb["c_onesr32"], KRr(o + 1), start=True, stop=True))
        wt = p.val("T")
        p.wait("vector", "T", wt)
        for m in range(KB):
            p.op("vector", "V", lambda v, m=m, hh=hh:
                 v.tensor_tensor(sb["k_xm"][:, m * TB + hh * TC:m * TB + (hh + 1) * TC],
                                 sb["k_ha"][:, m * TB + hh * TC:m * TB + (hh + 1) * TC],
                                 ps["pR"][:, 0:TC], ALU.subtract))
            p.op("vector", "V", lambda v, m=m, hh=hh:
                 v.tensor_tensor(sb["k_xm"][:, m * TB + hh * TC:m * TB + (hh + 1) * TC],
                                 sb["k_xm"][:, m * TB + hh * TC:m * TB + (hh + 1) * TC],
                                 ps["pV"][:, 0:TC], ALU.mult))
        p.flush()
    dbg("s_dbgXM", lambda m: sb["k_xm"][:, m * TB:(m + 1) * TB])

    for hh in range(2):
        for tb in range(TC // 128):
            wv = p.val("V")
            p.wait("tensor", "V", wv)
            p.wait("tensor", "S")
            for k in range(KB):
                p.op("tensor", "T", lambda t, k=k, hh=hh, tb=tb:
                     t.matmul(ps["pW"][:, 0:8],
                              sb["k_xm"][:, k * TB + hh * TC + tb * 128:
                                         k * TB + hh * TC + (tb + 1) * 128],
                              sb["c_rout"][:, k * E:(k + 1) * E],
                              start=(k == 0), stop=False))
            p.op("tensor", "T", lambda t:
                 t.matmul(ps["pW"][:, 0:8], sb["c_onesr"], sb["c_ber"],
                          start=False, stop=True))
            wt = p.val("T")
            p.wait("vector", "T", wt)
            TM, TCC = sb["k_tm"], sb["k_tc"]
            if "s_dbgLG" in dr:
                gb = (half * T + t0 + hh * TC + tb * 128) // 128
                p.op("vector", "V", lambda v: v.tensor_copy(
                    sb["k_t1"][:, 8:16], ps["pW"][:, 0:8]))
                wv2 = p.val("V")
                p.wait("sync", "V", wv2)
                p.op("sync", "qs", lambda s, gb=gb: s.dma_start(
                    out=dr["s_dbgLG"][:, gb * 8:(gb + 1) * 8], in_=sb["k_t1"][:, 8:16]))
                p.wait("vector", "qs")
            p.op("vector", "V", lambda v: v.tensor_reduce(
                TCC[:, 0:1], ps["pW"][:, 0:8], axis=mybir.AxisListType.X, op=ALU.max))
            p.wait("vector", "V")
            p.op("vector", "V", lambda v: v.tensor_scalar(
                TM, ps["pW"][:, 0:8], TCC[:, 0:1], None, ALU.subtract))
            p.wait("vector", "V")
            GE = sb["k_dl"][:, 0:8]
            L2 = sb["k_dl"][:, 8:16]
            p.op("vector", "V", lambda v, GE=GE: v.tensor_scalar(
                GE, TM, 0.0, None, ALU.is_ge))
            p.wait("vector", "V")
            p.op("vector", "V", lambda v, GE=GE, L2=L2: v.tensor_scalar(
                L2, GE, -1e30, None, ALU.mult))
            p.wait("vector", "V")
            p.op("vector", "V", lambda v, L2=L2: v.tensor_tensor(L2, TM, L2, ALU.add))
            p.wait("vector", "V")
            p.op("vector", "V", lambda v, L2=L2: v.tensor_reduce(
                TCC[:, 1:2], L2, axis=mybir.AxisListType.X, op=ALU.max))
            wv = p.val("V")
            p.wait("scalar", "V", wv)
            p.op("scalar", "S", lambda s: s.activation(sb["k_t1"][:, 0:8], TM, AF.Exp))
            p.op("scalar", "S", lambda s: s.activation(TCC[:, 2:3], TCC[:, 1:2], AF.Exp))
            ws = p.val("S")
            p.wait("vector", "S", ws)
            p.op("vector", "V", lambda v, GE=GE: v.tensor_scalar(
                GE, TM, TCC[:, 1:2], None, ALU.is_ge))
            p.op("vector", "V", lambda v: v.tensor_scalar(
                TCC[:, 3:4], TCC[:, 2:3], 1.0, None, ALU.add))
            p.wait("vector", "V")
            p.op("vector", "V", lambda v: v.reciprocal(TCC[:, 3:4], TCC[:, 3:4]))
            p.wait("vector", "V")
            p.op("vector", "V", lambda v, GE=GE: v.tensor_tensor(
                TM, sb["k_t1"][:, 0:8], GE, ALU.mult))
            p.wait("vector", "V")
            p.op("vector", "V", lambda v: v.tensor_scalar(
                TM, TM, TCC[:, 3:4], None, ALU.mult))
            p.wait("vector", "V")
            p.op("vector", "V", lambda v: v.tensor_copy(sb["k_t0"][:, 0:8], TM))
            if "s_dbgT2" in dr:
                gb = (half * T + t0 + hh * TC + tb * 128) // 128
                wv2 = p.val("V")
                p.wait("sync", "V", wv2)
                p.op("sync", "qs", lambda s, gb=gb: s.dma_start(
                    out=dr["s_dbgT2"][:, gb * 24:gb * 24 + 4], in_=TCC[:, 0:4]))
                p.op("sync", "qs", lambda s, gb=gb, GE=GE: s.dma_start(
                    out=dr["s_dbgT2"][:, gb * 24 + 4:gb * 24 + 12], in_=GE))
                p.op("sync", "qs", lambda s, gb=gb: s.dma_start(
                    out=dr["s_dbgT2"][:, gb * 24 + 12:gb * 24 + 20], in_=TM))
                p.wait("vector", "qs")
            wv = p.val("V")
            p.wait("tensor", "V", wv)
            p.op("tensor", "T", lambda t:
                 t.matmul(ps["pX"][0:8, 0:128], sb["k_t0"][:, 0:8], sb["c_id"],
                          start=True, stop=True))
            wt = p.val("T")
            p.wait("vector", "T", wt)
            p.op("vector", "V", lambda v, hh=hh, tb=tb: v.tensor_copy(
                sb["k_w"][:, hh * TC + tb * 128:hh * TC + (tb + 1) * 128],
                ps["pX"][0:8, 0:128]))
            p.op("vector", "V", lambda v, hh=hh, tb=tb: v.tensor_copy(
                sb["k_wb"][:, hh * TC + tb * 128:hh * TC + (tb + 1) * 128],
                ps["pX"][0:8, 0:128]))
            p.flush()

    dbg("s_dbgW", lambda m: sb["k_w"][0:8, 0:TB], nblk=1)
    if "s_dbgRT" in dr:
        p.op("sync", "qs", lambda s: s.dma_start(out=dr["s_dbgRT"], in_=sb["c_rout"]))
        p.op("sync", "qs", lambda s: s.dma_start(out=dr["s_dbgBR"], in_=sb["c_ber"]))

    for hh in range(2):
        for m in range(KB):
            wv = p.val("V")
            p.wait("tensor", "V", wv)
            pp = ps["pT"] if m % 2 == 0 else ps["pU"]
            p.op("tensor", "T", lambda t, m=m, hh=hh, pp=pp:
                 t.matmul(pp[:, 0:TC], sb["c_b2"][:, m * 128:(m + 1) * 128],
                          sb["k_wb"][:, hh * TC:(hh + 1) * TC], start=True, stop=True))
            wt = p.val("T")
            p.wait("vector", "T", wt)
            p.op("vector", "V", lambda v, m=m, hh=hh, pp=pp:
                 v.tensor_copy(sb["k_acc"][:, m * TB + hh * TC:m * TB + (hh + 1) * TC],
                               pp[:, 0:TC]))
    p.flush()
    for e in range(E):
        for hh in range(2):
            wv = p.val("V")
            p.wait("tensor", "V", wv)
            p.op("tensor", "T", lambda t, e=e, hh=hh:
                 t.matmul(ps["pR"][:, 0:TC], sb["c_oh8"][:, e * 128:(e + 1) * 128],
                          sb["k_wb"][:, hh * TC:(hh + 1) * TC],
                          start=True, stop=True))
            wt = p.val("T")
            p.wait("scalar", "T", wt)
            p.wait("scalar", "V")
            p.op("scalar", "S", lambda s: s.activation(sb["k_wr"][:, 0:TC],
                                                       ps["pR"][:, 0:TC], AF.Copy))
            ws = p.val("S")
            p.wait("tensor", "S", ws)
            p.wait("vector", "T")
            for mb in range(FB):
                wbuf = sb["wA"] if mb % 2 == 0 else sb["wB"]
                p.wait("scalar", "T")
                p.op("scalar", "qc", lambda s, e=e, mb=mb, wbuf=wbuf:
                     s.dma_start(out=wbuf[:, 0:KB * 128], in_=dr["sW1"][e, mb]))
                wq = p.val("qc")
                p.wait("tensor", "qc", wq)
                pp = ps["pT"] if mb % 2 == 0 else ps["pU"]
                for k in range(KB):
                    p.op("tensor", "T", lambda t, k=k, wbuf=wbuf, pp=pp, hh=hh:
                         t.matmul(pp[:, 0:TC], wbuf[:, k * 128:(k + 1) * 128],
                                  sb["k_xm"][:, k * TB + hh * TC:k * TB + (hh + 1) * TC],
                                  start=(k == 0), stop=(k == KB - 1)))
                wt = p.val("T")
                p.wait("scalar", "T", wt)
                p.wait("scalar", "V")
                p.op("scalar", "S", lambda s, pp=pp, e=e, mb=mb:
                     s.activation(sb["k_t0"], pp[:, 0:TC], AF.Silu,
                                  bias=sb["c_bew1"][:, e * FB + mb:e * FB + mb + 1]))
                ws = p.val("S")
                p.wait("vector", "S", ws)
                p.op("vector", "V", lambda v, mb=mb:
                     v.tensor_tensor(sb["k_h1"][:, mb * TC:(mb + 1) * TC],
                                     sb["k_t0"], sb["k_wr"][:, 0:TC], ALU.mult))
            for m in range(KB):
                wbuf2 = sb["wA"] if m % 2 == 0 else sb["wB"]
                p.wait("scalar", "T")
                p.op("scalar", "qc", lambda s, e=e, m=m, wbuf2=wbuf2:
                     s.dma_start(out=wbuf2, in_=dr["sW2"][m, e]))
                wq = p.val("qc")
                wv = p.val("V")
                p.wait("tensor", "qc", wq)
                p.wait("tensor", "V", wv)
                pp = ps["pT"] if m % 2 == 0 else ps["pU"]
                for k in range(FB):
                    p.op("tensor", "T", lambda t, k=k, wbuf2=wbuf2, pp=pp:
                         t.matmul(pp[:, 0:TC], wbuf2[:, k * 128:(k + 1) * 128],
                                  sb["k_h1"][:, k * TC:(k + 1) * TC],
                                  start=(k == 0), stop=(k == FB - 1)))
                wt = p.val("T")
                p.wait("vector", "T", wt)
                p.op("vector", "V", lambda v, m=m, hh=hh, pp=pp:
                     v.tensor_tensor(sb["k_acc"][:, m * TB + hh * TC:m * TB + (hh + 1) * TC],
                                     sb["k_acc"][:, m * TB + hh * TC:m * TB + (hh + 1) * TC],
                                     pp[:, 0:TC], ALU.add))
        p.flush()
    dbg("s_dbgAC", lambda m: sb["k_acc"][:, m * TB:(m + 1) * TB])

    for m in range(KB):
        p.wait("sync", "V")
        p.op("sync", "qs", lambda s, m=m:
             s.dma_start(out=sb["k_x"], in_=xsrc[m * 128:(m + 1) * 128, t0:t0 + TB]))
        p.wait("vector", "qs")
        p.op("vector", "V", lambda v, m=m:
             v.tensor_tensor(sb["k_x2a"][:, m * TB:(m + 1) * TB],
                             sb["k_ha"][:, m * TB:(m + 1) * TB], sb["k_x"], ALU.add))
        p.op("vector", "V", lambda v, m=m:
             v.tensor_tensor(sb["k_x2a"][:, m * TB:(m + 1) * TB],
                             sb["k_x2a"][:, m * TB:(m + 1) * TB],
                             sb["k_acc"][:, m * TB:(m + 1) * TB], ALU.add))
    dbg("s_dbgX2", lambda m: sb["k_x2a"][:, m * TB:(m + 1) * TB])
    wv = p.val("V")
    p.wait("scalar", "V", wv)
    for hh in range(2):
        stats(lambda m, hh=hh: sb["k_x2a"][:, m * TB + hh * TC:m * TB + (hh + 1) * TC])
        lnrows(hh * 4)
    p.op("vector", "V", lambda v: v.tensor_copy(sb["k_rb"][0:1, 0:1], sb["st_p2"]))
    p.op("vector", "V", lambda v: v.tensor_copy(sb["k_rb"][0:1, 1:TC + 1], KRr(2)))
    p.op("vector", "V", lambda v: v.tensor_copy(sb["k_rb"][0:1, TC + 1:TB + 1], KRr(6)))
    for hh in range(2):
        o = hh * 4
        wv = p.val("V")
        p.wait("tensor", "V", wv)
        p.op("tensor", "T", lambda t, o=o:
             t.matmul(ps["pR"][:, 0:TC], sb["c_onesr32"], KRr(o + 1), start=True, stop=True))
        wt = p.val("T")
        p.wait("vector", "T", wt)
        for m in range(KB):
            p.op("vector", "V", lambda v, m=m, hh=hh:
                 v.tensor_tensor(sb["k_xs1"][:, m * W1 + 1 + hh * TC:m * W1 + 1 + (hh + 1) * TC],
                                 sb["k_x2a"][:, m * TB + hh * TC:m * TB + (hh + 1) * TC],
                                 ps["pR"][:, 0:TC], ALU.mult))
    p.op("vector", "V", lambda v:
         v.tensor_copy(sb["k_xs1"][:, 0:KB * W1:W1], sb["st_x1p"]))
    if prefix:
        p.op("vector", "V", lambda v:
             v.tensor_scalar(sb["st_x1p"], sb["k_xs1"][:, TB:KB * W1:W1],
                             sb["c_mask"][:, 0:1], None, ALU.mult))
        p.op("vector", "V", lambda v:
             v.tensor_scalar(sb["st_p2"], KRr(6)[:, TC - 1:TC], sb["c_mask"][0:1, 0:1],
                             None, ALU.mult))
        p.flush()
        return
    p.op("vector", "V", lambda v:
         v.tensor_copy(sb["st_x1p"], sb["k_xs1"][:, TB:KB * W1:W1]))
    p.op("vector", "V", lambda v: v.tensor_copy(sb["st_p2"], KRr(6)[:, TC - 1:TC]))
    dbg("s_dbgXS", lambda m: sb["k_xs1"][:, m * W1 + 1:m * W1 + 1 + TB])
    wv_xs1 = p.val("V")

    for sf in range(TB // TF):
        c0 = sf * TF
        p.wait("vector", "qs")
        for m in range(KB):
            p.wait("scalar", "T")
            p.op("scalar", "qc", lambda s, m=m:
                 s.dma_start(out=sb["wA"][:, 0:KB * 128], in_=dr["sWr"][0, m]))
            p.op("scalar", "qc", lambda s, m=m:
                 s.dma_start(out=sb["wB"][:, 0:KB * 128], in_=dr["sWr"][1, m]))
            wq = p.val("qc")
            p.wait("tensor", "qc", wq)
            p.wait("tensor", "V", wv_xs1)
            for k in range(KB):
                base = k * W1
                p.op("tensor", "T", lambda t, k=k, base=base, c0=c0:
                     t.matmul(ps["pT"][:, 0:TF], sb["wA"][:, k * 128:(k + 1) * 128],
                              sb["k_xs1"][:, base + 1 + c0:base + 1 + c0 + TF],
                              start=(k == 0), stop=False))
                p.op("tensor", "T", lambda t, k=k, base=base, c0=c0:
                     t.matmul(ps["pT"][:, 0:TF], sb["wB"][:, k * 128:(k + 1) * 128],
                              sb["k_xs1"][:, base + c0:base + c0 + TF],
                              start=False, stop=False))
            p.op("tensor", "T", lambda t, m=m, c0=c0:
                 t.matmul(ps["pT"][:, 0:TF], sb["c_swr"][0:1, m * 128:(m + 1) * 128],
                          sb["k_rb"][0:1, 1 + c0:1 + c0 + TF], start=False, stop=False))
            p.op("tensor", "T", lambda t, m=m, c0=c0:
                 t.matmul(ps["pT"][:, 0:TF], sb["c_swr"][0:1, D + m * 128:D + (m + 1) * 128],
                          sb["k_rb"][0:1, c0:c0 + TF], start=False, stop=True))
            wt = p.val("T")
            p.wait("scalar", "T", wt)
            p.op("scalar", "S", lambda s, m=m:
                 s.activation(sb["k_r"][:, m * TF:(m + 1) * TF], ps["pT"][:, 0:TF],
                              AF.Sigmoid))
            p.op("scalar", "qc", lambda s, m=m:
                 s.dma_start(out=sb["wA"][:, 0:KB * 128], in_=dr["sWk"][0, m]))
            p.op("scalar", "qc", lambda s, m=m:
                 s.dma_start(out=sb["wB"][:, 0:KB * 128], in_=dr["sWk"][1, m]))
            wq = p.val("qc")
            p.wait("tensor", "qc", wq)
            for k in range(KB):
                base = k * W1
                p.op("tensor", "T", lambda t, k=k, base=base, c0=c0:
                     t.matmul(ps["pU"][:, 0:TF], sb["wA"][:, k * 128:(k + 1) * 128],
                              sb["k_xs1"][:, base + 1 + c0:base + 1 + c0 + TF],
                              start=(k == 0), stop=False))
                p.op("tensor", "T", lambda t, k=k, base=base, c0=c0:
                     t.matmul(ps["pU"][:, 0:TF], sb["wB"][:, k * 128:(k + 1) * 128],
                              sb["k_xs1"][:, base + c0:base + c0 + TF],
                              start=False, stop=False))
            p.op("tensor", "T", lambda t, m=m, c0=c0:
                 t.matmul(ps["pU"][:, 0:TF], sb["c_swk"][0:1, m * 128:(m + 1) * 128],
                          sb["k_rb"][0:1, 1 + c0:1 + c0 + TF], start=False, stop=False))
            p.op("tensor", "T", lambda t, m=m, c0=c0:
                 t.matmul(ps["pU"][:, 0:TF], sb["c_swk"][0:1, D + m * 128:D + (m + 1) * 128],
                          sb["k_rb"][0:1, c0:c0 + TF], start=False, stop=True))
            wt = p.val("T")
            p.wait("scalar", "T", wt)
            p.op("scalar", "S", lambda s: s.activation(sb["k_t1"][:, 0:TF],
                                                       ps["pU"][:, 0:TF], AF.Relu))
            p.op("scalar", "S", lambda s, m=m:
                 s.activation(sb["k_ka"][:, m * TF:(m + 1) * TF], sb["k_t1"][:, 0:TF],
                              AF.Square))
        ws = p.val("S")
        p.wait("tensor", "S", ws)
        for m in range(KB):
            wbuf = sb["wA"] if m % 2 == 0 else sb["wB"]
            p.wait("scalar", "T")
            p.op("scalar", "qc", lambda s, m=m, wbuf=wbuf:
                 s.dma_start(out=wbuf[:, 0:KB * 128], in_=dr["sWv"][m]))
            wq = p.val("qc")
            p.wait("tensor", "qc", wq)
            pp = ps["pT"] if m % 2 == 0 else ps["pU"]
            for k in range(KB):
                p.op("tensor", "T", lambda t, k=k, wbuf=wbuf, pp=pp:
                     t.matmul(pp[:, 0:TF], wbuf[:, k * 128:(k + 1) * 128],
                              sb["k_ka"][:, k * TF:(k + 1) * TF],
                              start=(k == 0), stop=(k == KB - 1)))
            wt = p.val("T")
            p.wait("vector", "T", wt)
            p.wait("vector", "S")
            p.op("vector", "V", lambda v, m=m, pp=pp:
                 v.tensor_tensor(sb["k_w2s"], pp[:, 0:TF],
                                 sb["k_r"][:, m * TF:(m + 1) * TF], ALU.mult))
            base = m * W1
            p.op("vector", "V", lambda v, base=base, c0=c0:
                 v.tensor_tensor(sb["k_w2s"], sb["k_w2s"],
                                 sb["k_xs1"][:, base + 1 + c0:base + 1 + c0 + TF], ALU.add))
            p.op("vector", "V", lambda v, m=m:
                 v.tensor_copy(sb["k_out"][:, m * TF:(m + 1) * TF], sb["k_w2s"]))
            wv = p.val("V")
            p.wait("scalar", "V", wv)
            p.op("scalar", "S", lambda s: s.activation(sb["k_t1"][:, 0:TF], sb["k_w2s"],
                                                       AF.Square))
            ws = p.val("S")
            p.wait("tensor", "S", ws)
            p.wait("tensor", "V", wv)
            p.op("tensor", "T", lambda t, m=m:
                 t.matmul(ps["pA"][0:1, 0:TF], sb["c_ones32"], sb["k_w2s"],
                          start=(m == 0), stop=(m == KB - 1)))
            p.op("tensor", "T", lambda t, m=m:
                 t.matmul(ps["pA"][32:33, 0:TF], sb["c_ones32"], sb["k_t1"][:, 0:TF],
                          start=(m == 0), stop=(m == KB - 1)))
        wt = p.val("T")
        p.wait("vector", "T", wt)
        p.op("vector", "V", lambda v: v.tensor_scalar(KRr(0)[:, 0:TF], ps["pA"][0:1, 0:TF],
                                                      1.0 / D, None, ALU.mult))
        p.op("vector", "V", lambda v: v.tensor_scalar(KRr(1)[:, 0:TF], ps["pA"][32:33, 0:TF],
                                                      1.0 / D, None, ALU.mult))
        p.op("vector", "V", lambda v: v.tensor_tensor(KRr(3)[:, 0:TF], KRr(0)[:, 0:TF],
                                                      KRr(0)[:, 0:TF], ALU.mult))
        p.op("vector", "V", lambda v: v.tensor_tensor(KRr(1)[:, 0:TF], KRr(1)[:, 0:TF],
                                                      KRr(3)[:, 0:TF], ALU.subtract))
        wv = p.val("V")
        p.wait("scalar", "V", wv)
        p.op("scalar", "S", lambda s: s.activation(KRr(1)[:, 0:TF], KRr(1)[:, 0:TF],
                                                   AF.Ln, bias=sb["c_eps"][0:1, 0:1]))
        p.op("scalar", "S", lambda s: s.activation(KRr(1)[:, 0:TF], KRr(1)[:, 0:TF],
                                                   AF.Exp, scale=sb["c_nhalf"][0:1, 0:1]))
        ws = p.val("S")
        p.wait("tensor", "S", ws)
        p.op("tensor", "T", lambda t: t.matmul(ps["pR"][:, 0:TF], sb["c_onesr32"],
                                               KRr(0)[:, 0:TF], start=True, stop=True))
        p.op("tensor", "T", lambda t: t.matmul(ps["pV"][:, 0:TF], sb["c_onesr32"],
                                               KRr(1)[:, 0:TF], start=True, stop=True))
        wt = p.val("T")
        p.wait("vector", "T", wt)
        for m in range(KB):
            p.op("vector", "V", lambda v, m=m:
                 v.tensor_tensor(sb["k_out"][:, m * TF:(m + 1) * TF],
                                 sb["k_out"][:, m * TF:(m + 1) * TF],
                                 ps["pR"][:, 0:TF], ALU.subtract))
            p.op("vector", "V", lambda v, m=m:
                 v.tensor_tensor(sb["k_out"][:, m * TF:(m + 1) * TF],
                                 sb["k_out"][:, m * TF:(m + 1) * TF],
                                 ps["pV"][:, 0:TF], ALU.mult))
            wv = p.val("V")
            p.wait("sync", "V", wv)
            p.op("sync", "qs", lambda s, m=m, c0=c0:
                 s.dma_start(out=dr["outT"][m * 128:(m + 1) * 128, t0 + c0:t0 + c0 + TF],
                             in_=sb["k_out"][:, m * TF:(m + 1) * TF]))
        p.flush()


_NC_CACHE = {}


def _get_nc():
    if "nc" not in _NC_CACHE:
        _NC_CACHE["nc"] = build()
    return _NC_CACHE["nc"]


def kernel(**inputs):
    x = np.ascontiguousarray(inputs["L0_attn_x"])

    def g(k):
        return np.ascontiguousarray(inputs[k])

    def col(v, nb):
        return np.ascontiguousarray(v.reshape(nb, 128).T)

    shared = {
        "W_in_T": np.ascontiguousarray(g("L0_attn_in_proj_w").T),
        "b_in": col(g("L0_attn_in_proj_b"), 32),
        "lmg": col(g("L0_attn_ln_mamba_g"), KB),
        "lmb": col(g("L0_attn_ln_mamba_b"), KB),
        "leg": col(g("L0_attn_ln_moe_g"), KB),
        "leb": col(g("L0_attn_ln_moe_b"), KB),
        "convw": np.ascontiguousarray(
            g("L0_attn_conv_w").reshape(IB, 128, KC).transpose(1, 0, 2).reshape(128, IB * KC)),
        "convb": col(g("L0_attn_conv_b"), IB),
        "W_x_T": np.ascontiguousarray(
            g("L0_attn_x_proj_w").T.reshape(IB, 128, 96).transpose(1, 0, 2).reshape(128, IB * 96)),
        "W_dt_T": np.ascontiguousarray(g("L0_attn_dt_proj_w").T),
        "dtb": col(g("L0_attn_dt_proj_b"), IB),
        "A_log": np.ascontiguousarray(
            g("L0_attn_A_log").reshape(IB, 128, NS).transpose(1, 0, 2).reshape(128, IB * NS)),
        "Dsk": col(g("L0_attn_D_skip"), IB),
        "W_out_T": np.ascontiguousarray(g("L0_attn_out_proj_w").T),
        "router_T": np.ascontiguousarray(
            g("L0_attn_router_w").T.reshape(KB, 128, E).transpose(1, 0, 2).reshape(128, KB * E)),
        "w1_T": np.ascontiguousarray(g("L0_attn_w1").transpose(0, 2, 1)),
        "b1": np.ascontiguousarray(g("L0_attn_b1").reshape(E * FB, 128).T),
        "w2_T": np.ascontiguousarray(g("L0_attn_w2").transpose(0, 2, 1)),
        "b2r": g("L0_attn_b2"),
        "W_r_T": np.ascontiguousarray(g("L0_ffn_W_r").T),
        "W_k_T": np.ascontiguousarray(g("L0_ffn_W_k").T),
        "W_v_T": np.ascontiguousarray(g("L0_ffn_W_v").T),
        "mur": col(g("L0_ffn_mu_r"), KB),
        "muk": col(g("L0_ffn_mu_k"), KB),
    }
    in_maps = []
    for c in range(8):
        b, h = c // 2, c % 2
        m = dict(shared)
        m["xTm"] = np.ascontiguousarray(x[b, 4096 * h:4096 * (h + 1), :].T)
        m["xTp"] = (np.ascontiguousarray(x[b, 0:4096, :].T) if h == 1
                    else np.zeros((D, T), np.float32))
        m["maskv"] = np.full((128, 1), float(h), np.float32)
        in_maps.append(m)
    nc = _get_nc()
    res = run_bass_kernel_spmd(nc, in_maps, core_ids=list(range(8)))
    out = np.empty((4, 8192, 1024), np.float32)
    for c in range(8):
        b, h = c // 2, c % 2
        out[b, 4096 * h:4096 * (h + 1), :] = res.results[c]["outT"].T
    return out


# version 04
# speedup vs baseline: 1.4134x; 1.4134x over previous
import sys
sys.path.insert(0, '/opt/trn_rl_repo')
import numpy as np
import concourse.bass as bass
import concourse.mybir as mybir
from concourse.bass_utils import run_bass_kernel_spmd

F32 = mybir.dt.float32
BF16 = mybir.dt.bfloat16
AF = mybir.ActivationFunctionType
ALU = mybir.AluOpType

D, DI, NS, DTR, KC, E, DFF = 1024, 2048, 16, 64, 4, 8, 2048
T = 4096
TC = 512
NCH = T // TC
KB, IB, FB = 8, 16, 16
TB = 2 * TC
W1 = TB + 1
TF = 256
EPS = 1e-5
ENGS = ("tensor", "vector", "scalar", "gpsimd", "sync")


class P:
    def __init__(self, nc, block):
        self.nc, self.block = nc, block
        self.q = {e: [] for e in ENGS}
        self.sem, self.cnt = {}, {}

    def new_sem(self, name):
        self.sem[name] = self.nc.alloc_semaphore(name=name)
        self.cnt[name] = 0

    def flush(self):
        for e in ENGS:
            fns = self.q[e]
            if fns:
                def runner(ex, fns=tuple(fns)):
                    for f in fns:
                        f(ex)
                getattr(self.block, e)(runner)
                self.q[e] = []

    def op(self, eng, sname, fn):
        by = 16 if sname.startswith("q") else 1
        self.cnt[sname] += by
        s = self.sem[sname]

        def g(ex, fn=fn, s=s, by=by):
            fn(ex).then_inc(s, by)
        self.q[eng].append(g)
        return self.cnt[sname]

    def val(self, name):
        return self.cnt[name]

    def wait(self, eng, sname, val=None):
        v = self.cnt[sname] if val is None else val
        if v > 0:
            s = self.sem[sname]
            self.q[eng].append(lambda ex, s=s, v=v: ex.wait_ge(s, v))


class Pref:

    def __init__(self, p, bufs):
        self.p, self.bufs = p, bufs
        self.loads, self.q, self.wts = [], [], []
        self.ni = self.ci = 0

    def add(self, fn):
        self.loads.append(fn)

    def _issue(self):
        i = self.ni
        b = self.bufs[i % len(self.bufs)]
        j = i - len(self.bufs)
        if j >= 0:
            self.p.wait("gpsimd", "T", self.wts[j])
        fn = self.loads[i]
        self.p.op("gpsimd", "qc", lambda g, fn=fn, b=b: fn(g, b))
        self.q.append(self.p.val("qc"))
        self.ni += 1

    def consume(self):
        while self.ni <= self.ci + 1 and self.ni < len(self.loads):
            j = self.ni - len(self.bufs)
            if j >= 0 and j >= len(self.wts):
                if self.ni <= self.ci:
                    raise RuntimeError("Pref underflow: done() missing")
                break
            self._issue()
        self.p.wait("tensor", "qc", self.q[self.ci])
        b = self.bufs[self.ci % len(self.bufs)]
        self.ci += 1
        return b

    def done(self):
        self.wts.append(self.p.val("T"))


def build(debug=False):
    nc = bass.Bass("TRN2", target_bir_lowering=False)
    dr, sb, ps = {}, {}, {}

    def din(name, shape, dt=F32):
        dr[name] = nc.dram_tensor(name, shape, dt, kind="ExternalInput").ap()

    def dscr(name, shape, dt):
        dr[name] = nc.dram_tensor(name, shape, dt, kind="Internal").ap()

    din("xTm", [D, T]); din("xTp", [D, T]); din("maskv", [128, 1])
    din("W_in_T", [D, 2 * DI]); din("b_in", [128, 32])
    din("lmg", [128, KB]); din("lmb", [128, KB])
    din("leg", [128, KB]); din("leb", [128, KB])
    din("convw", [128, IB * KC]); din("convb", [128, IB])
    din("W_x_T", [128, IB * 96])
    din("W_dt_T", [DTR, DI]); din("dtb", [128, IB])
    din("A_log", [128, IB * NS]); din("Dsk", [128, IB])
    din("W_out_T", [DI, D])
    din("router_T", [128, KB * E])
    din("w1_T", [E, D, DFF]); din("b1", [128, E * FB])
    din("w2_T", [E, DFF, D]); din("b2r", [8, D])
    din("W_r_T", [D, D]); din("W_k_T", [D, D]); din("W_v_T", [D, D])
    din("mur", [128, KB]); din("muk", [128, KB])
    dr["outT"] = nc.dram_tensor("outT", [D, T], F32, kind="ExternalOutput").ap()

    dscr("sWin", [32, 128, KB * 128], BF16)
    dscr("sWout", [KB, 128, IB * 128], BF16)
    dscr("sW1", [E, FB, 128, KB * 128], BF16)
    dscr("sW2", [KB, E, 128, FB * 128], BF16)
    dscr("sWr", [2, KB, 128, KB * 128], BF16)
    dscr("sWk", [2, KB, 128, KB * 128], BF16)
    dscr("sWv", [KB, 128, KB * 128], BF16)
    scrkind = "ExternalOutput" if debug else "Internal"
    dr["s_yg"] = nc.dram_tensor("s_yg", [IB, 128, 2 * T], BF16, kind=scrkind).ap()
    dr["s_sz"] = nc.dram_tensor("s_sz", [IB, 128, 2 * T], BF16, kind=scrkind).ap()
    dr["s_xcD"] = nc.dram_tensor("s_xcD", [IB, 128, 2 * T], BF16, kind=scrkind).ap()
    if debug:
        dr["s_dbgA"] = nc.dram_tensor("s_dbgA", [128, 2 * T], F32, kind="ExternalOutput").ap()
        dr["s_dbgP"] = nc.dram_tensor("s_dbgP", [128, 2 * T], F32, kind="ExternalOutput").ap()
        dr["s_dbgD"] = nc.dram_tensor("s_dbgD", [128, 2 * T], F32, kind="ExternalOutput").ap()
        dr["s_dbgU"] = nc.dram_tensor("s_dbgU", [128, 2 * T], F32, kind="ExternalOutput").ap()
        dr["s_dbgH"] = nc.dram_tensor("s_dbgH", [128, 2 * T], F32, kind="ExternalOutput").ap()
        for nm in ["s_dbgHA", "s_dbgXM", "s_dbgAC", "s_dbgX2", "s_dbgXS"]:
            dr[nm] = nc.dram_tensor(nm, [KB, 128, 2 * T], BF16, kind="ExternalOutput").ap()
        dr["s_dbgW"] = nc.dram_tensor("s_dbgW", [1, 8, 2 * T], F32, kind="ExternalOutput").ap()
        dr["s_dbgLG"] = nc.dram_tensor("s_dbgLG", [128, (2 * T) // 128 * 8], F32,
                                       kind="ExternalOutput").ap()
        dr["s_dbgRT"] = nc.dram_tensor("s_dbgRT", [128, KB * E], BF16,
                                       kind="ExternalOutput").ap()
        dr["s_dbgBR"] = nc.dram_tensor("s_dbgBR", [1, 8], BF16,
                                       kind="ExternalOutput").ap()
        dr["s_dbgT2"] = nc.dram_tensor("s_dbgT2", [128, (2 * T) // 128 * 24], F32,
                                       kind="ExternalOutput").ap()

    def sbuf(name, shape, dt):
        sb[name] = nc.alloc_sbuf_tensor(name, shape, dt).ap()

    sbuf("c_lmb", [128, KB], F32); sbuf("c_leg", [128, KB], F32)
    sbuf("c_leb", [128, KB], F32)
    sbuf("c_convw", [128, IB * KC], F32); sbuf("c_convb", [128, IB], F32)
    sbuf("c_dtb", [128, IB], F32); sbuf("c_A", [128, IB * NS], F32)
    sbuf("c_D", [128, IB], F32)
    sbuf("c_mur", [128, KB], F32); sbuf("c_muk", [128, KB], F32)
    sbuf("c_mask", [128, 1], F32)
    sbuf("c_Wx", [128, IB * 96], BF16)
    sbuf("c_Wdt", [DTR, DI], BF16)
    sbuf("c_rout", [128, KB * E], BF16)
    sbuf("c_ones", [128, 1], BF16); sbuf("c_ones32", [128, 1], F32)
    sbuf("c_eps", [128, 1], F32); sbuf("c_neg1", [128, 1], F32)
    sbuf("c_nhalf", [128, 1], F32)
    sbuf("c_onesr", [1, 128], BF16); sbuf("c_onesr32", [1, 128], F32)
    sbuf("c_sin", [1, 2 * DI], BF16)
    sbuf("c_swr", [1, 2 * D], BF16); sbuf("c_swk", [1, 2 * D], BF16)
    sbuf("c_bein", [128, 32], F32)
    sbuf("c_bew1", [128, E * FB], F32)
    sbuf("c_b2", [8, D], BF16)
    sbuf("c_ber", [1, 8], BF16)
    sbuf("st_h", [128, IB * NS], F32)
    sbuf("st_hal", [128, IB * (KC - 1)], F32)
    sbuf("st_x1p", [128, KB], BF16)
    sbuf("st_p2", [1, 1], F32)
    sbuf("wA", [128, IB * 128], BF16)
    sbuf("wB", [128, IB * 128], BF16)
    sbuf("k_yg", [128, 2 * TC], BF16)
    sbuf("k_x", [128, TB], F32)
    sbuf("k_rows", [1, 8 * TC], F32)
    sbuf("k_rb", [1, TB + 2], BF16)
    sbuf("k_w", [8, 2 * TC], F32)
    sbuf("k_wb", [8, 2 * TC], BF16)
    sbuf("k_tm", [128, 8], F32)
    sbuf("k_tc", [128, 8], F32)

    ARENA = 120 * 1024
    arena_t = nc.alloc_sbuf_tensor("arena", [128, ARENA], mybir.dt.uint8)
    arena_f32 = arena_t.ap().bitcast(F32)
    arena_bf = arena_t.ap().bitcast(BF16)
    _off = {"f": 0, "k": 0}

    def view(group, name, cols, dt):
        esz = mybir.dt.size(dt)
        off = (_off[group] + 63) // 64 * 64
        base = arena_f32 if dt == F32 else arena_bf
        sb[name] = base[:, off // esz: off // esz + cols]
        _off[group] = off + cols * esz
        assert _off[group] <= ARENA, (name, _off[group])

    view("f", "f_xk", 2 * TC, F32)
    view("f", "f_rows", 8 * TC, F32)
    view("f", "f_rb", TC, BF16)
    view("f", "f_xr1", KB * TC, BF16)
    view("f", "f_xi", TC + KC - 1, F32)
    view("f", "f_cva", TC, F32)
    view("f", "f_cvb", TC, F32)
    view("f", "f_xc", IB * TC, BF16)
    view("f", "f_proj", TC, F32)
    view("f", "f_dtr", TC, BF16)
    view("f", "f_dt", 2 * TC, F32)
    view("f", "f_dtx", 2 * TC, BF16)
    view("f", "f_Br", NS * TC, BF16)
    view("f", "f_Cr", NS * TC, BF16)
    view("f", "f_BC", TC, BF16)
    view("f", "f_dA", 2 * (TC + 2), F32)
    view("f", "f_u", TC + 2, BF16)
    view("f", "f_h", TC + 2, BF16)
    view("f", "f_ya", TC, F32)
    view("f", "f_t0", TC, BF16)
    view("f", "f_yg", TC, BF16)
    view("k", "k_ha", KB * TB, BF16)
    view("k", "k_xm", KB * TB, BF16)
    view("k", "k_x2a", KB * TB, BF16)
    view("k", "k_acc", KB * TB, BF16)
    view("k", "k_xs1", KB * W1, BF16)
    view("k", "k_h1", FB * TC, BF16)
    view("k", "k_r", KB * TF, BF16)
    view("k", "k_ka", KB * TF, BF16)
    view("k", "k_out", KB * TF, F32)
    view("k", "k_wr", TC, BF16)
    view("k", "k_dl", TC, F32)
    view("k", "k_t0", TC, BF16)
    view("k", "k_t1", TC, F32)
    view("k", "k_w2s", TF, F32)
    sb["f_rb"] = sb["f_rb"][0:1, :]
    sb["u_f32"] = sb["f_xc"].bitcast(F32)[:, 0:2 * DI]
    sb["u_bf"] = sb["f_Br"][:, 0:2 * DI]

    import ml_dtypes
    oh32 = np.zeros((32, 32 * 128), np.float32)
    for r in range(32):
        oh32[r, r * 128:(r + 1) * 128] = 1.0
    dr["i_oh32"] = nc.inline_tensor(oh32.astype(ml_dtypes.bfloat16), name="i_oh32").ap()
    dr["i_id"] = nc.inline_tensor(np.eye(128, dtype=ml_dtypes.bfloat16), name="i_id").ap()
    oh8 = np.zeros((8, 8 * 128), np.float32)
    for r in range(8):
        oh8[r, r * 128:(r + 1) * 128] = 1.0
    dr["i_oh8"] = nc.inline_tensor(oh8.astype(ml_dtypes.bfloat16), name="i_oh8").ap()
    sbuf("c_oh32", [32, 32 * 128], BF16)
    sbuf("c_id", [128, 128], BF16)
    sbuf("c_oh8", [8, 8 * 128], BF16)

    def psum(name, shape):
        ps[name] = nc.alloc_psum_tensor(name, shape, F32).ap()

    psum("pA", [128, 512]); psum("pB", [128, 512]); psum("pR", [128, 512])
    psum("pT", [128, 512]); psum("pU", [128, 512]); psum("pV", [128, 512])
    psum("pW", [128, 512]); psum("pX", [128, 512])

    with nc.Block() as block:
        p = P(nc, block)
        for s in ["qs", "qg", "qc", "T", "V", "S"]:
            p.new_sem(s)
        _setup(p, dr, sb, ps)
        for ch in range(NCH):
            _front(p, dr, sb, ps, ch, True)
            _scan(p, dr, sb, ps, ch, True)
        _back(p, dr, sb, ps, NCH - 2, True)
        for ch in range(0, NCH, 2):
            _front(p, dr, sb, ps, ch, False)
            _scan(p, dr, sb, ps, ch, False)
            _front(p, dr, sb, ps, ch + 1, False)
            _scan(p, dr, sb, ps, ch + 1, False)
            _back(p, dr, sb, ps, ch, False)
        p.wait("sync", "qs"); p.wait("sync", "qg"); p.wait("sync", "qc")
        p.flush()
    return nc


def _setup(p, dr, sb, ps):
    for dst, src in [("c_lmb", "lmb"), ("c_leg", "leg"), ("c_leb", "leb"),
                     ("c_convw", "convw"), ("c_convb", "convb"),
                     ("c_dtb", "dtb"), ("c_D", "Dsk"), ("c_mur", "mur"),
                     ("c_muk", "muk"), ("c_mask", "maskv"),
                     ("c_bein", "b_in"), ("c_bew1", "b1")]:
        p.op("sync", "qs", lambda s, d=sb[dst], x=dr[src]: s.dma_start(out=d, in_=x))
    for dst, src in [("c_Wx", "W_x_T"), ("c_Wdt", "W_dt_T"),
                     ("c_rout", "router_T"), ("c_b2", "b2r")]:
        p.op("gpsimd", "qg", lambda g, d=sb[dst], x=dr[src]: g.dma_start(out=d, in_=x))
    p.op("sync", "qs", lambda s: s.dma_start(out=sb["c_A"], in_=dr["A_log"]))
    p.wait("scalar", "qs")
    p.op("scalar", "S", lambda s: s.activation(sb["c_A"], sb["c_A"], AF.Exp))
    p.op("vector", "V", lambda v: v.memset(sb["c_ones32"], 1.0))
    p.op("vector", "V", lambda v: v.memset(sb["c_eps"], EPS))
    p.op("vector", "V", lambda v: v.memset(sb["c_neg1"], -1.0))
    p.op("vector", "V", lambda v: v.memset(sb["c_nhalf"], -0.5))
    p.op("vector", "V", lambda v: v.memset(sb["c_onesr32"], 1.0))
    p.op("vector", "V", lambda v: v.tensor_copy(sb["c_ones"], sb["c_ones32"]))
    p.op("vector", "V", lambda v: v.tensor_copy(sb["c_onesr"], sb["c_onesr32"]))
    for dst, srcn in [("c_oh32", "i_oh32"), ("c_id", "i_id"), ("c_oh8", "i_oh8")]:
        p.op("sync", "qs", lambda s, d=sb[dst], x=dr[srcn]: s.dma_start(out=d, in_=x))
    for z in ["st_h", "st_hal", "st_x1p", "st_p2"]:
        p.op("vector", "V", lambda v, z=z: v.memset(sb[z], 0.0))
    p.op("vector", "V", lambda v: v.memset(sb["f_dA"][:, 0:2], 0.0))
    p.op("vector", "V", lambda v: v.memset(sb["f_dA"][:, TC + 2:TC + 4], 0.0))
    p.op("vector", "V", lambda v: v.memset(sb["f_u"][:, 0:1], 0.0))
    wv = p.val("V")
    p.wait("scalar", "V", wv)
    p.op("scalar", "S", lambda s: s.activation(sb["c_A"], sb["c_A"], AF.Copy,
                                               scale=sb["c_neg1"][:, 0:1]))
    p.flush()

    p.wait("vector", "qg")
    p.wait("vector", "qs")
    for k in range(KB):
        p.op("vector", "V", lambda v, k=k:
             v.tensor_scalar(sb["c_rout"][:, k * E:(k + 1) * E],
                             sb["c_rout"][:, k * E:(k + 1) * E],
                             sb["c_leg"][:, k:k + 1], None, ALU.mult))
    p.op("sync", "qs", lambda s: s.dma_start(out=sb["u_f32"][:, 0:KB * E],
                                             in_=dr["router_T"]))
    p.wait("tensor", "qs")
    p.wait("tensor", "V")
    for k in range(KB):
        p.op("tensor", "T", lambda t, k=k:
             t.matmul(ps["pV"][0:1, 0:8], sb["c_leb"][:, k:k + 1],
                      sb["u_f32"][:, k * E:(k + 1) * E],
                      start=(k == 0), stop=(k == KB - 1)))
    wt = p.val("T")
    p.wait("vector", "T", wt)
    p.op("vector", "V", lambda v: v.tensor_copy(sb["c_ber"], ps["pV"][0:1, 0:8]))
    p.flush()

    FR = lambda i: sb["f_rows"][0:1, i * TC:(i + 1) * TC]
    KRr = lambda i: sb["k_rows"][0:1, i * TC:(i + 1) * TC]

    def family(src3, kb, M, nmb, gain, lnb, sumdst, befdst, store_fn):
        nseg = (M + 511) // 512
        for k in range(kb):
            fb, bb = sb["u_f32"][:, 0:M], sb["u_bf"][:, 0:M]
            p.wait("sync", "V"); p.wait("sync", "T")
            p.op("sync", "qs", lambda s, a=src3(k), fb=fb: s.dma_start(out=fb, in_=a))
            p.wait("vector", "qs")
            if gain is not None:
                p.op("vector", "V", lambda v, fb=fb, bb=bb, g=gain, k=k:
                     v.tensor_scalar(bb, fb, g[:, k:k + 1], None, ALU.mult))
            else:
                p.op("vector", "V", lambda v, fb=fb, bb=bb: v.tensor_copy(bb, fb))
            wv = p.val("V")
            p.wait("tensor", "V", wv)
            if lnb is not None:
                for r in range(nseg):
                    so, sw = r * 512, min(512, M - r * 512)
                    if k == 0:
                        p.op("vector", "V", lambda v, r=r, sw=sw:
                             v.memset(KRr(r)[:, 0:sw], 0.0))
                    wvz = p.val("V")
                    p.wait("tensor", "V", wvz)
                    p.op("tensor", "T", lambda t, fb=fb, so=so, sw=sw, lb=lnb, k=k:
                         t.matmul(ps["pV"][0:1, 0:sw], lb[:, k:k + 1],
                                  fb[:, so:so + sw], start=True, stop=True))
                    wt = p.val("T")
                    p.wait("vector", "T", wt)
                    p.op("vector", "V", lambda v, r=r, sw=sw:
                         v.tensor_tensor(KRr(r)[:, 0:sw], KRr(r)[:, 0:sw],
                                         ps["pV"][0:1, 0:sw], ALU.add))
            if sumdst is not None:
                for r in range(nseg):
                    so, sw = r * 512, min(512, M - r * 512)
                    if k == 0:
                        p.op("vector", "V", lambda v, r=r, sw=sw:
                             v.memset(FR(r)[:, 0:sw], 0.0))
                    wvz = p.val("V")
                    p.wait("tensor", "V", wvz)
                    p.op("tensor", "T", lambda t, bb=bb, so=so, sw=sw:
                         t.matmul(ps["pB"][0:1, 0:sw], sb["c_ones"],
                                  bb[:, so:so + sw], start=True, stop=True))
                    wt = p.val("T")
                    p.wait("vector", "T", wt)
                    p.op("vector", "V", lambda v, r=r, sw=sw:
                         v.tensor_tensor(FR(r)[:, 0:sw], FR(r)[:, 0:sw],
                                         ps["pB"][0:1, 0:sw], ALU.add))
                    if k == kb - 1:
                        wvv = p.val("V")
                        p.wait("scalar", "V", wvv)
                        p.op("scalar", "S", lambda s, so=so, sw=sw, r=r:
                             sumdst(s, so, sw, r))
            wv2 = p.val("V")
            p.wait("sync", "V", wv2)
            for mb in range(nmb):
                dsl, c0 = store_fn(k, mb)
                p.op("sync", "qs", lambda s, d=dsl, bb=bb, c0=c0:
                     s.dma_start(out=d, in_=bb[:, c0:c0 + 128]))
            p.wait("vector", "qs")
            p.wait("vector", "T")
            p.flush()
        if lnb is not None:
            wv = p.val("V")
            p.wait("tensor", "V", wv)
            nmb2 = M // 128
            for m in range(nmb2):
                r, c = divmod(m * 128, TC)
                p.op("tensor", "T", lambda t, r=r, c=c, m=m:
                     t.matmul(ps["pV"][:, m:m + 1],
                              KRr(r)[:, c:c + 128], sb["c_onesr32"][0:1, 0:1],
                              start=True, stop=True))
            wt = p.val("T")
            p.wait("vector", "T", wt)
            p.op("vector", "V", lambda v, nmb2=nmb2, befdst=befdst:
                 v.tensor_tensor(befdst[:, 0:nmb2], befdst[:, 0:nmb2],
                                 ps["pV"][:, 0:nmb2], ALU.add))
            p.flush()

    lmg = p.nc.alloc_sbuf_tensor("tmp_lmg", [128, KB], F32).ap()
    p.op("sync", "qs", lambda s: s.dma_start(out=lmg, in_=dr["lmg"]))
    p.wait("vector", "qs")
    p.flush()

    def sum_in(s, so, sw, r):
        return s.activation(sb["c_sin"][0:1, so:so + sw], FR(r)[:, 0:sw], AF.Copy)

    family(lambda k: dr["W_in_T"][k * 128:(k + 1) * 128, :], KB, 2 * DI, 32,
           lmg, sb["c_lmb"], sum_in, sb["c_bein"],
           lambda k, mb: (dr["sWin"][mb, :, k * 128:(k + 1) * 128], mb * 128))
    family(lambda k: dr["W_out_T"][k * 128:(k + 1) * 128, :], IB, D, KB,
           None, None, None, None,
           lambda k, mb: (dr["sWout"][mb, :, k * 128:(k + 1) * 128], mb * 128))
    for e in range(E):
        family(lambda k, e=e: dr["w1_T"][e, k * 128:(k + 1) * 128, :], KB, DFF, FB,
               sb["c_leg"], sb["c_leb"], None,
               sb["c_bew1"][:, e * FB:(e + 1) * FB],
               lambda k, mb, e=e: (dr["sW1"][e, mb, :, k * 128:(k + 1) * 128], mb * 128))
    for e in range(E):
        family(lambda k, e=e: dr["w2_T"][e, k * 128:(k + 1) * 128, :], FB, D, KB,
               None, None, None, None,
               lambda k, mb, e=e: (dr["sW2"][mb, e, :, k * 128:(k + 1) * 128], mb * 128))

    for wsrc, scr, mucol, sumr in [("W_r_T", "sWr", "c_mur", "c_swr"),
                                   ("W_k_T", "sWk", "c_muk", "c_swk")]:
        for var in range(2):
            for k in range(KB):
                fb, bb = sb["u_f32"][:, 0:D], sb["u_bf"][:, 0:D]
                tmp = sb["u_f32"][:, D:2 * D]
                p.wait("sync", "V"); p.wait("sync", "T")
                p.op("sync", "qs", lambda s, k=k, fb=fb, w=wsrc:
                     s.dma_start(out=fb, in_=dr[w][k * 128:(k + 1) * 128, :]))
                p.wait("vector", "qs")
                if var == 0:
                    p.op("vector", "V", lambda v, fb=fb, bb=bb, k=k, mc=mucol:
                         v.tensor_scalar(bb, fb, sb[mc][:, k:k + 1], None, ALU.mult))
                else:
                    p.op("vector", "V", lambda v, fb=fb, k=k, mc=mucol, tmp=tmp:
                         v.tensor_scalar(tmp, fb, sb[mc][:, k:k + 1], None, ALU.mult))
                    p.op("vector", "V", lambda v, fb=fb, bb=bb, tmp=tmp:
                         v.tensor_tensor(bb, fb, tmp, ALU.subtract))
                wv = p.val("V")
                p.wait("tensor", "V", wv)
                for r in range(2):
                    so = r * 512
                    if k == 0:
                        p.op("vector", "V", lambda v, r=r: v.memset(FR(r), 0.0))
                    wvz = p.val("V")
                    p.wait("tensor", "V", wvz)
                    p.op("tensor", "T", lambda t, bb=bb, so=so:
                         t.matmul(ps["pB"][0:1, 0:512], sb["c_ones"],
                                  bb[:, so:so + 512], start=True, stop=True))
                    wt = p.val("T")
                    p.wait("vector", "T", wt)
                    p.op("vector", "V", lambda v, r=r:
                         v.tensor_tensor(FR(r), FR(r), ps["pB"][0:1, 0:512], ALU.add))
                    if k == KB - 1:
                        wvv = p.val("V")
                        p.wait("scalar", "V", wvv)
                        p.op("scalar", "S", lambda s, so=so, r=r, sumr=sumr, var=var:
                             s.activation(sb[sumr][0:1, var * D + so:var * D + so + 512],
                                          FR(r), AF.Copy))
                wv2 = p.val("V")
                p.wait("sync", "V", wv2)
                for mb in range(KB):
                    p.op("sync", "qs", lambda s, scr=scr, var=var, mb=mb, k=k, bb=bb:
                         s.dma_start(out=dr[scr][var, mb, :, k * 128:(k + 1) * 128],
                                     in_=bb[:, mb * 128:(mb + 1) * 128]))
                p.wait("vector", "qs")
                p.wait("vector", "T")
                p.flush()
    family(lambda k: dr["W_v_T"][k * 128:(k + 1) * 128, :], KB, D, KB,
           None, None, None, None,
           lambda k, mb: (dr["sWv"][mb, :, k * 128:(k + 1) * 128], mb * 128))

    for e in ENGS:
        p.wait(e, "qs"); p.wait(e, "qg"); p.wait(e, "V"); p.wait(e, "S"); p.wait(e, "T")
    p.flush()


def _front(p, dr, sb, ps, ch, prefix):
    t0 = ch * TC
    half = 0 if prefix else 1
    xsrc = dr["xTp"] if prefix else dr["xTm"]
    do_z = (not prefix) or ch >= NCH - 2
    FR = lambda i: sb["f_rows"][0:1, i * TC:(i + 1) * TC]

    for k in range(KB):
        par = k % 2
        xk = sb["f_xk"][:, par * TC:(par + 1) * TC]
        p.wait("sync", "V"); p.wait("sync", "T"); p.wait("sync", "S")
        p.op("sync", "qs", lambda s, xk=xk, k=k:
             s.dma_start(out=xk, in_=xsrc[k * 128:(k + 1) * 128, t0:t0 + TC]))
        p.wait("scalar", "qs")
        p.wait("scalar", "T")
        p.op("scalar", "S", lambda s, xk=xk: s.activation(sb["f_cva"], xk, AF.Square))
        ws = p.val("S")
        p.wait("tensor", "qs")
        p.wait("tensor", "S", ws)
        p.op("tensor", "T", lambda t, xk=xk, k=k:
             t.matmul(ps["pA"][0:1, 0:TC], sb["c_ones32"], xk,
                      start=(k == 0), stop=(k == KB - 1)))
        p.op("tensor", "T", lambda t, k=k:
             t.matmul(ps["pA"][32:33, 0:TC], sb["c_ones32"], sb["f_cva"],
                      start=(k == 0), stop=(k == KB - 1)))
    wt = p.val("T")
    p.wait("vector", "T", wt)
    p.op("vector", "V", lambda v: v.tensor_scalar(FR(0), ps["pA"][0:1, 0:TC],
                                                  1.0 / D, None, ALU.mult))
    p.op("vector", "V", lambda v: v.tensor_scalar(FR(1), ps["pA"][32:33, 0:TC],
                                                  1.0 / D, None, ALU.mult))
    p.op("vector", "V", lambda v: v.tensor_tensor(FR(2), FR(0), FR(0), ALU.mult))
    p.op("vector", "V", lambda v: v.tensor_tensor(FR(2), FR(1), FR(2), ALU.subtract))
    wv = p.val("V")
    p.wait("scalar", "V", wv)
    p.op("scalar", "S", lambda s: s.activation(FR(3), FR(2), AF.Ln, bias=sb["c_eps"][0:1, 0:1]))
    p.op("scalar", "S", lambda s: s.activation(FR(3), FR(3), AF.Exp, scale=sb["c_nhalf"][0:1, 0:1]))
    ws = p.val("S")
    p.wait("vector", "S", ws)
    p.op("vector", "V", lambda v: v.tensor_tensor(FR(4), FR(0), FR(3), ALU.mult))
    p.op("vector", "V", lambda v: v.tensor_scalar(FR(4), FR(4), -1.0, None, ALU.mult))
    p.op("vector", "V", lambda v: v.tensor_copy(sb["f_rb"], FR(4)))
    wv = p.val("V")
    p.wait("tensor", "V", wv)
    p.op("tensor", "T", lambda t: t.matmul(ps["pR"][:, 0:TC], sb["c_onesr32"], FR(3),
                                           start=True, stop=True))
    wt = p.val("T")
    for k in range(KB):
        par = k % 2
        xk = sb["f_xk"][:, par * TC:(par + 1) * TC]
        p.wait("sync", "V")
        p.op("sync", "qs", lambda s, xk=xk, k=k:
             s.dma_start(out=xk, in_=xsrc[k * 128:(k + 1) * 128, t0:t0 + TC]))
        p.wait("vector", "qs")
        p.wait("vector", "T", wt)
        p.op("vector", "V", lambda v, xk=xk, k=k:
             v.tensor_tensor(sb["f_xr1"][:, k * TC:(k + 1) * TC], xk,
                             ps["pR"][:, 0:TC], ALU.mult))
    p.flush()

    mlist = list(range(IB)) + (list(range(IB, 32)) if do_z else [])
    pfi = Pref(p, [sb["wA"], sb["wB"]])
    for m in mlist:
        pfi.add(lambda g, b, m=m:
                g.dma_start(out=b[:, 0:KB * 128], in_=dr["sWin"][m]))
    p.wait("gpsimd", "T")
    wv_xr1 = p.val("V")
    ws_ev = []
    for i, m in enumerate(mlist):
        wb = pfi.consume()
        if i >= 2:
            p.wait("tensor", "S", ws_ev[i - 2])
        else:
            p.wait("tensor", "V", wv_xr1)
            p.wait("tensor", "S")
        pp = ps["pA"] if i % 2 == 0 else ps["pB"]
        for k in range(KB):
            p.op("tensor", "T", lambda t, wb=wb, k=k, pp=pp:
                 t.matmul(pp[:, 0:TC], wb[:, k * 128:(k + 1) * 128],
                          sb["f_xr1"][:, k * TC:(k + 1) * TC],
                          start=(k == 0), stop=False))
        p.op("tensor", "T", lambda t, m=m, pp=pp:
             t.matmul(pp[:, 0:TC], sb["c_sin"][0:1, m * 128:(m + 1) * 128],
                      sb["f_rb"], start=False, stop=True))
        pfi.done()
        wt = p.val("T")
        if m < IB:
            p.wait("scalar", "T", wt)
            p.wait("scalar", "V")
            p.op("scalar", "S", lambda s, pp=pp, m=m:
                 s.activation(sb["f_xi"][:, KC - 1:], pp[:, 0:TC], AF.Identity,
                              bias=sb["c_bein"][:, m:m + 1]))
            ws = p.val("S")
            ws_ev.append(ws)
            p.wait("vector", "S", ws)
            p.op("vector", "V", lambda v, m=m:
                 v.tensor_copy(sb["f_xi"][:, 0:KC - 1],
                               sb["st_hal"][:, m * (KC - 1):(m + 1) * (KC - 1)]))
            p.op("vector", "V", lambda v, m=m:
                 v.tensor_scalar(sb["f_cva"], sb["f_xi"][:, 0:TC],
                                 sb["c_convw"][:, m * KC:m * KC + 1],
                                 sb["c_convb"][:, m:m + 1], ALU.mult, ALU.add))
        if m < IB:
            for kk in range(1, KC):
                srcb, dstb = ((sb["f_cva"], sb["f_cvb"]) if kk % 2 == 1
                              else (sb["f_cvb"], sb["f_cva"]))
                p.op("vector", "V", lambda v, m=m, kk=kk, srcb=srcb, dstb=dstb:
                     v.scalar_tensor_tensor(dstb, sb["f_xi"][:, kk:kk + TC],
                                            sb["c_convw"][:, m * KC + kk:m * KC + kk + 1],
                                            srcb, ALU.mult, ALU.add))
            p.op("vector", "V", lambda v, m=m:
                 v.tensor_copy(sb["st_hal"][:, m * (KC - 1):(m + 1) * (KC - 1)],
                               sb["f_xi"][:, TC:TC + KC - 1]))
            wv = p.val("V")
            p.wait("scalar", "V", wv)
            fin = sb["f_cvb"] if KC % 2 == 0 else sb["f_cva"]
            p.op("scalar", "S", lambda s, m=m, fin=fin:
                 s.activation(sb["f_xc"][:, m * TC:(m + 1) * TC], fin, AF.Silu))
        else:
            mb = m - IB
            p.wait("scalar", "T", wt)
            p.wait("scalar", "qs")
            p.op("scalar", "S", lambda s, pp=pp, m=m:
                 s.activation(sb["f_t0"], pp[:, 0:TC], AF.Silu,
                              bias=sb["c_bein"][:, m:m + 1]))
            ws = p.val("S")
            ws_ev.append(ws)
            p.wait("sync", "S", ws)
            p.op("sync", "qs", lambda s, mb=mb:
                 s.dma_start(out=dr["s_sz"][mb, :, half * T + t0:half * T + t0 + TC],
                             in_=sb["f_t0"]))
        p.flush()

    ws = p.val("S")
    p.wait("tensor", "S", ws)
    for k in range(IB):
        p.op("tensor", "T", lambda t, k=k:
             t.matmul(ps["pR"][0:96, 0:TC], sb["c_Wx"][:, k * 96:(k + 1) * 96],
                      sb["f_xc"][:, k * TC:(k + 1) * TC],
                      start=(k == 0), stop=(k == IB - 1)))
    wt = p.val("T")
    p.wait("scalar", "T", wt)
    p.wait("scalar", "V")
    p.op("scalar", "S", lambda s: s.activation(sb["f_proj"][0:96, :],
                                               ps["pR"][0:96, 0:TC], AF.Copy))
    p.op("scalar", "S", lambda s: s.activation(sb["f_dtr"][0:DTR, :],
                                               ps["pR"][0:DTR, 0:TC], AF.Copy))
    p.op("scalar", "S", lambda s: s.activation(sb["f_BC"][0:32, :],
                                               ps["pR"][64:96, 0:TC], AF.Copy))
    ws = p.val("S")
    if "s_dbgP" in dr:
        p.wait("sync", "S", ws)
        p.op("sync", "qs", lambda s: s.dma_start(
            out=dr["s_dbgP"][0:96, half * T + t0:half * T + t0 + TC], in_=sb["f_proj"][0:96, :]))
    for m in range(IB):
        p.wait("vector", "qs")
        p.wait("vector", "S", ws)
        p.op("vector", "V", lambda v, m=m:
             v.tensor_scalar(sb["f_t0"], sb["f_xc"][:, m * TC:(m + 1) * TC],
                             sb["c_D"][:, m:m + 1], None, ALU.mult))
        wv = p.val("V")
        p.wait("sync", "V", wv)
        p.op("sync", "qs", lambda s, m=m:
             s.dma_start(out=dr["s_xcD"][m, :, half * T + t0:half * T + t0 + TC],
                         in_=sb["f_t0"]))
    p.wait("tensor", "S", ws)
    p.wait("tensor", "V")
    p.wait("scalar", "V")
    wsl = []
    for n in range(2 * NS):
        dst = sb["f_Br"] if n < NS else sb["f_Cr"]
        nn = n % NS
        col = n if n < NS else 16 + nn
        bank = "pV" if n % 2 == 0 else "pW"
        if n >= 2:
            p.wait("tensor", "S", wsl[n - 2])
        p.op("tensor", "T", lambda t, col=col, bank=bank:
             t.matmul(ps[bank][:, 0:TC], sb["c_oh32"][:, col * 128:(col + 1) * 128],
                      sb["f_BC"][0:32, :], start=True, stop=True))
        wt = p.val("T")
        p.wait("scalar", "T", wt)
        p.op("scalar", "S", lambda s, dst=dst, nn=nn, bank=bank:
             s.activation(dst[:, nn * TC:(nn + 1) * TC], ps[bank][:, 0:TC], AF.Copy))
        wsl.append(p.val("S"))
    p.flush()


def _scan(p, dr, sb, ps, ch, prefix):
    t0 = ch * TC
    half = 0 if prefix else 1
    do_y = (not prefix) or ch >= NCH - 2

    if True:
        p.op("vector", "V", lambda v: v.memset(sb["f_dA"][:, 0:2], 0.0))
        p.op("vector", "V", lambda v: v.memset(sb["f_dA"][:, TC + 2:TC + 4], 0.0))
        p.op("vector", "V", lambda v: v.memset(sb["f_u"][:, 0:1], 0.0))
        p.wait("vector", "V")
    for m in range(IB):
        par = m % 2
        dt = sb["f_dt"][:, par * TC:(par + 1) * TC]
        dtx = sb["f_dtx"][:, par * TC:(par + 1) * TC]
        p.wait("tensor", "S")
        p.wait("tensor", "V")
        pp = ps["pA"] if par == 0 else ps["pB"]
        p.op("tensor", "T", lambda t, m=m, pp=pp:
             t.matmul(pp[:, 0:TC], sb["c_Wdt"][:, m * 128:(m + 1) * 128],
                      sb["f_dtr"][0:DTR, :], start=True, stop=True))
        wt = p.val("T")
        p.wait("scalar", "T", wt)
        p.wait("scalar", "V")
        p.op("scalar", "S", lambda s, m=m, pp=pp:
             s.activation(sb["f_cva"], pp[:, 0:TC], AF.Abs, bias=sb["c_dtb"][:, m:m + 1]))
        p.op("scalar", "S", lambda s: s.activation(sb["f_cva"], sb["f_cva"], AF.Exp,
                                                   scale=sb["c_neg1"][:, 0:1]))
        p.op("scalar", "S", lambda s: s.activation(sb["f_cva"], sb["f_cva"], AF.Ln,
                                                   bias=1.0))
        p.op("scalar", "S", lambda s, m=m, pp=pp:
             s.activation(sb["f_cvb"], pp[:, 0:TC], AF.Relu, bias=sb["c_dtb"][:, m:m + 1]))
        ws = p.val("S")
        p.wait("vector", "S", ws)
        p.op("vector", "V", lambda v, dt=dt: v.tensor_tensor(dt, sb["f_cva"],
                                                             sb["f_cvb"], ALU.add))
        mask = sb["c_mask"][:, 0:1] if prefix else 1.0
        p.op("vector", "V", lambda v, dt=dt, dtx=dtx, m=m, mask=mask:
             v.scalar_tensor_tensor(dtx, dt, mask, sb["f_xc"][:, m * TC:(m + 1) * TC],
                                    ALU.mult, ALU.mult))
        if do_y:
            p.op("vector", "V", lambda v: v.memset(sb["f_ya"], 0.0))
        wv_dt = p.val("V")
        if m == 0 and "s_dbgD" in dr:
            p.wait("sync", "V", wv_dt)
            p.op("sync", "qs", lambda s, dt=dt: s.dma_start(
                out=dr["s_dbgD"][:, half * T + t0:half * T + t0 + TC], in_=dt))
        p.wait("scalar", "V", wv_dt)
        for n in range(NS):
            npar = n % 2
            dA = sb["f_dA"][:, npar * (TC + 2):(npar + 1) * (TC + 2)]
            if n >= 2:
                p.wait("scalar", "V")
            p.op("scalar", "S", lambda s, dA=dA, dt=dt, m=m, n=n:
                 s.activation(dA[:, 2:], dt, AF.Exp,
                              scale=sb["c_A"][:, m * NS + n:m * NS + n + 1]))
            ws = p.val("S")
            p.op("vector", "V", lambda v, dtx=dtx, n=n:
                 v.tensor_tensor(sb["f_u"][:, 2:], dtx,
                                 sb["f_Br"][:, n * TC:(n + 1) * TC], ALU.mult))
            p.op("vector", "V", lambda v, m=m, n=n:
                 v.tensor_copy(sb["f_u"][:, 1:2],
                               sb["st_h"][:, m * NS + n:m * NS + n + 1]))
            p.wait("vector", "S", ws)
            p.wait("vector", "V")
            p.op("vector", "V", lambda v, dA=dA:
                 v.tensor_tensor_scan(sb["f_h"], dA, sb["f_u"], 0.0,
                                      ALU.mult, ALU.add))
            p.wait("vector", "V")
            p.op("vector", "V", lambda v, m=m, n=n:
                 v.tensor_copy(sb["st_h"][:, m * NS + n:m * NS + n + 1],
                               sb["f_h"][:, TC + 1:TC + 2]))
            if m == 0 and n == 0 and "s_dbgU" in dr:
                wvx = p.val("V")
                p.wait("gpsimd", "V", wvx)
                p.op("gpsimd", "qg", lambda g, dA=dA: g.dma_start(
                    out=dr["s_dbgA"][:, half * T + t0:half * T + t0 + TC], in_=dA[:, 2:]))
                p.op("gpsimd", "qg", lambda g: g.dma_start(
                    out=dr["s_dbgU"][:, half * T + t0:half * T + t0 + TC], in_=sb["f_u"][:, 2:]))
                p.op("gpsimd", "qg", lambda g: g.dma_start(
                    out=dr["s_dbgH"][:, half * T + t0:half * T + t0 + TC], in_=sb["f_h"][:, 2:]))
                p.wait("vector", "qg")
            if do_y:
                p.op("vector", "V", lambda v, n=n:
                     v.tensor_tensor(sb["f_t0"], sb["f_h"][:, 2:],
                                     sb["f_Cr"][:, n * TC:(n + 1) * TC], ALU.mult))
                p.op("vector", "V", lambda v:
                     v.tensor_tensor(sb["f_ya"], sb["f_ya"], sb["f_t0"], ALU.add))
        if do_y:
            p.wait("vector", "qs")
            p.wait("gpsimd", "qs")
            p.wait("gpsimd", "V")
            p.op("gpsimd", "qg", lambda g, m=m:
                 g.dma_start(out=sb["f_t0"],
                             in_=dr["s_xcD"][m, :, half * T + t0:half * T + t0 + TC]))
            p.op("gpsimd", "qg", lambda g, m=m:
                 g.dma_start(out=sb["f_yg"],
                             in_=dr["s_sz"][m, :, half * T + t0:half * T + t0 + TC]))
            p.wait("vector", "qg")
            p.op("vector", "V", lambda v:
                 v.tensor_tensor(sb["f_t0"], sb["f_ya"], sb["f_t0"], ALU.add))
            p.op("vector", "V", lambda v:
                 v.tensor_tensor(sb["f_yg"], sb["f_t0"], sb["f_yg"], ALU.mult))
            wv = p.val("V")
            p.wait("sync", "V", wv)
            p.op("sync", "qs", lambda s, m=m:
                 s.dma_start(out=dr["s_yg"][m, :, half * T + t0:half * T + t0 + TC],
                             in_=sb["f_yg"]))
        p.flush()


def _back(p, dr, sb, ps, ch, prefix):
    t0 = ch * TC
    half = 0 if prefix else 1
    xsrc = dr["xTp"] if prefix else dr["xTm"]
    KRr = lambda i: sb["k_rows"][0:1, i * TC:(i + 1) * TC]

    def dbg(key, srcfn, nblk=KB):
        if key not in dr:
            return
        wv = p.val("V")
        p.wait("sync", "V", wv)
        for m in range(nblk):
            p.op("sync", "qs", lambda s, m=m: s.dma_start(
                out=dr[key][m, :, half * T + t0:half * T + t0 + TB], in_=srcfn(m)))
        p.wait("vector", "qs")
        p.wait("scalar", "qs")
        p.wait("tensor", "qs")

    PBANK = [("pA", "pB"), ("pR", "pV"), ("pT", "pU"), ("pW", "pX")]
    w4 = sb["k_h1"]
    ygbufs = [sb["k_yg"][:, 0:TB], sb["k_r"][:, 0:TB]]
    xbufs = [sb["k_x"], sb["k_ka"].bitcast(F32)[:, 0:TB]]
    for g in range(2):
        p.wait("gpsimd", "T")
        p.wait("gpsimd", "V")
        for mi in range(4):
            p.op("gpsimd", "qc", lambda gps, g=g, mi=mi:
                 gps.dma_start(out=w4[:, mi * 2048:(mi + 1) * 2048],
                               in_=dr["sWout"][g * 4 + mi]))
        p.op("gpsimd", "qc", lambda gps: gps.dma_start(
            out=ygbufs[0], in_=dr["s_yg"][0, :, half * T + t0:half * T + t0 + TB]))
        ygq = [p.val("qc")]
        wts = []
        for k in range(IB):
            if k + 1 < IB:
                if k >= 1:
                    p.wait("gpsimd", "T", wts[k - 1])
                p.op("gpsimd", "qc", lambda gps, k=k: gps.dma_start(
                    out=ygbufs[(k + 1) % 2],
                    in_=dr["s_yg"][k + 1, :, half * T + t0:half * T + t0 + TB]))
                ygq.append(p.val("qc"))
            ygb = ygbufs[k % 2]
            p.wait("tensor", "qc", ygq[k])
            if k == 0:
                p.wait("tensor", "V")
                p.wait("tensor", "S")
            for mi in range(4):
                b0, b1 = PBANK[mi]
                wsl = w4[:, mi * 2048 + k * 128:mi * 2048 + (k + 1) * 128]
                p.op("tensor", "T", lambda t, wsl=wsl, ygb=ygb, b0=b0, k=k:
                     t.matmul(ps[b0][:, 0:TC], wsl, ygb[:, 0:TC],
                              start=(k == 0), stop=(k == IB - 1)))
                p.op("tensor", "T", lambda t, wsl=wsl, ygb=ygb, b1=b1, k=k:
                     t.matmul(ps[b1][:, 0:TC], wsl, ygb[:, TC:TB],
                              start=(k == 0), stop=(k == IB - 1)))
            wts.append(p.val("T"))
        wtall = p.val("T")
        xq = []
        for mi in range(2):
            p.op("gpsimd", "qc", lambda gps, g=g, mi=mi, b=xbufs[mi]:
                 gps.dma_start(out=b, in_=xsrc[(g * 4 + mi) * 128:(g * 4 + mi + 1) * 128,
                                               t0:t0 + TB]))
            xq.append(p.val("qc"))
        wv_ev = []
        for mi in range(4):
            m = g * 4 + mi
            b0, b1 = PBANK[mi]
            xb = xbufs[mi % 2]
            p.wait("vector", "qc", xq[mi])
            if mi == 0:
                p.wait("vector", "T", wtall)
            p.op("vector", "V", lambda v, m=m, xb=xb, b0=b0:
                 v.tensor_tensor(sb["k_ha"][:, m * TB:m * TB + TC],
                                 xb[:, 0:TC], ps[b0][:, 0:TC], ALU.add))
            p.op("vector", "V", lambda v, m=m, xb=xb, b1=b1:
                 v.tensor_tensor(sb["k_ha"][:, m * TB + TC:(m + 1) * TB],
                                 xb[:, TC:TB], ps[b1][:, 0:TC], ALU.add))
            wv_ev.append(p.val("V"))
            if mi + 2 < 4:
                p.wait("gpsimd", "V", wv_ev[mi])
                p.op("gpsimd", "qc", lambda gps, g=g, mi=mi, b=xbufs[mi % 2]:
                     gps.dma_start(out=b,
                                   in_=xsrc[(g * 4 + mi + 2) * 128:(g * 4 + mi + 3) * 128,
                                            t0:t0 + TB]))
                xq.append(p.val("qc"))
        p.flush()
    dbg("s_dbgHA", lambda m: sb["k_ha"][:, m * TB:(m + 1) * TB])

    def stats(srcfn):
        for m in range(KB):
            p.wait("scalar", "T")
            p.wait("scalar", "V")
            p.op("scalar", "S", lambda s, m=m:
                 s.activation(sb["k_dl"], srcfn(m), AF.Square))
            ws = p.val("S")
            p.wait("tensor", "S", ws)
            p.op("tensor", "T", lambda t, m=m:
                 t.matmul(ps["pA"][0:1, 0:TC], sb["c_ones"], srcfn(m),
                          start=(m == 0), stop=(m == KB - 1)))
            p.op("tensor", "T", lambda t, m=m:
                 t.matmul(ps["pA"][32:33, 0:TC], sb["c_ones32"], sb["k_dl"],
                          start=(m == 0), stop=(m == KB - 1)))

    def lnrows(o):
        wt = p.val("T")
        p.wait("vector", "T", wt)
        p.op("vector", "V", lambda v: v.tensor_scalar(KRr(o), ps["pA"][0:1, 0:TC],
                                                      1.0 / D, None, ALU.mult))
        p.op("vector", "V", lambda v: v.tensor_scalar(KRr(o + 1), ps["pA"][32:33, 0:TC],
                                                      1.0 / D, None, ALU.mult))
        p.op("vector", "V", lambda v: v.tensor_tensor(KRr(o + 2), KRr(o), KRr(o), ALU.mult))
        p.op("vector", "V", lambda v: v.tensor_tensor(KRr(o + 1), KRr(o + 1), KRr(o + 2),
                                                      ALU.subtract))
        wv = p.val("V")
        p.wait("scalar", "V", wv)
        p.op("scalar", "S", lambda s: s.activation(KRr(o + 1), KRr(o + 1), AF.Ln, bias=sb["c_eps"][0:1, 0:1]))
        p.op("scalar", "S", lambda s: s.activation(KRr(o + 1), KRr(o + 1), AF.Exp, scale=sb["c_nhalf"][0:1, 0:1]))
        ws = p.val("S")
        p.wait("vector", "S", ws)
        p.op("vector", "V", lambda v: v.tensor_tensor(KRr(o + 2), KRr(o), KRr(o + 1), ALU.mult))
        p.op("vector", "V", lambda v: v.tensor_scalar(KRr(o + 2), KRr(o + 2), -1.0,
                                                      None, ALU.mult))

    for hh in range(2):
        stats(lambda m, hh=hh: sb["k_ha"][:, m * TB + hh * TC:m * TB + (hh + 1) * TC])
        lnrows(hh * 4)
        wv = p.val("V")
        p.wait("tensor", "V", wv)
        p.op("tensor", "T", lambda t, o=hh * 4:
             t.matmul(ps["pR"][:, 0:TC], sb["c_onesr32"], KRr(o), start=True, stop=True))
        p.op("tensor", "T", lambda t, o=hh * 4:
             t.matmul(ps["pV"][:, 0:TC], sb["c_onesr32"], KRr(o + 1), start=True, stop=True))
        wt = p.val("T")
        p.wait("vector", "T", wt)
        for m in range(KB):
            p.op("vector", "V", lambda v, m=m, hh=hh:
                 v.tensor_tensor(sb["k_xm"][:, m * TB + hh * TC:m * TB + (hh + 1) * TC],
                                 sb["k_ha"][:, m * TB + hh * TC:m * TB + (hh + 1) * TC],
                                 ps["pR"][:, 0:TC], ALU.subtract))
            p.op("vector", "V", lambda v, m=m, hh=hh:
                 v.tensor_tensor(sb["k_xm"][:, m * TB + hh * TC:m * TB + (hh + 1) * TC],
                                 sb["k_xm"][:, m * TB + hh * TC:m * TB + (hh + 1) * TC],
                                 ps["pV"][:, 0:TC], ALU.mult))
        p.flush()
    dbg("s_dbgXM", lambda m: sb["k_xm"][:, m * TB:(m + 1) * TB])

    for hh in range(2):
        for tb in range(TC // 128):
            wv = p.val("V")
            p.wait("tensor", "V", wv)
            p.wait("tensor", "S")
            for k in range(KB):
                p.op("tensor", "T", lambda t, k=k, hh=hh, tb=tb:
                     t.matmul(ps["pW"][:, 0:8],
                              sb["k_xm"][:, k * TB + hh * TC + tb * 128:
                                         k * TB + hh * TC + (tb + 1) * 128],
                              sb["c_rout"][:, k * E:(k + 1) * E],
                              start=(k == 0), stop=False))
            p.op("tensor", "T", lambda t:
                 t.matmul(ps["pW"][:, 0:8], sb["c_onesr"], sb["c_ber"],
                          start=False, stop=True))
            wt = p.val("T")
            p.wait("vector", "T", wt)
            TM, TCC = sb["k_tm"], sb["k_tc"]
            if "s_dbgLG" in dr:
                gb = (half * T + t0 + hh * TC + tb * 128) // 128
                p.op("vector", "V", lambda v: v.tensor_copy(
                    sb["k_t1"][:, 8:16], ps["pW"][:, 0:8]))
                wv2 = p.val("V")
                p.wait("sync", "V", wv2)
                p.op("sync", "qs", lambda s, gb=gb: s.dma_start(
                    out=dr["s_dbgLG"][:, gb * 8:(gb + 1) * 8], in_=sb["k_t1"][:, 8:16]))
                p.wait("vector", "qs")
            p.op("vector", "V", lambda v: v.tensor_reduce(
                TCC[:, 0:1], ps["pW"][:, 0:8], axis=mybir.AxisListType.X, op=ALU.max))
            p.wait("vector", "V")
            p.op("vector", "V", lambda v: v.tensor_scalar(
                TM, ps["pW"][:, 0:8], TCC[:, 0:1], None, ALU.subtract))
            p.wait("vector", "V")
            GE = sb["k_dl"][:, 0:8]
            L2 = sb["k_dl"][:, 8:16]
            p.op("vector", "V", lambda v, GE=GE: v.tensor_scalar(
                GE, TM, 0.0, None, ALU.is_ge))
            p.wait("vector", "V")
            p.op("vector", "V", lambda v, GE=GE, L2=L2: v.tensor_scalar(
                L2, GE, -1e30, None, ALU.mult))
            p.wait("vector", "V")
            p.op("vector", "V", lambda v, L2=L2: v.tensor_tensor(L2, TM, L2, ALU.add))
            p.wait("vector", "V")
            p.op("vector", "V", lambda v, L2=L2: v.tensor_reduce(
                TCC[:, 1:2], L2, axis=mybir.AxisListType.X, op=ALU.max))
            wv = p.val("V")
            p.wait("scalar", "V", wv)
            p.op("scalar", "S", lambda s: s.activation(sb["k_t1"][:, 0:8], TM, AF.Exp))
            p.op("scalar", "S", lambda s: s.activation(TCC[:, 2:3], TCC[:, 1:2], AF.Exp))
            ws = p.val("S")
            p.wait("vector", "S", ws)
            p.op("vector", "V", lambda v, GE=GE: v.tensor_scalar(
                GE, TM, TCC[:, 1:2], None, ALU.is_ge))
            p.op("vector", "V", lambda v: v.tensor_scalar(
                TCC[:, 3:4], TCC[:, 2:3], 1.0, None, ALU.add))
            p.wait("vector", "V")
            p.op("vector", "V", lambda v: v.reciprocal(TCC[:, 3:4], TCC[:, 3:4]))
            p.wait("vector", "V")
            p.op("vector", "V", lambda v, GE=GE: v.tensor_tensor(
                TM, sb["k_t1"][:, 0:8], GE, ALU.mult))
            p.wait("vector", "V")
            p.op("vector", "V", lambda v: v.tensor_scalar(
                TM, TM, TCC[:, 3:4], None, ALU.mult))
            p.wait("vector", "V")
            p.op("vector", "V", lambda v: v.tensor_copy(sb["k_t0"][:, 0:8], TM))
            if "s_dbgT2" in dr:
                gb = (half * T + t0 + hh * TC + tb * 128) // 128
                wv2 = p.val("V")
                p.wait("sync", "V", wv2)
                p.op("sync", "qs", lambda s, gb=gb: s.dma_start(
                    out=dr["s_dbgT2"][:, gb * 24:gb * 24 + 4], in_=TCC[:, 0:4]))
                p.op("sync", "qs", lambda s, gb=gb, GE=GE: s.dma_start(
                    out=dr["s_dbgT2"][:, gb * 24 + 4:gb * 24 + 12], in_=GE))
                p.op("sync", "qs", lambda s, gb=gb: s.dma_start(
                    out=dr["s_dbgT2"][:, gb * 24 + 12:gb * 24 + 20], in_=TM))
                p.wait("vector", "qs")
            wv = p.val("V")
            p.wait("tensor", "V", wv)
            p.op("tensor", "T", lambda t:
                 t.matmul(ps["pX"][0:8, 0:128], sb["k_t0"][:, 0:8], sb["c_id"],
                          start=True, stop=True))
            wt = p.val("T")
            p.wait("vector", "T", wt)
            p.op("vector", "V", lambda v, hh=hh, tb=tb: v.tensor_copy(
                sb["k_w"][:, hh * TC + tb * 128:hh * TC + (tb + 1) * 128],
                ps["pX"][0:8, 0:128]))
            p.op("vector", "V", lambda v, hh=hh, tb=tb: v.tensor_copy(
                sb["k_wb"][:, hh * TC + tb * 128:hh * TC + (tb + 1) * 128],
                ps["pX"][0:8, 0:128]))
            p.flush()

    dbg("s_dbgW", lambda m: sb["k_w"][0:8, 0:TB], nblk=1)
    if "s_dbgRT" in dr:
        p.op("sync", "qs", lambda s: s.dma_start(out=dr["s_dbgRT"], in_=sb["c_rout"]))
        p.op("sync", "qs", lambda s: s.dma_start(out=dr["s_dbgBR"], in_=sb["c_ber"]))

    for hh in range(2):
        for m in range(KB):
            wv = p.val("V")
            p.wait("tensor", "V", wv)
            pp = ps["pT"] if m % 2 == 0 else ps["pU"]
            p.op("tensor", "T", lambda t, m=m, hh=hh, pp=pp:
                 t.matmul(pp[:, 0:TC], sb["c_b2"][:, m * 128:(m + 1) * 128],
                          sb["k_wb"][:, hh * TC:(hh + 1) * TC], start=True, stop=True))
            wt = p.val("T")
            p.wait("vector", "T", wt)
            p.op("vector", "V", lambda v, m=m, hh=hh, pp=pp:
                 v.tensor_copy(sb["k_acc"][:, m * TB + hh * TC:m * TB + (hh + 1) * TC],
                               pp[:, 0:TC]))
    p.flush()
    pf = Pref(p, [sb["wA"], sb["wB"]])
    for e in range(E):
        for hh in range(2):
            for mb in range(FB):
                pf.add(lambda g, b, e=e, mb=mb:
                       g.dma_start(out=b[:, 0:KB * 128], in_=dr["sW1"][e, mb]))
            for m in range(KB):
                pf.add(lambda g, b, e=e, m=m:
                       g.dma_start(out=b[:, 0:FB * 128], in_=dr["sW2"][m, e]))
    silb = [sb["k_t0"], sb["k_t1"].bitcast(BF16)[:, 0:TC]]
    p.wait("gpsimd", "T")
    for e in range(E):
        for hh in range(2):
            wv = p.val("V")
            p.wait("tensor", "V", wv)
            p.op("tensor", "T", lambda t, e=e, hh=hh:
                 t.matmul(ps["pR"][:, 0:TC], sb["c_oh8"][:, e * 128:(e + 1) * 128],
                          sb["k_wb"][:, hh * TC:(hh + 1) * TC],
                          start=True, stop=True))
            wt = p.val("T")
            p.wait("scalar", "T", wt)
            p.wait("scalar", "V")
            p.op("scalar", "S", lambda s: s.activation(sb["k_wr"][:, 0:TC],
                                                       ps["pR"][:, 0:TC], AF.Copy))
            ws = p.val("S")
            p.wait("tensor", "S", ws)
            ws_sil, wv_mul = [], []
            for mb in range(FB):
                wbuf = pf.consume()
                if mb >= 2:
                    p.wait("tensor", "S", ws_sil[mb - 2])
                pp = ps["pT"] if mb % 2 == 0 else ps["pU"]
                for k in range(KB):
                    p.op("tensor", "T", lambda t, k=k, wbuf=wbuf, pp=pp, hh=hh:
                         t.matmul(pp[:, 0:TC], wbuf[:, k * 128:(k + 1) * 128],
                                  sb["k_xm"][:, k * TB + hh * TC:k * TB + (hh + 1) * TC],
                                  start=(k == 0), stop=(k == KB - 1)))
                pf.done()
                wt = p.val("T")
                p.wait("scalar", "T", wt)
                if mb >= 2:
                    p.wait("scalar", "V", wv_mul[mb - 2])
                sdst = silb[mb % 2]
                p.op("scalar", "S", lambda s, pp=pp, e=e, mb=mb, sdst=sdst:
                     s.activation(sdst, pp[:, 0:TC], AF.Silu,
                                  bias=sb["c_bew1"][:, e * FB + mb:e * FB + mb + 1]))
                ws_sil.append(p.val("S"))
                p.wait("vector", "S", ws_sil[-1])
                p.op("vector", "V", lambda v, mb=mb, sdst=sdst:
                     v.tensor_tensor(sb["k_h1"][:, mb * TC:(mb + 1) * TC],
                                     sdst, sb["k_wr"][:, 0:TC], ALU.mult))
                wv_mul.append(p.val("V"))
            wv_acc = []
            for m in range(KB):
                wbuf2 = pf.consume()
                p.wait("tensor", "V", wv_acc[m - 2] if m >= 2 else wv_mul[-1])
                pp = ps["pT"] if m % 2 == 0 else ps["pU"]
                for k in range(FB):
                    p.op("tensor", "T", lambda t, k=k, wbuf2=wbuf2, pp=pp:
                         t.matmul(pp[:, 0:TC], wbuf2[:, k * 128:(k + 1) * 128],
                                  sb["k_h1"][:, k * TC:(k + 1) * TC],
                                  start=(k == 0), stop=(k == FB - 1)))
                pf.done()
                wt = p.val("T")
                p.wait("vector", "T", wt)
                p.op("vector", "V", lambda v, m=m, hh=hh, pp=pp:
                     v.tensor_tensor(sb["k_acc"][:, m * TB + hh * TC:m * TB + (hh + 1) * TC],
                                     sb["k_acc"][:, m * TB + hh * TC:m * TB + (hh + 1) * TC],
                                     pp[:, 0:TC], ALU.add))
                wv_acc.append(p.val("V"))
        p.flush()
    dbg("s_dbgAC", lambda m: sb["k_acc"][:, m * TB:(m + 1) * TB])

    xq2, wv_x2 = [], []
    for mi in range(2):
        p.op("gpsimd", "qc", lambda gps, mi=mi, b=xbufs[mi]:
             gps.dma_start(out=b, in_=xsrc[mi * 128:(mi + 1) * 128, t0:t0 + TB]))
        xq2.append(p.val("qc"))
    for m in range(KB):
        xb = xbufs[m % 2]
        p.wait("vector", "qc", xq2[m])
        p.op("vector", "V", lambda v, m=m, xb=xb:
             v.tensor_tensor(sb["k_x2a"][:, m * TB:(m + 1) * TB],
                             sb["k_ha"][:, m * TB:(m + 1) * TB], xb, ALU.add))
        p.op("vector", "V", lambda v, m=m:
             v.tensor_tensor(sb["k_x2a"][:, m * TB:(m + 1) * TB],
                             sb["k_x2a"][:, m * TB:(m + 1) * TB],
                             sb["k_acc"][:, m * TB:(m + 1) * TB], ALU.add))
        wv_x2.append(p.val("V"))
        if m + 2 < KB:
            p.wait("gpsimd", "V", wv_x2[m])
            p.op("gpsimd", "qc", lambda gps, m=m, b=xbufs[m % 2]:
                 gps.dma_start(out=b, in_=xsrc[(m + 2) * 128:(m + 3) * 128, t0:t0 + TB]))
            xq2.append(p.val("qc"))
    dbg("s_dbgX2", lambda m: sb["k_x2a"][:, m * TB:(m + 1) * TB])
    wv = p.val("V")
    p.wait("scalar", "V", wv)
    for hh in range(2):
        stats(lambda m, hh=hh: sb["k_x2a"][:, m * TB + hh * TC:m * TB + (hh + 1) * TC])
        lnrows(hh * 4)
    p.op("vector", "V", lambda v: v.tensor_copy(sb["k_rb"][0:1, 0:1], sb["st_p2"]))
    p.op("vector", "V", lambda v: v.tensor_copy(sb["k_rb"][0:1, 1:TC + 1], KRr(2)))
    p.op("vector", "V", lambda v: v.tensor_copy(sb["k_rb"][0:1, TC + 1:TB + 1], KRr(6)))
    for hh in range(2):
        o = hh * 4
        wv = p.val("V")
        p.wait("tensor", "V", wv)
        p.op("tensor", "T", lambda t, o=o:
             t.matmul(ps["pR"][:, 0:TC], sb["c_onesr32"], KRr(o + 1), start=True, stop=True))
        wt = p.val("T")
        p.wait("vector", "T", wt)
        for m in range(KB):
            p.op("vector", "V", lambda v, m=m, hh=hh:
                 v.tensor_tensor(sb["k_xs1"][:, m * W1 + 1 + hh * TC:m * W1 + 1 + (hh + 1) * TC],
                                 sb["k_x2a"][:, m * TB + hh * TC:m * TB + (hh + 1) * TC],
                                 ps["pR"][:, 0:TC], ALU.mult))
    p.op("vector", "V", lambda v:
         v.tensor_copy(sb["k_xs1"][:, 0:KB * W1:W1], sb["st_x1p"]))
    if prefix:
        p.op("vector", "V", lambda v:
             v.tensor_scalar(sb["st_x1p"], sb["k_xs1"][:, TB:KB * W1:W1],
                             sb["c_mask"][:, 0:1], None, ALU.mult))
        p.op("vector", "V", lambda v:
             v.tensor_scalar(sb["st_p2"], KRr(6)[:, TC - 1:TC], sb["c_mask"][0:1, 0:1],
                             None, ALU.mult))
        p.flush()
        return
    p.op("vector", "V", lambda v:
         v.tensor_copy(sb["st_x1p"], sb["k_xs1"][:, TB:KB * W1:W1]))
    p.op("vector", "V", lambda v: v.tensor_copy(sb["st_p2"], KRr(6)[:, TC - 1:TC]))
    dbg("s_dbgXS", lambda m: sb["k_xs1"][:, m * W1 + 1:m * W1 + 1 + TB])
    wv_xs1 = p.val("V")

    pfv = Pref(p, [sb["wA"], sb["wB"]])
    for sf in range(TB // TF):
        for m in range(KB):
            for wnm, var in (("sWr", 0), ("sWr", 1), ("sWk", 0), ("sWk", 1)):
                pfv.add(lambda g, b, wnm=wnm, var=var, m=m:
                        g.dma_start(out=b[:, 0:KB * 128], in_=dr[wnm][var, m]))
        for m in range(KB):
            pfv.add(lambda g, b, m=m:
                    g.dma_start(out=b[:, 0:KB * 128], in_=dr["sWv"][m]))
    p.wait("gpsimd", "T")
    for sf in range(TB // TF):
        c0 = sf * TF
        p.wait("vector", "qs")
        ws_r, ws_k = [], []
        for m in range(KB):
            bA = pfv.consume()
            bB = pfv.consume()
            if m >= 1:
                p.wait("tensor", "S", ws_r[m - 1])
            p.wait("tensor", "V", wv_xs1)
            for k in range(KB):
                base = k * W1
                p.op("tensor", "T", lambda t, k=k, base=base, c0=c0, bA=bA:
                     t.matmul(ps["pT"][:, 0:TF], bA[:, k * 128:(k + 1) * 128],
                              sb["k_xs1"][:, base + 1 + c0:base + 1 + c0 + TF],
                              start=(k == 0), stop=False))
                p.op("tensor", "T", lambda t, k=k, base=base, c0=c0, bB=bB:
                     t.matmul(ps["pT"][:, 0:TF], bB[:, k * 128:(k + 1) * 128],
                              sb["k_xs1"][:, base + c0:base + c0 + TF],
                              start=False, stop=False))
            p.op("tensor", "T", lambda t, m=m, c0=c0:
                 t.matmul(ps["pT"][:, 0:TF], sb["c_swr"][0:1, m * 128:(m + 1) * 128],
                          sb["k_rb"][0:1, 1 + c0:1 + c0 + TF], start=False, stop=False))
            p.op("tensor", "T", lambda t, m=m, c0=c0:
                 t.matmul(ps["pT"][:, 0:TF], sb["c_swr"][0:1, D + m * 128:D + (m + 1) * 128],
                          sb["k_rb"][0:1, c0:c0 + TF], start=False, stop=True))
            pfv.done()
            pfv.done()
            wt = p.val("T")
            p.wait("scalar", "T", wt)
            p.op("scalar", "S", lambda s, m=m:
                 s.activation(sb["k_r"][:, m * TF:(m + 1) * TF], ps["pT"][:, 0:TF],
                              AF.Sigmoid))
            ws_r.append(p.val("S"))
            bA = pfv.consume()
            bB = pfv.consume()
            if m >= 1:
                p.wait("tensor", "S", ws_k[m - 1])
            for k in range(KB):
                base = k * W1
                p.op("tensor", "T", lambda t, k=k, base=base, c0=c0, bA=bA:
                     t.matmul(ps["pU"][:, 0:TF], bA[:, k * 128:(k + 1) * 128],
                              sb["k_xs1"][:, base + 1 + c0:base + 1 + c0 + TF],
                              start=(k == 0), stop=False))
                p.op("tensor", "T", lambda t, k=k, base=base, c0=c0, bB=bB:
                     t.matmul(ps["pU"][:, 0:TF], bB[:, k * 128:(k + 1) * 128],
                              sb["k_xs1"][:, base + c0:base + c0 + TF],
                              start=False, stop=False))
            p.op("tensor", "T", lambda t, m=m, c0=c0:
                 t.matmul(ps["pU"][:, 0:TF], sb["c_swk"][0:1, m * 128:(m + 1) * 128],
                          sb["k_rb"][0:1, 1 + c0:1 + c0 + TF], start=False, stop=False))
            p.op("tensor", "T", lambda t, m=m, c0=c0:
                 t.matmul(ps["pU"][:, 0:TF], sb["c_swk"][0:1, D + m * 128:D + (m + 1) * 128],
                          sb["k_rb"][0:1, c0:c0 + TF], start=False, stop=True))
            pfv.done()
            pfv.done()
            wt = p.val("T")
            p.wait("scalar", "T", wt)
            p.op("scalar", "S", lambda s: s.activation(sb["k_t1"][:, 0:TF],
                                                       ps["pU"][:, 0:TF], AF.Relu))
            ws_k.append(p.val("S"))
            p.op("scalar", "S", lambda s, m=m:
                 s.activation(sb["k_ka"][:, m * TF:(m + 1) * TF], sb["k_t1"][:, 0:TF],
                              AF.Square))
        ws = p.val("S")
        p.wait("tensor", "S", ws)
        for m in range(KB):
            wbuf = pfv.consume()
            pp = ps["pT"] if m % 2 == 0 else ps["pU"]
            for k in range(KB):
                p.op("tensor", "T", lambda t, k=k, wbuf=wbuf, pp=pp:
                     t.matmul(pp[:, 0:TF], wbuf[:, k * 128:(k + 1) * 128],
                              sb["k_ka"][:, k * TF:(k + 1) * TF],
                              start=(k == 0), stop=(k == KB - 1)))
            pfv.done()
            wt = p.val("T")
            p.wait("vector", "T", wt)
            p.wait("vector", "S")
            p.op("vector", "V", lambda v, m=m, pp=pp:
                 v.tensor_tensor(sb["k_w2s"], pp[:, 0:TF],
                                 sb["k_r"][:, m * TF:(m + 1) * TF], ALU.mult))
            base = m * W1
            p.op("vector", "V", lambda v, base=base, c0=c0:
                 v.tensor_tensor(sb["k_w2s"], sb["k_w2s"],
                                 sb["k_xs1"][:, base + 1 + c0:base + 1 + c0 + TF], ALU.add))
            p.op("vector", "V", lambda v, m=m:
                 v.tensor_copy(sb["k_out"][:, m * TF:(m + 1) * TF], sb["k_w2s"]))
            wv = p.val("V")
            p.wait("scalar", "V", wv)
            p.op("scalar", "S", lambda s: s.activation(sb["k_t1"][:, 0:TF], sb["k_w2s"],
                                                       AF.Square))
            ws = p.val("S")
            p.wait("tensor", "S", ws)
            p.wait("tensor", "V", wv)
            p.op("tensor", "T", lambda t, m=m:
                 t.matmul(ps["pA"][0:1, 0:TF], sb["c_ones32"], sb["k_w2s"],
                          start=(m == 0), stop=(m == KB - 1)))
            p.op("tensor", "T", lambda t, m=m:
                 t.matmul(ps["pA"][32:33, 0:TF], sb["c_ones32"], sb["k_t1"][:, 0:TF],
                          start=(m == 0), stop=(m == KB - 1)))
        wt = p.val("T")
        p.wait("vector", "T", wt)
        p.op("vector", "V", lambda v: v.tensor_scalar(KRr(0)[:, 0:TF], ps["pA"][0:1, 0:TF],
                                                      1.0 / D, None, ALU.mult))
        p.op("vector", "V", lambda v: v.tensor_scalar(KRr(1)[:, 0:TF], ps["pA"][32:33, 0:TF],
                                                      1.0 / D, None, ALU.mult))
        p.op("vector", "V", lambda v: v.tensor_tensor(KRr(3)[:, 0:TF], KRr(0)[:, 0:TF],
                                                      KRr(0)[:, 0:TF], ALU.mult))
        p.op("vector", "V", lambda v: v.tensor_tensor(KRr(1)[:, 0:TF], KRr(1)[:, 0:TF],
                                                      KRr(3)[:, 0:TF], ALU.subtract))
        wv = p.val("V")
        p.wait("scalar", "V", wv)
        p.op("scalar", "S", lambda s: s.activation(KRr(1)[:, 0:TF], KRr(1)[:, 0:TF],
                                                   AF.Ln, bias=sb["c_eps"][0:1, 0:1]))
        p.op("scalar", "S", lambda s: s.activation(KRr(1)[:, 0:TF], KRr(1)[:, 0:TF],
                                                   AF.Exp, scale=sb["c_nhalf"][0:1, 0:1]))
        ws = p.val("S")
        p.wait("tensor", "S", ws)
        p.op("tensor", "T", lambda t: t.matmul(ps["pR"][:, 0:TF], sb["c_onesr32"],
                                               KRr(0)[:, 0:TF], start=True, stop=True))
        p.op("tensor", "T", lambda t: t.matmul(ps["pV"][:, 0:TF], sb["c_onesr32"],
                                               KRr(1)[:, 0:TF], start=True, stop=True))
        wt = p.val("T")
        p.wait("vector", "T", wt)
        for m in range(KB):
            p.op("vector", "V", lambda v, m=m:
                 v.tensor_tensor(sb["k_out"][:, m * TF:(m + 1) * TF],
                                 sb["k_out"][:, m * TF:(m + 1) * TF],
                                 ps["pR"][:, 0:TF], ALU.subtract))
            p.op("vector", "V", lambda v, m=m:
                 v.tensor_tensor(sb["k_out"][:, m * TF:(m + 1) * TF],
                                 sb["k_out"][:, m * TF:(m + 1) * TF],
                                 ps["pV"][:, 0:TF], ALU.mult))
            wv = p.val("V")
            p.wait("sync", "V", wv)
            p.op("sync", "qs", lambda s, m=m, c0=c0:
                 s.dma_start(out=dr["outT"][m * 128:(m + 1) * 128, t0 + c0:t0 + c0 + TF],
                             in_=sb["k_out"][:, m * TF:(m + 1) * TF]))
        p.flush()


_NC_CACHE = {}


def _get_nc():
    if "nc" not in _NC_CACHE:
        _NC_CACHE["nc"] = build()
    return _NC_CACHE["nc"]


def kernel(**inputs):
    x = np.ascontiguousarray(inputs["L0_attn_x"])

    def g(k):
        return np.ascontiguousarray(inputs[k])

    def col(v, nb):
        return np.ascontiguousarray(v.reshape(nb, 128).T)

    shared = {
        "W_in_T": np.ascontiguousarray(g("L0_attn_in_proj_w").T),
        "b_in": col(g("L0_attn_in_proj_b"), 32),
        "lmg": col(g("L0_attn_ln_mamba_g"), KB),
        "lmb": col(g("L0_attn_ln_mamba_b"), KB),
        "leg": col(g("L0_attn_ln_moe_g"), KB),
        "leb": col(g("L0_attn_ln_moe_b"), KB),
        "convw": np.ascontiguousarray(
            g("L0_attn_conv_w").reshape(IB, 128, KC).transpose(1, 0, 2).reshape(128, IB * KC)),
        "convb": col(g("L0_attn_conv_b"), IB),
        "W_x_T": np.ascontiguousarray(
            g("L0_attn_x_proj_w").T.reshape(IB, 128, 96).transpose(1, 0, 2).reshape(128, IB * 96)),
        "W_dt_T": np.ascontiguousarray(g("L0_attn_dt_proj_w").T),
        "dtb": col(g("L0_attn_dt_proj_b"), IB),
        "A_log": np.ascontiguousarray(
            g("L0_attn_A_log").reshape(IB, 128, NS).transpose(1, 0, 2).reshape(128, IB * NS)),
        "Dsk": col(g("L0_attn_D_skip"), IB),
        "W_out_T": np.ascontiguousarray(g("L0_attn_out_proj_w").T),
        "router_T": np.ascontiguousarray(
            g("L0_attn_router_w").T.reshape(KB, 128, E).transpose(1, 0, 2).reshape(128, KB * E)),
        "w1_T": np.ascontiguousarray(g("L0_attn_w1").transpose(0, 2, 1)),
        "b1": np.ascontiguousarray(g("L0_attn_b1").reshape(E * FB, 128).T),
        "w2_T": np.ascontiguousarray(g("L0_attn_w2").transpose(0, 2, 1)),
        "b2r": g("L0_attn_b2"),
        "W_r_T": np.ascontiguousarray(g("L0_ffn_W_r").T),
        "W_k_T": np.ascontiguousarray(g("L0_ffn_W_k").T),
        "W_v_T": np.ascontiguousarray(g("L0_ffn_W_v").T),
        "mur": col(g("L0_ffn_mu_r"), KB),
        "muk": col(g("L0_ffn_mu_k"), KB),
    }
    in_maps = []
    for c in range(8):
        b, h = c // 2, c % 2
        m = dict(shared)
        m["xTm"] = np.ascontiguousarray(x[b, 4096 * h:4096 * (h + 1), :].T)
        m["xTp"] = (np.ascontiguousarray(x[b, 0:4096, :].T) if h == 1
                    else np.zeros((D, T), np.float32))
        m["maskv"] = np.full((128, 1), float(h), np.float32)
        in_maps.append(m)
    nc = _get_nc()
    res = run_bass_kernel_spmd(nc, in_maps, core_ids=list(range(8)))
    out = np.empty((4, 8192, 1024), np.float32)
    for c in range(8):
        b, h = c // 2, c % 2
        out[b, 4096 * h:4096 * (h + 1), :] = res.results[c]["outT"].T
    return out


# version 05
# speedup vs baseline: 1.5075x; 1.0666x over previous
import sys
sys.path.insert(0, '/opt/trn_rl_repo')
import numpy as np
import concourse.bass as bass
import concourse.mybir as mybir
from concourse.bass_utils import run_bass_kernel_spmd

F32 = mybir.dt.float32
BF16 = mybir.dt.bfloat16
AF = mybir.ActivationFunctionType
ALU = mybir.AluOpType

D, DI, NS, DTR, KC, E, DFF = 1024, 2048, 16, 64, 4, 8, 2048
T = 4096
TC = 512
NCH = T // TC
KB, IB, FB = 8, 16, 16
TB = 2 * TC
W1 = TB + 1
TF = 256
EPS = 1e-5
ENGS = ("tensor", "vector", "scalar", "gpsimd", "sync")


class P:
    def __init__(self, nc, block):
        self.nc, self.block = nc, block
        self.q = {e: [] for e in ENGS}
        self.sem, self.cnt = {}, {}

    def new_sem(self, name):
        self.sem[name] = self.nc.alloc_semaphore(name=name)
        self.cnt[name] = 0

    def flush(self):
        for e in ENGS:
            fns = self.q[e]
            if fns:
                def runner(ex, fns=tuple(fns)):
                    for f in fns:
                        f(ex)
                getattr(self.block, e)(runner)
                self.q[e] = []

    def op(self, eng, sname, fn):
        by = 16 if sname.startswith("q") else 1
        self.cnt[sname] += by
        s = self.sem[sname]

        def g(ex, fn=fn, s=s, by=by):
            fn(ex).then_inc(s, by)
        self.q[eng].append(g)
        return self.cnt[sname]

    def val(self, name):
        return self.cnt[name]

    def wait(self, eng, sname, val=None):
        v = self.cnt[sname] if val is None else val
        if v > 0:
            s = self.sem[sname]
            self.q[eng].append(lambda ex, s=s, v=v: ex.wait_ge(s, v))


class Pref:

    def __init__(self, p, bufs):
        self.p, self.bufs = p, bufs
        self.loads, self.q, self.wts = [], [], []
        self.ni = self.ci = 0

    def add(self, fn):
        self.loads.append(fn)

    def _issue(self):
        i = self.ni
        b = self.bufs[i % len(self.bufs)]
        j = i - len(self.bufs)
        if j >= 0:
            self.p.wait("gpsimd", "T", self.wts[j])
        fn = self.loads[i]
        self.p.op("gpsimd", "qc", lambda g, fn=fn, b=b: fn(g, b))
        self.q.append(self.p.val("qc"))
        self.ni += 1

    def consume(self):
        while self.ni <= self.ci + 1 and self.ni < len(self.loads):
            j = self.ni - len(self.bufs)
            if j >= 0 and j >= len(self.wts):
                if self.ni <= self.ci:
                    raise RuntimeError("Pref underflow: done() missing")
                break
            self._issue()
        self.p.wait("tensor", "qc", self.q[self.ci])
        b = self.bufs[self.ci % len(self.bufs)]
        self.ci += 1
        return b

    def done(self):
        self.wts.append(self.p.val("T"))


def build(debug=False):
    nc = bass.Bass("TRN2", target_bir_lowering=False)
    dr, sb, ps = {}, {}, {}

    def din(name, shape, dt=F32):
        dr[name] = nc.dram_tensor(name, shape, dt, kind="ExternalInput").ap()

    def dscr(name, shape, dt):
        dr[name] = nc.dram_tensor(name, shape, dt, kind="Internal").ap()

    din("xTm", [D, T]); din("xTp", [D, T]); din("maskv", [128, 1])
    din("W_in_T", [D, 2 * DI]); din("b_in", [128, 32])
    din("lmg", [128, KB]); din("lmb", [128, KB])
    din("leg", [128, KB]); din("leb", [128, KB])
    din("convw", [128, IB * KC]); din("convb", [128, IB])
    din("W_x_T", [128, IB * 96])
    din("W_dt_T", [DTR, DI]); din("dtb", [128, IB])
    din("A_log", [128, IB * NS]); din("Dsk", [128, IB])
    din("W_out_T", [DI, D])
    din("router_T", [128, KB * E])
    din("w1_T", [E, D, DFF]); din("b1", [128, E * FB])
    din("w2_T", [E, DFF, D]); din("b2r", [8, D])
    din("W_r_T", [D, D]); din("W_k_T", [D, D]); din("W_v_T", [D, D])
    din("mur", [128, KB]); din("muk", [128, KB])
    dr["outT"] = nc.dram_tensor("outT", [D, T], F32, kind="ExternalOutput").ap()

    dscr("sWin", [32, 128, KB * 128], BF16)
    dscr("sWout", [KB, 128, IB * 128], BF16)
    dscr("sW1", [E, FB, 128, KB * 128], BF16)
    dscr("sW2", [KB, E, 128, FB * 128], BF16)
    dscr("sWr", [2, KB, 128, KB * 128], BF16)
    dscr("sWk", [2, KB, 128, KB * 128], BF16)
    dscr("sWv", [KB, 128, KB * 128], BF16)
    scrkind = "ExternalOutput" if debug else "Internal"
    dr["s_yg"] = nc.dram_tensor("s_yg", [IB, 128, 2 * T], BF16, kind=scrkind).ap()
    dr["s_sz"] = nc.dram_tensor("s_sz", [IB, 128, 2 * T], BF16, kind=scrkind).ap()
    dr["s_xcD"] = nc.dram_tensor("s_xcD", [IB, 128, 2 * T], BF16, kind=scrkind).ap()
    if debug:
        dr["s_dbgA"] = nc.dram_tensor("s_dbgA", [128, 2 * T], F32, kind="ExternalOutput").ap()
        dr["s_dbgP"] = nc.dram_tensor("s_dbgP", [128, 2 * T], F32, kind="ExternalOutput").ap()
        dr["s_dbgD"] = nc.dram_tensor("s_dbgD", [128, 2 * T], F32, kind="ExternalOutput").ap()
        dr["s_dbgU"] = nc.dram_tensor("s_dbgU", [128, 2 * T], F32, kind="ExternalOutput").ap()
        dr["s_dbgH"] = nc.dram_tensor("s_dbgH", [128, 2 * T], F32, kind="ExternalOutput").ap()
        for nm in ["s_dbgHA", "s_dbgXM", "s_dbgAC", "s_dbgX2", "s_dbgXS"]:
            dr[nm] = nc.dram_tensor(nm, [KB, 128, 2 * T], BF16, kind="ExternalOutput").ap()
        dr["s_dbgW"] = nc.dram_tensor("s_dbgW", [1, 8, 2 * T], F32, kind="ExternalOutput").ap()
        dr["s_dbgLG"] = nc.dram_tensor("s_dbgLG", [128, (2 * T) // 128 * 8], F32,
                                       kind="ExternalOutput").ap()
        dr["s_dbgRT"] = nc.dram_tensor("s_dbgRT", [128, KB * E], BF16,
                                       kind="ExternalOutput").ap()
        dr["s_dbgBR"] = nc.dram_tensor("s_dbgBR", [1, 8], BF16,
                                       kind="ExternalOutput").ap()
        dr["s_dbgT2"] = nc.dram_tensor("s_dbgT2", [128, (2 * T) // 128 * 24], F32,
                                       kind="ExternalOutput").ap()

    def sbuf(name, shape, dt):
        sb[name] = nc.alloc_sbuf_tensor(name, shape, dt).ap()

    sbuf("c_lmb", [128, KB], F32); sbuf("c_leg", [128, KB], F32)
    sbuf("c_leb", [128, KB], F32)
    sbuf("c_convw", [128, IB * KC], F32); sbuf("c_convb", [128, IB], F32)
    sbuf("c_dtb", [128, IB], F32); sbuf("c_A", [128, IB * NS], F32)
    sbuf("c_D", [128, IB], F32)
    sbuf("c_mur", [128, KB], F32); sbuf("c_muk", [128, KB], F32)
    sbuf("c_mask", [128, 1], F32)
    sbuf("c_Wx", [128, IB * 96], BF16)
    sbuf("c_Wdt", [DTR, DI], BF16)
    sbuf("c_rout", [128, KB * E], BF16)
    sbuf("c_ones", [128, 1], BF16); sbuf("c_ones32", [128, 1], F32)
    sbuf("c_eps", [128, 1], F32); sbuf("c_neg1", [128, 1], F32)
    sbuf("c_nhalf", [128, 1], F32)
    sbuf("c_onesr", [1, 128], BF16); sbuf("c_onesr32", [1, 128], F32)
    sbuf("c_sin", [1, 2 * DI], BF16)
    sbuf("c_swr", [1, 2 * D], BF16); sbuf("c_swk", [1, 2 * D], BF16)
    sbuf("c_bein", [128, 32], F32)
    sbuf("c_bew1", [128, E * FB], F32)
    sbuf("c_b2", [8, D], BF16)
    sbuf("c_ber", [1, 8], BF16)
    sbuf("st_h", [128, IB * NS], F32)
    sbuf("st_hal", [128, IB * (KC - 1)], F32)
    sbuf("st_x1p", [128, KB], BF16)
    sbuf("st_p2", [1, 1], F32)
    sbuf("wA", [128, IB * 128], BF16)
    sbuf("wB", [128, IB * 128], BF16)
    sbuf("k_yg", [128, 2 * TC], BF16)
    sbuf("k_x", [128, TB], F32)
    sbuf("k_rows", [1, 8 * TC], F32)
    sbuf("k_rb", [1, TB + 2], BF16)
    sbuf("k_w", [8, 2 * TC], F32)
    sbuf("k_wb", [8, 2 * TC], BF16)
    sbuf("k_tm", [128, 8], F32)
    sbuf("k_tc", [128, 8], F32)

    ARENA = 120 * 1024
    arena_t = nc.alloc_sbuf_tensor("arena", [128, ARENA], mybir.dt.uint8)
    arena_f32 = arena_t.ap().bitcast(F32)
    arena_bf = arena_t.ap().bitcast(BF16)
    _off = {"f": 0, "k": 0}

    def view(group, name, cols, dt):
        esz = mybir.dt.size(dt)
        off = (_off[group] + 63) // 64 * 64
        base = arena_f32 if dt == F32 else arena_bf
        sb[name] = base[:, off // esz: off // esz + cols]
        _off[group] = off + cols * esz
        assert _off[group] <= ARENA, (name, _off[group])

    view("f", "f_xk", 2 * TC, F32)
    view("f", "f_rows", 8 * TC, F32)
    view("f", "f_rb", TC, BF16)
    view("f", "f_xr1", KB * TC, BF16)
    view("f", "f_xi", TC + KC - 1, F32)
    view("f", "f_cva", TC, F32)
    view("f", "f_cvb", TC, F32)
    view("f", "f_xc", IB * TC, BF16)
    view("f", "f_proj", TC, F32)
    view("f", "f_dtr", TC, BF16)
    view("f", "f_dt", 2 * TC, F32)
    view("f", "f_dtx", 2 * TC, BF16)
    view("f", "f_Br", NS * TC, BF16)
    view("f", "f_Cr", NS * TC, BF16)
    view("f", "f_BC", TC, BF16)
    view("f", "f_dA", 2 * (TC + 2), F32)
    view("f", "f_u", TC + 2, BF16)
    view("f", "f_h", TC + 2, BF16)
    view("f", "f_ya", TC, F32)
    view("f", "f_t0", TC, BF16)
    view("f", "f_yg", TC, BF16)
    view("k", "k_ha", KB * TB, BF16)
    view("k", "k_xm", KB * TB, BF16)
    view("k", "k_x2a", KB * TB, BF16)
    view("k", "k_acc", KB * TB, BF16)
    view("k", "k_xs1", KB * W1, BF16)
    view("k", "k_h1", FB * TC, BF16)
    view("k", "k_r", KB * TF, BF16)
    view("k", "k_ka", KB * TF, BF16)
    view("k", "k_out", KB * TF, F32)
    view("k", "k_wr", TC, BF16)
    view("k", "k_dl", TC, F32)
    view("k", "k_t0", TC, BF16)
    view("k", "k_t1", TC, F32)
    view("k", "k_w2s", TF, F32)
    sb["f_rb"] = sb["f_rb"][0:1, :]
    sb["u_f32"] = sb["f_xc"].bitcast(F32)[:, 0:2 * DI]
    sb["u_bf"] = sb["f_Br"][:, 0:2 * DI]

    import ml_dtypes
    oh32 = np.zeros((32, 32 * 128), np.float32)
    for r in range(32):
        oh32[r, r * 128:(r + 1) * 128] = 1.0
    dr["i_oh32"] = nc.inline_tensor(oh32.astype(ml_dtypes.bfloat16), name="i_oh32").ap()
    dr["i_id"] = nc.inline_tensor(np.eye(128, dtype=ml_dtypes.bfloat16), name="i_id").ap()
    oh8 = np.zeros((8, 8 * 128), np.float32)
    for r in range(8):
        oh8[r, r * 128:(r + 1) * 128] = 1.0
    dr["i_oh8"] = nc.inline_tensor(oh8.astype(ml_dtypes.bfloat16), name="i_oh8").ap()
    sbuf("c_oh32", [32, 32 * 128], BF16)
    sbuf("c_id", [128, 128], BF16)
    sbuf("c_oh8", [8, 8 * 128], BF16)

    def psum(name, shape):
        ps[name] = nc.alloc_psum_tensor(name, shape, F32).ap()

    psum("pA", [128, 512]); psum("pB", [128, 512]); psum("pR", [128, 512])
    psum("pT", [128, 512]); psum("pU", [128, 512]); psum("pV", [128, 512])
    psum("pW", [128, 512]); psum("pX", [128, 512])

    with nc.Block() as block:
        p = P(nc, block)
        for s in ["qs", "qg", "qc", "T", "V", "S"]:
            p.new_sem(s)
        _setup(p, dr, sb, ps)
        for ch in range(NCH):
            _front(p, dr, sb, ps, ch, True)
            _scan(p, dr, sb, ps, ch, True)
        _back(p, dr, sb, ps, NCH - 2, True)
        for ch in range(0, NCH, 2):
            _front(p, dr, sb, ps, ch, False)
            _scan(p, dr, sb, ps, ch, False)
            _front(p, dr, sb, ps, ch + 1, False)
            _scan(p, dr, sb, ps, ch + 1, False)
            _back(p, dr, sb, ps, ch, False)
        p.wait("sync", "qs"); p.wait("sync", "qg"); p.wait("sync", "qc")
        p.flush()
    return nc


def _setup(p, dr, sb, ps):
    for dst, src in [("c_lmb", "lmb"), ("c_leg", "leg"), ("c_leb", "leb"),
                     ("c_convw", "convw"), ("c_convb", "convb"),
                     ("c_dtb", "dtb"), ("c_D", "Dsk"), ("c_mur", "mur"),
                     ("c_muk", "muk"), ("c_mask", "maskv"),
                     ("c_bein", "b_in"), ("c_bew1", "b1")]:
        p.op("sync", "qs", lambda s, d=sb[dst], x=dr[src]: s.dma_start(out=d, in_=x))
    for dst, src in [("c_Wx", "W_x_T"), ("c_Wdt", "W_dt_T"),
                     ("c_rout", "router_T"), ("c_b2", "b2r")]:
        p.op("gpsimd", "qg", lambda g, d=sb[dst], x=dr[src]: g.dma_start(out=d, in_=x))
    p.op("sync", "qs", lambda s: s.dma_start(out=sb["c_A"], in_=dr["A_log"]))
    p.wait("scalar", "qs")
    p.op("scalar", "S", lambda s: s.activation(sb["c_A"], sb["c_A"], AF.Exp))
    p.op("vector", "V", lambda v: v.memset(sb["c_ones32"], 1.0))
    p.op("vector", "V", lambda v: v.memset(sb["c_eps"], EPS))
    p.op("vector", "V", lambda v: v.memset(sb["c_neg1"], -1.0))
    p.op("vector", "V", lambda v: v.memset(sb["c_nhalf"], -0.5))
    p.op("vector", "V", lambda v: v.memset(sb["c_onesr32"], 1.0))
    p.op("vector", "V", lambda v: v.tensor_copy(sb["c_ones"], sb["c_ones32"]))
    p.op("vector", "V", lambda v: v.tensor_copy(sb["c_onesr"], sb["c_onesr32"]))
    for dst, srcn in [("c_oh32", "i_oh32"), ("c_id", "i_id"), ("c_oh8", "i_oh8")]:
        p.op("sync", "qs", lambda s, d=sb[dst], x=dr[srcn]: s.dma_start(out=d, in_=x))
    for z in ["st_h", "st_hal", "st_x1p", "st_p2"]:
        p.op("vector", "V", lambda v, z=z: v.memset(sb[z], 0.0))
    p.op("vector", "V", lambda v: v.memset(sb["f_dA"][:, 0:2], 0.0))
    p.op("vector", "V", lambda v: v.memset(sb["f_dA"][:, TC + 2:TC + 4], 0.0))
    p.op("vector", "V", lambda v: v.memset(sb["f_u"][:, 0:1], 0.0))
    wv = p.val("V")
    p.wait("scalar", "V", wv)
    p.op("scalar", "S", lambda s: s.activation(sb["c_A"], sb["c_A"], AF.Copy,
                                               scale=sb["c_neg1"][:, 0:1]))
    p.flush()

    p.wait("vector", "qg")
    p.wait("vector", "qs")
    for k in range(KB):
        p.op("vector", "V", lambda v, k=k:
             v.tensor_scalar(sb["c_rout"][:, k * E:(k + 1) * E],
                             sb["c_rout"][:, k * E:(k + 1) * E],
                             sb["c_leg"][:, k:k + 1], None, ALU.mult))
    p.op("sync", "qs", lambda s: s.dma_start(out=sb["u_f32"][:, 0:KB * E],
                                             in_=dr["router_T"]))
    p.wait("tensor", "qs")
    p.wait("tensor", "V")
    for k in range(KB):
        p.op("tensor", "T", lambda t, k=k:
             t.matmul(ps["pV"][0:1, 0:8], sb["c_leb"][:, k:k + 1],
                      sb["u_f32"][:, k * E:(k + 1) * E],
                      start=(k == 0), stop=(k == KB - 1)))
    wt = p.val("T")
    p.wait("vector", "T", wt)
    p.op("vector", "V", lambda v: v.tensor_copy(sb["c_ber"], ps["pV"][0:1, 0:8]))
    p.flush()

    FR = lambda i: sb["f_rows"][0:1, i * TC:(i + 1) * TC]
    KRr = lambda i: sb["k_rows"][0:1, i * TC:(i + 1) * TC]

    def family(src3, kb, M, nmb, gain, lnb, sumdst, befdst, store_fn):
        nseg = (M + 511) // 512
        for k in range(kb):
            fb, bb = sb["u_f32"][:, 0:M], sb["u_bf"][:, 0:M]
            p.wait("sync", "V"); p.wait("sync", "T")
            p.op("sync", "qs", lambda s, a=src3(k), fb=fb: s.dma_start(out=fb, in_=a))
            p.wait("vector", "qs")
            if gain is not None:
                p.op("vector", "V", lambda v, fb=fb, bb=bb, g=gain, k=k:
                     v.tensor_scalar(bb, fb, g[:, k:k + 1], None, ALU.mult))
            else:
                p.op("vector", "V", lambda v, fb=fb, bb=bb: v.tensor_copy(bb, fb))
            wv = p.val("V")
            p.wait("tensor", "V", wv)
            if lnb is not None:
                for r in range(nseg):
                    so, sw = r * 512, min(512, M - r * 512)
                    if k == 0:
                        p.op("vector", "V", lambda v, r=r, sw=sw:
                             v.memset(KRr(r)[:, 0:sw], 0.0))
                    wvz = p.val("V")
                    p.wait("tensor", "V", wvz)
                    p.op("tensor", "T", lambda t, fb=fb, so=so, sw=sw, lb=lnb, k=k:
                         t.matmul(ps["pV"][0:1, 0:sw], lb[:, k:k + 1],
                                  fb[:, so:so + sw], start=True, stop=True))
                    wt = p.val("T")
                    p.wait("vector", "T", wt)
                    p.op("vector", "V", lambda v, r=r, sw=sw:
                         v.tensor_tensor(KRr(r)[:, 0:sw], KRr(r)[:, 0:sw],
                                         ps["pV"][0:1, 0:sw], ALU.add))
            if sumdst is not None:
                for r in range(nseg):
                    so, sw = r * 512, min(512, M - r * 512)
                    if k == 0:
                        p.op("vector", "V", lambda v, r=r, sw=sw:
                             v.memset(FR(r)[:, 0:sw], 0.0))
                    wvz = p.val("V")
                    p.wait("tensor", "V", wvz)
                    p.op("tensor", "T", lambda t, bb=bb, so=so, sw=sw:
                         t.matmul(ps["pB"][0:1, 0:sw], sb["c_ones"],
                                  bb[:, so:so + sw], start=True, stop=True))
                    wt = p.val("T")
                    p.wait("vector", "T", wt)
                    p.op("vector", "V", lambda v, r=r, sw=sw:
                         v.tensor_tensor(FR(r)[:, 0:sw], FR(r)[:, 0:sw],
                                         ps["pB"][0:1, 0:sw], ALU.add))
                    if k == kb - 1:
                        wvv = p.val("V")
                        p.wait("scalar", "V", wvv)
                        p.op("scalar", "S", lambda s, so=so, sw=sw, r=r:
                             sumdst(s, so, sw, r))
            wv2 = p.val("V")
            p.wait("sync", "V", wv2)
            for mb in range(nmb):
                dsl, c0 = store_fn(k, mb)
                p.op("sync", "qs", lambda s, d=dsl, bb=bb, c0=c0:
                     s.dma_start(out=d, in_=bb[:, c0:c0 + 128]))
            p.wait("vector", "qs")
            p.wait("vector", "T")
            p.flush()
        if lnb is not None:
            wv = p.val("V")
            p.wait("tensor", "V", wv)
            nmb2 = M // 128
            for m in range(nmb2):
                r, c = divmod(m * 128, TC)
                p.op("tensor", "T", lambda t, r=r, c=c, m=m:
                     t.matmul(ps["pV"][:, m:m + 1],
                              KRr(r)[:, c:c + 128], sb["c_onesr32"][0:1, 0:1],
                              start=True, stop=True))
            wt = p.val("T")
            p.wait("vector", "T", wt)
            p.op("vector", "V", lambda v, nmb2=nmb2, befdst=befdst:
                 v.tensor_tensor(befdst[:, 0:nmb2], befdst[:, 0:nmb2],
                                 ps["pV"][:, 0:nmb2], ALU.add))
            p.flush()

    lmg = p.nc.alloc_sbuf_tensor("tmp_lmg", [128, KB], F32).ap()
    p.op("sync", "qs", lambda s: s.dma_start(out=lmg, in_=dr["lmg"]))
    p.wait("vector", "qs")
    p.flush()

    def sum_in(s, so, sw, r):
        return s.activation(sb["c_sin"][0:1, so:so + sw], FR(r)[:, 0:sw], AF.Copy)

    family(lambda k: dr["W_in_T"][k * 128:(k + 1) * 128, :], KB, 2 * DI, 32,
           lmg, sb["c_lmb"], sum_in, sb["c_bein"],
           lambda k, mb: (dr["sWin"][mb, :, k * 128:(k + 1) * 128], mb * 128))
    family(lambda k: dr["W_out_T"][k * 128:(k + 1) * 128, :], IB, D, KB,
           None, None, None, None,
           lambda k, mb: (dr["sWout"][mb, :, k * 128:(k + 1) * 128], mb * 128))
    for e in range(E):
        family(lambda k, e=e: dr["w1_T"][e, k * 128:(k + 1) * 128, :], KB, DFF, FB,
               sb["c_leg"], sb["c_leb"], None,
               sb["c_bew1"][:, e * FB:(e + 1) * FB],
               lambda k, mb, e=e: (dr["sW1"][e, mb, :, k * 128:(k + 1) * 128], mb * 128))
    for e in range(E):
        family(lambda k, e=e: dr["w2_T"][e, k * 128:(k + 1) * 128, :], FB, D, KB,
               None, None, None, None,
               lambda k, mb, e=e: (dr["sW2"][mb, e, :, k * 128:(k + 1) * 128], mb * 128))

    for wsrc, scr, mucol, sumr in [("W_r_T", "sWr", "c_mur", "c_swr"),
                                   ("W_k_T", "sWk", "c_muk", "c_swk")]:
        for var in range(2):
            for k in range(KB):
                fb, bb = sb["u_f32"][:, 0:D], sb["u_bf"][:, 0:D]
                tmp = sb["u_f32"][:, D:2 * D]
                p.wait("sync", "V"); p.wait("sync", "T")
                p.op("sync", "qs", lambda s, k=k, fb=fb, w=wsrc:
                     s.dma_start(out=fb, in_=dr[w][k * 128:(k + 1) * 128, :]))
                p.wait("vector", "qs")
                if var == 0:
                    p.op("vector", "V", lambda v, fb=fb, bb=bb, k=k, mc=mucol:
                         v.tensor_scalar(bb, fb, sb[mc][:, k:k + 1], None, ALU.mult))
                else:
                    p.op("vector", "V", lambda v, fb=fb, k=k, mc=mucol, tmp=tmp:
                         v.tensor_scalar(tmp, fb, sb[mc][:, k:k + 1], None, ALU.mult))
                    p.op("vector", "V", lambda v, fb=fb, bb=bb, tmp=tmp:
                         v.tensor_tensor(bb, fb, tmp, ALU.subtract))
                wv = p.val("V")
                p.wait("tensor", "V", wv)
                for r in range(2):
                    so = r * 512
                    if k == 0:
                        p.op("vector", "V", lambda v, r=r: v.memset(FR(r), 0.0))
                    wvz = p.val("V")
                    p.wait("tensor", "V", wvz)
                    p.op("tensor", "T", lambda t, bb=bb, so=so:
                         t.matmul(ps["pB"][0:1, 0:512], sb["c_ones"],
                                  bb[:, so:so + 512], start=True, stop=True))
                    wt = p.val("T")
                    p.wait("vector", "T", wt)
                    p.op("vector", "V", lambda v, r=r:
                         v.tensor_tensor(FR(r), FR(r), ps["pB"][0:1, 0:512], ALU.add))
                    if k == KB - 1:
                        wvv = p.val("V")
                        p.wait("scalar", "V", wvv)
                        p.op("scalar", "S", lambda s, so=so, r=r, sumr=sumr, var=var:
                             s.activation(sb[sumr][0:1, var * D + so:var * D + so + 512],
                                          FR(r), AF.Copy))
                wv2 = p.val("V")
                p.wait("sync", "V", wv2)
                for mb in range(KB):
                    p.op("sync", "qs", lambda s, scr=scr, var=var, mb=mb, k=k, bb=bb:
                         s.dma_start(out=dr[scr][var, mb, :, k * 128:(k + 1) * 128],
                                     in_=bb[:, mb * 128:(mb + 1) * 128]))
                p.wait("vector", "qs")
                p.wait("vector", "T")
                p.flush()
    family(lambda k: dr["W_v_T"][k * 128:(k + 1) * 128, :], KB, D, KB,
           None, None, None, None,
           lambda k, mb: (dr["sWv"][mb, :, k * 128:(k + 1) * 128], mb * 128))

    for e in ENGS:
        p.wait(e, "qs"); p.wait(e, "qg"); p.wait(e, "V"); p.wait(e, "S"); p.wait(e, "T")
    p.flush()


def _front(p, dr, sb, ps, ch, prefix):
    t0 = ch * TC
    half = 0 if prefix else 1
    xsrc = dr["xTp"] if prefix else dr["xTm"]
    do_z = (not prefix) or ch >= NCH - 2
    FR = lambda i: sb["f_rows"][0:1, i * TC:(i + 1) * TC]

    for k in range(KB):
        par = k % 2
        xk = sb["f_xk"][:, par * TC:(par + 1) * TC]
        p.wait("sync", "V"); p.wait("sync", "T"); p.wait("sync", "S")
        p.op("sync", "qs", lambda s, xk=xk, k=k:
             s.dma_start(out=xk, in_=xsrc[k * 128:(k + 1) * 128, t0:t0 + TC]))
        p.wait("scalar", "qs")
        p.wait("scalar", "T")
        p.op("scalar", "S", lambda s, xk=xk: s.activation(sb["f_cva"], xk, AF.Square))
        ws = p.val("S")
        p.wait("tensor", "qs")
        p.wait("tensor", "S", ws)
        p.op("tensor", "T", lambda t, xk=xk, k=k:
             t.matmul(ps["pA"][0:1, 0:TC], sb["c_ones32"], xk,
                      start=(k == 0), stop=(k == KB - 1)))
        p.op("tensor", "T", lambda t, k=k:
             t.matmul(ps["pA"][32:33, 0:TC], sb["c_ones32"], sb["f_cva"],
                      start=(k == 0), stop=(k == KB - 1)))
    wt = p.val("T")
    p.wait("vector", "T", wt)
    p.op("vector", "V", lambda v: v.tensor_scalar(FR(0), ps["pA"][0:1, 0:TC],
                                                  1.0 / D, None, ALU.mult))
    p.op("vector", "V", lambda v: v.tensor_scalar(FR(1), ps["pA"][32:33, 0:TC],
                                                  1.0 / D, None, ALU.mult))
    p.op("vector", "V", lambda v: v.tensor_tensor(FR(2), FR(0), FR(0), ALU.mult))
    p.op("vector", "V", lambda v: v.tensor_tensor(FR(2), FR(1), FR(2), ALU.subtract))
    wv = p.val("V")
    p.wait("scalar", "V", wv)
    p.op("scalar", "S", lambda s: s.activation(FR(3), FR(2), AF.Ln, bias=sb["c_eps"][0:1, 0:1]))
    p.op("scalar", "S", lambda s: s.activation(FR(3), FR(3), AF.Exp, scale=sb["c_nhalf"][0:1, 0:1]))
    ws = p.val("S")
    p.wait("vector", "S", ws)
    p.op("vector", "V", lambda v: v.tensor_tensor(FR(4), FR(0), FR(3), ALU.mult))
    p.op("vector", "V", lambda v: v.tensor_scalar(FR(4), FR(4), -1.0, None, ALU.mult))
    p.op("vector", "V", lambda v: v.tensor_copy(sb["f_rb"], FR(4)))
    wv = p.val("V")
    p.wait("tensor", "V", wv)
    p.op("tensor", "T", lambda t: t.matmul(ps["pR"][:, 0:TC], sb["c_onesr32"], FR(3),
                                           start=True, stop=True))
    wt = p.val("T")
    for k in range(KB):
        par = k % 2
        xk = sb["f_xk"][:, par * TC:(par + 1) * TC]
        p.wait("sync", "V")
        p.op("sync", "qs", lambda s, xk=xk, k=k:
             s.dma_start(out=xk, in_=xsrc[k * 128:(k + 1) * 128, t0:t0 + TC]))
        p.wait("vector", "qs")
        p.wait("vector", "T", wt)
        p.op("vector", "V", lambda v, xk=xk, k=k:
             v.tensor_tensor(sb["f_xr1"][:, k * TC:(k + 1) * TC], xk,
                             ps["pR"][:, 0:TC], ALU.mult))
    p.flush()

    mlist = list(range(IB)) + (list(range(IB, 32)) if do_z else [])
    pfi = Pref(p, [sb["wA"], sb["wB"]])
    for m in mlist:
        pfi.add(lambda g, b, m=m:
                g.dma_start(out=b[:, 0:KB * 128], in_=dr["sWin"][m]))
    p.wait("gpsimd", "T")
    wv_xr1 = p.val("V")
    ws_ev = []
    for i, m in enumerate(mlist):
        wb = pfi.consume()
        if i >= 2:
            p.wait("tensor", "S", ws_ev[i - 2])
        else:
            p.wait("tensor", "V", wv_xr1)
            p.wait("tensor", "S")
        pp = ps["pA"] if i % 2 == 0 else ps["pB"]
        for k in range(KB):
            p.op("tensor", "T", lambda t, wb=wb, k=k, pp=pp:
                 t.matmul(pp[:, 0:TC], wb[:, k * 128:(k + 1) * 128],
                          sb["f_xr1"][:, k * TC:(k + 1) * TC],
                          start=(k == 0), stop=False))
        p.op("tensor", "T", lambda t, m=m, pp=pp:
             t.matmul(pp[:, 0:TC], sb["c_sin"][0:1, m * 128:(m + 1) * 128],
                      sb["f_rb"], start=False, stop=True))
        pfi.done()
        wt = p.val("T")
        if m < IB:
            p.wait("scalar", "T", wt)
            p.wait("scalar", "V")
            p.op("scalar", "S", lambda s, pp=pp, m=m:
                 s.activation(sb["f_xi"][:, KC - 1:], pp[:, 0:TC], AF.Identity,
                              bias=sb["c_bein"][:, m:m + 1]))
            ws = p.val("S")
            ws_ev.append(ws)
            p.wait("vector", "S", ws)
            p.op("vector", "V", lambda v, m=m:
                 v.tensor_copy(sb["f_xi"][:, 0:KC - 1],
                               sb["st_hal"][:, m * (KC - 1):(m + 1) * (KC - 1)]))
            p.op("vector", "V", lambda v, m=m:
                 v.tensor_scalar(sb["f_cva"], sb["f_xi"][:, 0:TC],
                                 sb["c_convw"][:, m * KC:m * KC + 1],
                                 sb["c_convb"][:, m:m + 1], ALU.mult, ALU.add))
        if m < IB:
            for kk in range(1, KC):
                srcb, dstb = ((sb["f_cva"], sb["f_cvb"]) if kk % 2 == 1
                              else (sb["f_cvb"], sb["f_cva"]))
                p.op("vector", "V", lambda v, m=m, kk=kk, srcb=srcb, dstb=dstb:
                     v.scalar_tensor_tensor(dstb, sb["f_xi"][:, kk:kk + TC],
                                            sb["c_convw"][:, m * KC + kk:m * KC + kk + 1],
                                            srcb, ALU.mult, ALU.add))
            p.op("vector", "V", lambda v, m=m:
                 v.tensor_copy(sb["st_hal"][:, m * (KC - 1):(m + 1) * (KC - 1)],
                               sb["f_xi"][:, TC:TC + KC - 1]))
            wv = p.val("V")
            p.wait("scalar", "V", wv)
            fin = sb["f_cvb"] if KC % 2 == 0 else sb["f_cva"]
            p.op("scalar", "S", lambda s, m=m, fin=fin:
                 s.activation(sb["f_xc"][:, m * TC:(m + 1) * TC], fin, AF.Silu))
        else:
            mb = m - IB
            p.wait("scalar", "T", wt)
            p.wait("scalar", "qs")
            p.op("scalar", "S", lambda s, pp=pp, m=m:
                 s.activation(sb["f_t0"], pp[:, 0:TC], AF.Silu,
                              bias=sb["c_bein"][:, m:m + 1]))
            ws = p.val("S")
            ws_ev.append(ws)
            p.wait("sync", "S", ws)
            p.op("sync", "qs", lambda s, mb=mb:
                 s.dma_start(out=dr["s_sz"][mb, :, half * T + t0:half * T + t0 + TC],
                             in_=sb["f_t0"]))
        p.flush()

    ws = p.val("S")
    p.wait("tensor", "S", ws)
    for k in range(IB):
        p.op("tensor", "T", lambda t, k=k:
             t.matmul(ps["pR"][0:96, 0:TC], sb["c_Wx"][:, k * 96:(k + 1) * 96],
                      sb["f_xc"][:, k * TC:(k + 1) * TC],
                      start=(k == 0), stop=(k == IB - 1)))
    wt = p.val("T")
    p.wait("scalar", "T", wt)
    p.wait("scalar", "V")
    p.op("scalar", "S", lambda s: s.activation(sb["f_proj"][0:96, :],
                                               ps["pR"][0:96, 0:TC], AF.Copy))
    p.op("scalar", "S", lambda s: s.activation(sb["f_dtr"][0:DTR, :],
                                               ps["pR"][0:DTR, 0:TC], AF.Copy))
    p.op("scalar", "S", lambda s: s.activation(sb["f_BC"][0:32, :],
                                               ps["pR"][64:96, 0:TC], AF.Copy))
    ws = p.val("S")
    if "s_dbgP" in dr:
        p.wait("sync", "S", ws)
        p.op("sync", "qs", lambda s: s.dma_start(
            out=dr["s_dbgP"][0:96, half * T + t0:half * T + t0 + TC], in_=sb["f_proj"][0:96, :]))
    for m in range(IB):
        p.wait("vector", "qs")
        p.wait("vector", "S", ws)
        p.op("vector", "V", lambda v, m=m:
             v.tensor_scalar(sb["f_t0"], sb["f_xc"][:, m * TC:(m + 1) * TC],
                             sb["c_D"][:, m:m + 1], None, ALU.mult))
        wv = p.val("V")
        p.wait("sync", "V", wv)
        p.op("sync", "qs", lambda s, m=m:
             s.dma_start(out=dr["s_xcD"][m, :, half * T + t0:half * T + t0 + TC],
                         in_=sb["f_t0"]))
    p.wait("tensor", "S", ws)
    p.wait("tensor", "V")
    p.wait("scalar", "V")
    wsl = []
    for n in range(2 * NS):
        dst = sb["f_Br"] if n < NS else sb["f_Cr"]
        nn = n % NS
        col = n if n < NS else 16 + nn
        bank = "pV" if n % 2 == 0 else "pW"
        if n >= 2:
            p.wait("tensor", "S", wsl[n - 2])
        p.op("tensor", "T", lambda t, col=col, bank=bank:
             t.matmul(ps[bank][:, 0:TC], sb["c_oh32"][:, col * 128:(col + 1) * 128],
                      sb["f_BC"][0:32, :], start=True, stop=True))
        wt = p.val("T")
        p.wait("scalar", "T", wt)
        p.op("scalar", "S", lambda s, dst=dst, nn=nn, bank=bank:
             s.activation(dst[:, nn * TC:(nn + 1) * TC], ps[bank][:, 0:TC], AF.Copy))
        wsl.append(p.val("S"))
    p.flush()


def _scan(p, dr, sb, ps, ch, prefix):
    t0 = ch * TC
    half = 0 if prefix else 1
    do_y = (not prefix) or ch >= NCH - 2

    if True:
        p.op("vector", "V", lambda v: v.memset(sb["f_dA"][:, 0:2], 0.0))
        p.op("vector", "V", lambda v: v.memset(sb["f_dA"][:, TC + 2:TC + 4], 0.0))
        p.op("vector", "V", lambda v: v.memset(sb["f_u"][:, 0:1], 0.0))
        p.wait("vector", "V")
    ws_sp = []
    for m in range(IB):
        par = m % 2
        dt = sb["f_dt"][:, par * TC:(par + 1) * TC]
        dtx = sb["f_dtx"][:, par * TC:(par + 1) * TC]
        if m >= 2:
            p.wait("tensor", "S", ws_sp[m - 2])
        else:
            p.wait("tensor", "S")
            p.wait("tensor", "V")
        pp = ps["pA"] if par == 0 else ps["pB"]
        p.op("tensor", "T", lambda t, m=m, pp=pp:
             t.matmul(pp[:, 0:TC], sb["c_Wdt"][:, m * 128:(m + 1) * 128],
                      sb["f_dtr"][0:DTR, :], start=True, stop=True))
        wt = p.val("T")
        p.wait("scalar", "T", wt)
        p.wait("scalar", "V")
        p.op("scalar", "S", lambda s, m=m, pp=pp:
             s.activation(sb["f_cva"], pp[:, 0:TC], AF.Abs, bias=sb["c_dtb"][:, m:m + 1]))
        p.op("scalar", "S", lambda s: s.activation(sb["f_cva"], sb["f_cva"], AF.Exp,
                                                   scale=sb["c_neg1"][:, 0:1]))
        p.op("scalar", "S", lambda s: s.activation(sb["f_cva"], sb["f_cva"], AF.Ln,
                                                   bias=1.0))
        p.op("scalar", "S", lambda s, m=m, pp=pp:
             s.activation(sb["f_cvb"], pp[:, 0:TC], AF.Relu, bias=sb["c_dtb"][:, m:m + 1]))
        ws = p.val("S")
        ws_sp.append(ws)
        p.wait("vector", "S", ws)
        p.op("vector", "V", lambda v, dt=dt: v.tensor_tensor(dt, sb["f_cva"],
                                                             sb["f_cvb"], ALU.add))
        mask = sb["c_mask"][:, 0:1] if prefix else 1.0
        p.op("vector", "V", lambda v, dt=dt, dtx=dtx, m=m, mask=mask:
             v.scalar_tensor_tensor(dtx, dt, mask, sb["f_xc"][:, m * TC:(m + 1) * TC],
                                    ALU.mult, ALU.mult))
        if do_y:
            p.op("vector", "V", lambda v: v.memset(sb["f_ya"], 0.0))
        wv_dt = p.val("V")
        if m == 0 and "s_dbgD" in dr:
            p.wait("sync", "V", wv_dt)
            p.op("sync", "qs", lambda s, dt=dt: s.dma_start(
                out=dr["s_dbgD"][:, half * T + t0:half * T + t0 + TC], in_=dt))
        p.wait("scalar", "V", wv_dt)
        wv_scan = []
        for n in range(NS):
            npar = n % 2
            dA = sb["f_dA"][:, npar * (TC + 2):(npar + 1) * (TC + 2)]
            if n >= 2:
                p.wait("scalar", "V", wv_scan[n - 2])
            p.op("scalar", "S", lambda s, dA=dA, dt=dt, m=m, n=n:
                 s.activation(dA[:, 2:], dt, AF.Exp,
                              scale=sb["c_A"][:, m * NS + n:m * NS + n + 1]))
            ws = p.val("S")
            p.op("vector", "V", lambda v, dtx=dtx, n=n:
                 v.tensor_tensor(sb["f_u"][:, 2:], dtx,
                                 sb["f_Br"][:, n * TC:(n + 1) * TC], ALU.mult))
            p.op("vector", "V", lambda v, m=m, n=n:
                 v.tensor_copy(sb["f_u"][:, 1:2],
                               sb["st_h"][:, m * NS + n:m * NS + n + 1]))
            p.wait("vector", "S", ws)
            p.wait("vector", "V")
            p.op("vector", "V", lambda v, dA=dA:
                 v.tensor_tensor_scan(sb["f_h"], dA, sb["f_u"], 0.0,
                                      ALU.mult, ALU.add))
            wv_scan.append(p.val("V"))
            p.wait("vector", "V")
            p.op("vector", "V", lambda v, m=m, n=n:
                 v.tensor_copy(sb["st_h"][:, m * NS + n:m * NS + n + 1],
                               sb["f_h"][:, TC + 1:TC + 2]))
            if m == 0 and n == 0 and "s_dbgU" in dr:
                wvx = p.val("V")
                p.wait("gpsimd", "V", wvx)
                p.op("gpsimd", "qg", lambda g, dA=dA: g.dma_start(
                    out=dr["s_dbgA"][:, half * T + t0:half * T + t0 + TC], in_=dA[:, 2:]))
                p.op("gpsimd", "qg", lambda g: g.dma_start(
                    out=dr["s_dbgU"][:, half * T + t0:half * T + t0 + TC], in_=sb["f_u"][:, 2:]))
                p.op("gpsimd", "qg", lambda g: g.dma_start(
                    out=dr["s_dbgH"][:, half * T + t0:half * T + t0 + TC], in_=sb["f_h"][:, 2:]))
                p.wait("vector", "qg")
            if do_y:
                p.op("vector", "V", lambda v, n=n:
                     v.tensor_tensor(sb["f_t0"], sb["f_h"][:, 2:],
                                     sb["f_Cr"][:, n * TC:(n + 1) * TC], ALU.mult))
                p.op("vector", "V", lambda v:
                     v.tensor_tensor(sb["f_ya"], sb["f_ya"], sb["f_t0"], ALU.add))
        if do_y:
            p.wait("vector", "qs")
            p.wait("gpsimd", "qs")
            p.wait("gpsimd", "V")
            p.op("gpsimd", "qg", lambda g, m=m:
                 g.dma_start(out=sb["f_t0"],
                             in_=dr["s_xcD"][m, :, half * T + t0:half * T + t0 + TC]))
            p.op("gpsimd", "qg", lambda g, m=m:
                 g.dma_start(out=sb["f_yg"],
                             in_=dr["s_sz"][m, :, half * T + t0:half * T + t0 + TC]))
            p.wait("vector", "qg")
            p.op("vector", "V", lambda v:
                 v.tensor_tensor(sb["f_t0"], sb["f_ya"], sb["f_t0"], ALU.add))
            p.op("vector", "V", lambda v:
                 v.tensor_tensor(sb["f_yg"], sb["f_t0"], sb["f_yg"], ALU.mult))
            wv = p.val("V")
            p.wait("sync", "V", wv)
            p.op("sync", "qs", lambda s, m=m:
                 s.dma_start(out=dr["s_yg"][m, :, half * T + t0:half * T + t0 + TC],
                             in_=sb["f_yg"]))
        p.flush()


def _back(p, dr, sb, ps, ch, prefix):
    t0 = ch * TC
    half = 0 if prefix else 1
    xsrc = dr["xTp"] if prefix else dr["xTm"]
    KRr = lambda i: sb["k_rows"][0:1, i * TC:(i + 1) * TC]

    def dbg(key, srcfn, nblk=KB):
        if key not in dr:
            return
        wv = p.val("V")
        p.wait("sync", "V", wv)
        for m in range(nblk):
            p.op("sync", "qs", lambda s, m=m: s.dma_start(
                out=dr[key][m, :, half * T + t0:half * T + t0 + TB], in_=srcfn(m)))
        p.wait("vector", "qs")
        p.wait("scalar", "qs")
        p.wait("tensor", "qs")

    PBANK = [("pA", "pB"), ("pR", "pV"), ("pT", "pU"), ("pW", "pX")]
    w4 = sb["k_h1"]
    ygbufs = [sb["k_yg"][:, 0:TB], sb["k_r"][:, 0:TB]]
    xbufs = [sb["k_x"], sb["k_ka"].bitcast(F32)[:, 0:TB]]
    for g in range(2):
        p.wait("gpsimd", "T")
        p.wait("gpsimd", "V")
        for mi in range(4):
            p.op("gpsimd", "qc", lambda gps, g=g, mi=mi:
                 gps.dma_start(out=w4[:, mi * 2048:(mi + 1) * 2048],
                               in_=dr["sWout"][g * 4 + mi]))
        p.op("gpsimd", "qc", lambda gps: gps.dma_start(
            out=ygbufs[0], in_=dr["s_yg"][0, :, half * T + t0:half * T + t0 + TB]))
        ygq = [p.val("qc")]
        wts = []
        for k in range(IB):
            if k + 1 < IB:
                if k >= 1:
                    p.wait("gpsimd", "T", wts[k - 1])
                p.op("gpsimd", "qc", lambda gps, k=k: gps.dma_start(
                    out=ygbufs[(k + 1) % 2],
                    in_=dr["s_yg"][k + 1, :, half * T + t0:half * T + t0 + TB]))
                ygq.append(p.val("qc"))
            ygb = ygbufs[k % 2]
            p.wait("tensor", "qc", ygq[k])
            if k == 0:
                p.wait("tensor", "V")
                p.wait("tensor", "S")
            for mi in range(4):
                b0, b1 = PBANK[mi]
                wsl = w4[:, mi * 2048 + k * 128:mi * 2048 + (k + 1) * 128]
                p.op("tensor", "T", lambda t, wsl=wsl, ygb=ygb, b0=b0, k=k:
                     t.matmul(ps[b0][:, 0:TC], wsl, ygb[:, 0:TC],
                              start=(k == 0), stop=(k == IB - 1)))
                p.op("tensor", "T", lambda t, wsl=wsl, ygb=ygb, b1=b1, k=k:
                     t.matmul(ps[b1][:, 0:TC], wsl, ygb[:, TC:TB],
                              start=(k == 0), stop=(k == IB - 1)))
            wts.append(p.val("T"))
        wtall = p.val("T")
        xq = []
        for mi in range(2):
            p.op("gpsimd", "qc", lambda gps, g=g, mi=mi, b=xbufs[mi]:
                 gps.dma_start(out=b, in_=xsrc[(g * 4 + mi) * 128:(g * 4 + mi + 1) * 128,
                                               t0:t0 + TB]))
            xq.append(p.val("qc"))
        wv_ev = []
        for mi in range(4):
            m = g * 4 + mi
            b0, b1 = PBANK[mi]
            xb = xbufs[mi % 2]
            p.wait("vector", "qc", xq[mi])
            if mi == 0:
                p.wait("vector", "T", wtall)
            p.op("vector", "V", lambda v, m=m, xb=xb, b0=b0:
                 v.tensor_tensor(sb["k_ha"][:, m * TB:m * TB + TC],
                                 xb[:, 0:TC], ps[b0][:, 0:TC], ALU.add))
            p.op("vector", "V", lambda v, m=m, xb=xb, b1=b1:
                 v.tensor_tensor(sb["k_ha"][:, m * TB + TC:(m + 1) * TB],
                                 xb[:, TC:TB], ps[b1][:, 0:TC], ALU.add))
            wv_ev.append(p.val("V"))
            if mi + 2 < 4:
                p.wait("gpsimd", "V", wv_ev[mi])
                p.op("gpsimd", "qc", lambda gps, g=g, mi=mi, b=xbufs[mi % 2]:
                     gps.dma_start(out=b,
                                   in_=xsrc[(g * 4 + mi + 2) * 128:(g * 4 + mi + 3) * 128,
                                            t0:t0 + TB]))
                xq.append(p.val("qc"))
        p.flush()
    dbg("s_dbgHA", lambda m: sb["k_ha"][:, m * TB:(m + 1) * TB])

    def stats(srcfn):
        for m in range(KB):
            p.wait("scalar", "T")
            p.wait("scalar", "V")
            p.op("scalar", "S", lambda s, m=m:
                 s.activation(sb["k_dl"], srcfn(m), AF.Square))
            ws = p.val("S")
            p.wait("tensor", "S", ws)
            p.op("tensor", "T", lambda t, m=m:
                 t.matmul(ps["pA"][0:1, 0:TC], sb["c_ones"], srcfn(m),
                          start=(m == 0), stop=(m == KB - 1)))
            p.op("tensor", "T", lambda t, m=m:
                 t.matmul(ps["pA"][32:33, 0:TC], sb["c_ones32"], sb["k_dl"],
                          start=(m == 0), stop=(m == KB - 1)))

    def lnrows(o):
        wt = p.val("T")
        p.wait("vector", "T", wt)
        p.op("vector", "V", lambda v: v.tensor_scalar(KRr(o), ps["pA"][0:1, 0:TC],
                                                      1.0 / D, None, ALU.mult))
        p.op("vector", "V", lambda v: v.tensor_scalar(KRr(o + 1), ps["pA"][32:33, 0:TC],
                                                      1.0 / D, None, ALU.mult))
        p.op("vector", "V", lambda v: v.tensor_tensor(KRr(o + 2), KRr(o), KRr(o), ALU.mult))
        p.op("vector", "V", lambda v: v.tensor_tensor(KRr(o + 1), KRr(o + 1), KRr(o + 2),
                                                      ALU.subtract))
        wv = p.val("V")
        p.wait("scalar", "V", wv)
        p.op("scalar", "S", lambda s: s.activation(KRr(o + 1), KRr(o + 1), AF.Ln, bias=sb["c_eps"][0:1, 0:1]))
        p.op("scalar", "S", lambda s: s.activation(KRr(o + 1), KRr(o + 1), AF.Exp, scale=sb["c_nhalf"][0:1, 0:1]))
        ws = p.val("S")
        p.wait("vector", "S", ws)
        p.op("vector", "V", lambda v: v.tensor_tensor(KRr(o + 2), KRr(o), KRr(o + 1), ALU.mult))
        p.op("vector", "V", lambda v: v.tensor_scalar(KRr(o + 2), KRr(o + 2), -1.0,
                                                      None, ALU.mult))

    for hh in range(2):
        stats(lambda m, hh=hh: sb["k_ha"][:, m * TB + hh * TC:m * TB + (hh + 1) * TC])
        lnrows(hh * 4)
        wv = p.val("V")
        p.wait("tensor", "V", wv)
        p.op("tensor", "T", lambda t, o=hh * 4:
             t.matmul(ps["pR"][:, 0:TC], sb["c_onesr32"], KRr(o), start=True, stop=True))
        p.op("tensor", "T", lambda t, o=hh * 4:
             t.matmul(ps["pV"][:, 0:TC], sb["c_onesr32"], KRr(o + 1), start=True, stop=True))
        wt = p.val("T")
        p.wait("vector", "T", wt)
        for m in range(KB):
            p.op("vector", "V", lambda v, m=m, hh=hh:
                 v.tensor_tensor(sb["k_xm"][:, m * TB + hh * TC:m * TB + (hh + 1) * TC],
                                 sb["k_ha"][:, m * TB + hh * TC:m * TB + (hh + 1) * TC],
                                 ps["pR"][:, 0:TC], ALU.subtract))
            p.op("vector", "V", lambda v, m=m, hh=hh:
                 v.tensor_tensor(sb["k_xm"][:, m * TB + hh * TC:m * TB + (hh + 1) * TC],
                                 sb["k_xm"][:, m * TB + hh * TC:m * TB + (hh + 1) * TC],
                                 ps["pV"][:, 0:TC], ALU.mult))
        p.flush()
    dbg("s_dbgXM", lambda m: sb["k_xm"][:, m * TB:(m + 1) * TB])

    for hh in range(2):
        for tb in range(TC // 128):
            wv = p.val("V")
            p.wait("tensor", "V", wv)
            p.wait("tensor", "S")
            for k in range(KB):
                p.op("tensor", "T", lambda t, k=k, hh=hh, tb=tb:
                     t.matmul(ps["pW"][:, 0:8],
                              sb["k_xm"][:, k * TB + hh * TC + tb * 128:
                                         k * TB + hh * TC + (tb + 1) * 128],
                              sb["c_rout"][:, k * E:(k + 1) * E],
                              start=(k == 0), stop=False))
            p.op("tensor", "T", lambda t:
                 t.matmul(ps["pW"][:, 0:8], sb["c_onesr"], sb["c_ber"],
                          start=False, stop=True))
            wt = p.val("T")
            p.wait("vector", "T", wt)
            TM, TCC = sb["k_tm"], sb["k_tc"]
            if "s_dbgLG" in dr:
                gb = (half * T + t0 + hh * TC + tb * 128) // 128
                p.op("vector", "V", lambda v: v.tensor_copy(
                    sb["k_t1"][:, 8:16], ps["pW"][:, 0:8]))
                wv2 = p.val("V")
                p.wait("sync", "V", wv2)
                p.op("sync", "qs", lambda s, gb=gb: s.dma_start(
                    out=dr["s_dbgLG"][:, gb * 8:(gb + 1) * 8], in_=sb["k_t1"][:, 8:16]))
                p.wait("vector", "qs")
            p.op("vector", "V", lambda v: v.tensor_reduce(
                TCC[:, 0:1], ps["pW"][:, 0:8], axis=mybir.AxisListType.X, op=ALU.max))
            p.wait("vector", "V")
            p.op("vector", "V", lambda v: v.tensor_scalar(
                TM, ps["pW"][:, 0:8], TCC[:, 0:1], None, ALU.subtract))
            p.wait("vector", "V")
            GE = sb["k_dl"][:, 0:8]
            L2 = sb["k_dl"][:, 8:16]
            p.op("vector", "V", lambda v, GE=GE: v.tensor_scalar(
                GE, TM, 0.0, None, ALU.is_ge))
            p.wait("vector", "V")
            p.op("vector", "V", lambda v, GE=GE, L2=L2: v.tensor_scalar(
                L2, GE, -1e30, None, ALU.mult))
            p.wait("vector", "V")
            p.op("vector", "V", lambda v, L2=L2: v.tensor_tensor(L2, TM, L2, ALU.add))
            p.wait("vector", "V")
            p.op("vector", "V", lambda v, L2=L2: v.tensor_reduce(
                TCC[:, 1:2], L2, axis=mybir.AxisListType.X, op=ALU.max))
            wv = p.val("V")
            p.wait("scalar", "V", wv)
            p.op("scalar", "S", lambda s: s.activation(sb["k_t1"][:, 0:8], TM, AF.Exp))
            p.op("scalar", "S", lambda s: s.activation(TCC[:, 2:3], TCC[:, 1:2], AF.Exp))
            ws = p.val("S")
            p.wait("vector", "S", ws)
            p.op("vector", "V", lambda v, GE=GE: v.tensor_scalar(
                GE, TM, TCC[:, 1:2], None, ALU.is_ge))
            p.op("vector", "V", lambda v: v.tensor_scalar(
                TCC[:, 3:4], TCC[:, 2:3], 1.0, None, ALU.add))
            p.wait("vector", "V")
            p.op("vector", "V", lambda v: v.reciprocal(TCC[:, 3:4], TCC[:, 3:4]))
            p.wait("vector", "V")
            p.op("vector", "V", lambda v, GE=GE: v.tensor_tensor(
                TM, sb["k_t1"][:, 0:8], GE, ALU.mult))
            p.wait("vector", "V")
            p.op("vector", "V", lambda v: v.tensor_scalar(
                TM, TM, TCC[:, 3:4], None, ALU.mult))
            p.wait("vector", "V")
            p.op("vector", "V", lambda v: v.tensor_copy(sb["k_t0"][:, 0:8], TM))
            if "s_dbgT2" in dr:
                gb = (half * T + t0 + hh * TC + tb * 128) // 128
                wv2 = p.val("V")
                p.wait("sync", "V", wv2)
                p.op("sync", "qs", lambda s, gb=gb: s.dma_start(
                    out=dr["s_dbgT2"][:, gb * 24:gb * 24 + 4], in_=TCC[:, 0:4]))
                p.op("sync", "qs", lambda s, gb=gb, GE=GE: s.dma_start(
                    out=dr["s_dbgT2"][:, gb * 24 + 4:gb * 24 + 12], in_=GE))
                p.op("sync", "qs", lambda s, gb=gb: s.dma_start(
                    out=dr["s_dbgT2"][:, gb * 24 + 12:gb * 24 + 20], in_=TM))
                p.wait("vector", "qs")
            wv = p.val("V")
            p.wait("tensor", "V", wv)
            p.op("tensor", "T", lambda t:
                 t.matmul(ps["pX"][0:8, 0:128], sb["k_t0"][:, 0:8], sb["c_id"],
                          start=True, stop=True))
            wt = p.val("T")
            p.wait("vector", "T", wt)
            p.op("vector", "V", lambda v, hh=hh, tb=tb: v.tensor_copy(
                sb["k_w"][:, hh * TC + tb * 128:hh * TC + (tb + 1) * 128],
                ps["pX"][0:8, 0:128]))
            p.op("vector", "V", lambda v, hh=hh, tb=tb: v.tensor_copy(
                sb["k_wb"][:, hh * TC + tb * 128:hh * TC + (tb + 1) * 128],
                ps["pX"][0:8, 0:128]))
            p.flush()

    dbg("s_dbgW", lambda m: sb["k_w"][0:8, 0:TB], nblk=1)
    if "s_dbgRT" in dr:
        p.op("sync", "qs", lambda s: s.dma_start(out=dr["s_dbgRT"], in_=sb["c_rout"]))
        p.op("sync", "qs", lambda s: s.dma_start(out=dr["s_dbgBR"], in_=sb["c_ber"]))

    for hh in range(2):
        for m in range(KB):
            wv = p.val("V")
            p.wait("tensor", "V", wv)
            pp = ps["pT"] if m % 2 == 0 else ps["pU"]
            p.op("tensor", "T", lambda t, m=m, hh=hh, pp=pp:
                 t.matmul(pp[:, 0:TC], sb["c_b2"][:, m * 128:(m + 1) * 128],
                          sb["k_wb"][:, hh * TC:(hh + 1) * TC], start=True, stop=True))
            wt = p.val("T")
            p.wait("vector", "T", wt)
            p.op("vector", "V", lambda v, m=m, hh=hh, pp=pp:
                 v.tensor_copy(sb["k_acc"][:, m * TB + hh * TC:m * TB + (hh + 1) * TC],
                               pp[:, 0:TC]))
    p.flush()
    pf = Pref(p, [sb["wA"], sb["wB"]])
    for e in range(E):
        for hh in range(2):
            for mb in range(FB):
                pf.add(lambda g, b, e=e, mb=mb:
                       g.dma_start(out=b[:, 0:KB * 128], in_=dr["sW1"][e, mb]))
            for m in range(KB):
                pf.add(lambda g, b, e=e, m=m:
                       g.dma_start(out=b[:, 0:FB * 128], in_=dr["sW2"][m, e]))
    silb = [sb["k_t0"], sb["k_t1"].bitcast(BF16)[:, 0:TC]]
    p.wait("gpsimd", "T")
    for e in range(E):
        for hh in range(2):
            wv = p.val("V")
            p.wait("tensor", "V", wv)
            p.op("tensor", "T", lambda t, e=e, hh=hh:
                 t.matmul(ps["pR"][:, 0:TC], sb["c_oh8"][:, e * 128:(e + 1) * 128],
                          sb["k_wb"][:, hh * TC:(hh + 1) * TC],
                          start=True, stop=True))
            wt = p.val("T")
            p.wait("scalar", "T", wt)
            p.wait("scalar", "V")
            p.op("scalar", "S", lambda s: s.activation(sb["k_wr"][:, 0:TC],
                                                       ps["pR"][:, 0:TC], AF.Copy))
            ws = p.val("S")
            p.wait("tensor", "S", ws)
            ws_sil, wv_mul = [], []
            for mb in range(FB):
                wbuf = pf.consume()
                if mb >= 2:
                    p.wait("tensor", "S", ws_sil[mb - 2])
                pp = ps["pT"] if mb % 2 == 0 else ps["pU"]
                for k in range(KB):
                    p.op("tensor", "T", lambda t, k=k, wbuf=wbuf, pp=pp, hh=hh:
                         t.matmul(pp[:, 0:TC], wbuf[:, k * 128:(k + 1) * 128],
                                  sb["k_xm"][:, k * TB + hh * TC:k * TB + (hh + 1) * TC],
                                  start=(k == 0), stop=(k == KB - 1)))
                pf.done()
                wt = p.val("T")
                p.wait("scalar", "T", wt)
                if mb >= 2:
                    p.wait("scalar", "V", wv_mul[mb - 2])
                sdst = silb[mb % 2]
                p.op("scalar", "S", lambda s, pp=pp, e=e, mb=mb, sdst=sdst:
                     s.activation(sdst, pp[:, 0:TC], AF.Silu,
                                  bias=sb["c_bew1"][:, e * FB + mb:e * FB + mb + 1]))
                ws_sil.append(p.val("S"))
                p.wait("vector", "S", ws_sil[-1])
                p.op("vector", "V", lambda v, mb=mb, sdst=sdst:
                     v.tensor_tensor(sb["k_h1"][:, mb * TC:(mb + 1) * TC],
                                     sdst, sb["k_wr"][:, 0:TC], ALU.mult))
                wv_mul.append(p.val("V"))
            wv_acc = []
            for m in range(KB):
                wbuf2 = pf.consume()
                p.wait("tensor", "V", wv_acc[m - 2] if m >= 2 else wv_mul[-1])
                pp = ps["pT"] if m % 2 == 0 else ps["pU"]
                for k in range(FB):
                    p.op("tensor", "T", lambda t, k=k, wbuf2=wbuf2, pp=pp:
                         t.matmul(pp[:, 0:TC], wbuf2[:, k * 128:(k + 1) * 128],
                                  sb["k_h1"][:, k * TC:(k + 1) * TC],
                                  start=(k == 0), stop=(k == FB - 1)))
                pf.done()
                wt = p.val("T")
                p.wait("vector", "T", wt)
                p.op("vector", "V", lambda v, m=m, hh=hh, pp=pp:
                     v.tensor_tensor(sb["k_acc"][:, m * TB + hh * TC:m * TB + (hh + 1) * TC],
                                     sb["k_acc"][:, m * TB + hh * TC:m * TB + (hh + 1) * TC],
                                     pp[:, 0:TC], ALU.add))
                wv_acc.append(p.val("V"))
        p.flush()
    dbg("s_dbgAC", lambda m: sb["k_acc"][:, m * TB:(m + 1) * TB])

    xq2, wv_x2 = [], []
    for mi in range(2):
        p.op("gpsimd", "qc", lambda gps, mi=mi, b=xbufs[mi]:
             gps.dma_start(out=b, in_=xsrc[mi * 128:(mi + 1) * 128, t0:t0 + TB]))
        xq2.append(p.val("qc"))
    for m in range(KB):
        xb = xbufs[m % 2]
        p.wait("vector", "qc", xq2[m])
        p.op("vector", "V", lambda v, m=m, xb=xb:
             v.tensor_tensor(sb["k_x2a"][:, m * TB:(m + 1) * TB],
                             sb["k_ha"][:, m * TB:(m + 1) * TB], xb, ALU.add))
        p.op("vector", "V", lambda v, m=m:
             v.tensor_tensor(sb["k_x2a"][:, m * TB:(m + 1) * TB],
                             sb["k_x2a"][:, m * TB:(m + 1) * TB],
                             sb["k_acc"][:, m * TB:(m + 1) * TB], ALU.add))
        wv_x2.append(p.val("V"))
        if m + 2 < KB:
            p.wait("gpsimd", "V", wv_x2[m])
            p.op("gpsimd", "qc", lambda gps, m=m, b=xbufs[m % 2]:
                 gps.dma_start(out=b, in_=xsrc[(m + 2) * 128:(m + 3) * 128, t0:t0 + TB]))
            xq2.append(p.val("qc"))
    dbg("s_dbgX2", lambda m: sb["k_x2a"][:, m * TB:(m + 1) * TB])
    wv = p.val("V")
    p.wait("scalar", "V", wv)
    for hh in range(2):
        stats(lambda m, hh=hh: sb["k_x2a"][:, m * TB + hh * TC:m * TB + (hh + 1) * TC])
        lnrows(hh * 4)
    p.op("vector", "V", lambda v: v.tensor_copy(sb["k_rb"][0:1, 0:1], sb["st_p2"]))
    p.op("vector", "V", lambda v: v.tensor_copy(sb["k_rb"][0:1, 1:TC + 1], KRr(2)))
    p.op("vector", "V", lambda v: v.tensor_copy(sb["k_rb"][0:1, TC + 1:TB + 1], KRr(6)))
    for hh in range(2):
        o = hh * 4
        wv = p.val("V")
        p.wait("tensor", "V", wv)
        p.op("tensor", "T", lambda t, o=o:
             t.matmul(ps["pR"][:, 0:TC], sb["c_onesr32"], KRr(o + 1), start=True, stop=True))
        wt = p.val("T")
        p.wait("vector", "T", wt)
        for m in range(KB):
            p.op("vector", "V", lambda v, m=m, hh=hh:
                 v.tensor_tensor(sb["k_xs1"][:, m * W1 + 1 + hh * TC:m * W1 + 1 + (hh + 1) * TC],
                                 sb["k_x2a"][:, m * TB + hh * TC:m * TB + (hh + 1) * TC],
                                 ps["pR"][:, 0:TC], ALU.mult))
    p.op("vector", "V", lambda v:
         v.tensor_copy(sb["k_xs1"][:, 0:KB * W1:W1], sb["st_x1p"]))
    if prefix:
        p.op("vector", "V", lambda v:
             v.tensor_scalar(sb["st_x1p"], sb["k_xs1"][:, TB:KB * W1:W1],
                             sb["c_mask"][:, 0:1], None, ALU.mult))
        p.op("vector", "V", lambda v:
             v.tensor_scalar(sb["st_p2"], KRr(6)[:, TC - 1:TC], sb["c_mask"][0:1, 0:1],
                             None, ALU.mult))
        p.flush()
        return
    p.op("vector", "V", lambda v:
         v.tensor_copy(sb["st_x1p"], sb["k_xs1"][:, TB:KB * W1:W1]))
    p.op("vector", "V", lambda v: v.tensor_copy(sb["st_p2"], KRr(6)[:, TC - 1:TC]))
    dbg("s_dbgXS", lambda m: sb["k_xs1"][:, m * W1 + 1:m * W1 + 1 + TB])
    wv_xs1 = p.val("V")

    pfv = Pref(p, [sb["wA"], sb["wB"]])
    for sf in range(TB // TF):
        for m in range(KB):
            for wnm, var in (("sWr", 0), ("sWr", 1), ("sWk", 0), ("sWk", 1)):
                pfv.add(lambda g, b, wnm=wnm, var=var, m=m:
                        g.dma_start(out=b[:, 0:KB * 128], in_=dr[wnm][var, m]))
        for m in range(KB):
            pfv.add(lambda g, b, m=m:
                    g.dma_start(out=b[:, 0:KB * 128], in_=dr["sWv"][m]))
    p.wait("gpsimd", "T")
    for sf in range(TB // TF):
        c0 = sf * TF
        p.wait("vector", "qs")
        ws_r, ws_k = [], []
        for m in range(KB):
            bA = pfv.consume()
            bB = pfv.consume()
            if m >= 1:
                p.wait("tensor", "S", ws_r[m - 1])
            p.wait("tensor", "V", wv_xs1)
            for k in range(KB):
                base = k * W1
                p.op("tensor", "T", lambda t, k=k, base=base, c0=c0, bA=bA:
                     t.matmul(ps["pT"][:, 0:TF], bA[:, k * 128:(k + 1) * 128],
                              sb["k_xs1"][:, base + 1 + c0:base + 1 + c0 + TF],
                              start=(k == 0), stop=False))
                p.op("tensor", "T", lambda t, k=k, base=base, c0=c0, bB=bB:
                     t.matmul(ps["pT"][:, 0:TF], bB[:, k * 128:(k + 1) * 128],
                              sb["k_xs1"][:, base + c0:base + c0 + TF],
                              start=False, stop=False))
            p.op("tensor", "T", lambda t, m=m, c0=c0:
                 t.matmul(ps["pT"][:, 0:TF], sb["c_swr"][0:1, m * 128:(m + 1) * 128],
                          sb["k_rb"][0:1, 1 + c0:1 + c0 + TF], start=False, stop=False))
            p.op("tensor", "T", lambda t, m=m, c0=c0:
                 t.matmul(ps["pT"][:, 0:TF], sb["c_swr"][0:1, D + m * 128:D + (m + 1) * 128],
                          sb["k_rb"][0:1, c0:c0 + TF], start=False, stop=True))
            pfv.done()
            pfv.done()
            wt = p.val("T")
            p.wait("scalar", "T", wt)
            p.op("scalar", "S", lambda s, m=m:
                 s.activation(sb["k_r"][:, m * TF:(m + 1) * TF], ps["pT"][:, 0:TF],
                              AF.Sigmoid))
            ws_r.append(p.val("S"))
            bA = pfv.consume()
            bB = pfv.consume()
            if m >= 1:
                p.wait("tensor", "S", ws_k[m - 1])
            for k in range(KB):
                base = k * W1
                p.op("tensor", "T", lambda t, k=k, base=base, c0=c0, bA=bA:
                     t.matmul(ps["pU"][:, 0:TF], bA[:, k * 128:(k + 1) * 128],
                              sb["k_xs1"][:, base + 1 + c0:base + 1 + c0 + TF],
                              start=(k == 0), stop=False))
                p.op("tensor", "T", lambda t, k=k, base=base, c0=c0, bB=bB:
                     t.matmul(ps["pU"][:, 0:TF], bB[:, k * 128:(k + 1) * 128],
                              sb["k_xs1"][:, base + c0:base + c0 + TF],
                              start=False, stop=False))
            p.op("tensor", "T", lambda t, m=m, c0=c0:
                 t.matmul(ps["pU"][:, 0:TF], sb["c_swk"][0:1, m * 128:(m + 1) * 128],
                          sb["k_rb"][0:1, 1 + c0:1 + c0 + TF], start=False, stop=False))
            p.op("tensor", "T", lambda t, m=m, c0=c0:
                 t.matmul(ps["pU"][:, 0:TF], sb["c_swk"][0:1, D + m * 128:D + (m + 1) * 128],
                          sb["k_rb"][0:1, c0:c0 + TF], start=False, stop=True))
            pfv.done()
            pfv.done()
            wt = p.val("T")
            p.wait("scalar", "T", wt)
            p.op("scalar", "S", lambda s: s.activation(sb["k_t1"][:, 0:TF],
                                                       ps["pU"][:, 0:TF], AF.Relu))
            ws_k.append(p.val("S"))
            p.op("scalar", "S", lambda s, m=m:
                 s.activation(sb["k_ka"][:, m * TF:(m + 1) * TF], sb["k_t1"][:, 0:TF],
                              AF.Square))
        ws = p.val("S")
        p.wait("tensor", "S", ws)
        for m in range(KB):
            wbuf = pfv.consume()
            pp = ps["pT"] if m % 2 == 0 else ps["pU"]
            for k in range(KB):
                p.op("tensor", "T", lambda t, k=k, wbuf=wbuf, pp=pp:
                     t.matmul(pp[:, 0:TF], wbuf[:, k * 128:(k + 1) * 128],
                              sb["k_ka"][:, k * TF:(k + 1) * TF],
                              start=(k == 0), stop=(k == KB - 1)))
            pfv.done()
            wt = p.val("T")
            p.wait("vector", "T", wt)
            p.wait("vector", "S")
            p.op("vector", "V", lambda v, m=m, pp=pp:
                 v.tensor_tensor(sb["k_w2s"], pp[:, 0:TF],
                                 sb["k_r"][:, m * TF:(m + 1) * TF], ALU.mult))
            base = m * W1
            p.op("vector", "V", lambda v, base=base, c0=c0:
                 v.tensor_tensor(sb["k_w2s"], sb["k_w2s"],
                                 sb["k_xs1"][:, base + 1 + c0:base + 1 + c0 + TF], ALU.add))
            p.op("vector", "V", lambda v, m=m:
                 v.tensor_copy(sb["k_out"][:, m * TF:(m + 1) * TF], sb["k_w2s"]))
            wv = p.val("V")
            p.wait("scalar", "V", wv)
            p.op("scalar", "S", lambda s: s.activation(sb["k_t1"][:, 0:TF], sb["k_w2s"],
                                                       AF.Square))
            ws = p.val("S")
            p.wait("tensor", "S", ws)
            p.wait("tensor", "V", wv)
            p.op("tensor", "T", lambda t, m=m:
                 t.matmul(ps["pA"][0:1, 0:TF], sb["c_ones32"], sb["k_w2s"],
                          start=(m == 0), stop=(m == KB - 1)))
            p.op("tensor", "T", lambda t, m=m:
                 t.matmul(ps["pA"][32:33, 0:TF], sb["c_ones32"], sb["k_t1"][:, 0:TF],
                          start=(m == 0), stop=(m == KB - 1)))
        wt = p.val("T")
        p.wait("vector", "T", wt)
        p.op("vector", "V", lambda v: v.tensor_scalar(KRr(0)[:, 0:TF], ps["pA"][0:1, 0:TF],
                                                      1.0 / D, None, ALU.mult))
        p.op("vector", "V", lambda v: v.tensor_scalar(KRr(1)[:, 0:TF], ps["pA"][32:33, 0:TF],
                                                      1.0 / D, None, ALU.mult))
        p.op("vector", "V", lambda v: v.tensor_tensor(KRr(3)[:, 0:TF], KRr(0)[:, 0:TF],
                                                      KRr(0)[:, 0:TF], ALU.mult))
        p.op("vector", "V", lambda v: v.tensor_tensor(KRr(1)[:, 0:TF], KRr(1)[:, 0:TF],
                                                      KRr(3)[:, 0:TF], ALU.subtract))
        wv = p.val("V")
        p.wait("scalar", "V", wv)
        p.op("scalar", "S", lambda s: s.activation(KRr(1)[:, 0:TF], KRr(1)[:, 0:TF],
                                                   AF.Ln, bias=sb["c_eps"][0:1, 0:1]))
        p.op("scalar", "S", lambda s: s.activation(KRr(1)[:, 0:TF], KRr(1)[:, 0:TF],
                                                   AF.Exp, scale=sb["c_nhalf"][0:1, 0:1]))
        ws = p.val("S")
        p.wait("tensor", "S", ws)
        p.op("tensor", "T", lambda t: t.matmul(ps["pR"][:, 0:TF], sb["c_onesr32"],
                                               KRr(0)[:, 0:TF], start=True, stop=True))
        p.op("tensor", "T", lambda t: t.matmul(ps["pV"][:, 0:TF], sb["c_onesr32"],
                                               KRr(1)[:, 0:TF], start=True, stop=True))
        wt = p.val("T")
        p.wait("vector", "T", wt)
        for m in range(KB):
            p.op("vector", "V", lambda v, m=m:
                 v.tensor_tensor(sb["k_out"][:, m * TF:(m + 1) * TF],
                                 sb["k_out"][:, m * TF:(m + 1) * TF],
                                 ps["pR"][:, 0:TF], ALU.subtract))
            p.op("vector", "V", lambda v, m=m:
                 v.tensor_tensor(sb["k_out"][:, m * TF:(m + 1) * TF],
                                 sb["k_out"][:, m * TF:(m + 1) * TF],
                                 ps["pV"][:, 0:TF], ALU.mult))
            wv = p.val("V")
            p.wait("sync", "V", wv)
            p.op("sync", "qs", lambda s, m=m, c0=c0:
                 s.dma_start(out=dr["outT"][m * 128:(m + 1) * 128, t0 + c0:t0 + c0 + TF],
                             in_=sb["k_out"][:, m * TF:(m + 1) * TF]))
        p.flush()


_NC_CACHE = {}


def _get_nc():
    if "nc" not in _NC_CACHE:
        _NC_CACHE["nc"] = build()
    return _NC_CACHE["nc"]


def kernel(**inputs):
    x = np.ascontiguousarray(inputs["L0_attn_x"])

    def g(k):
        return np.ascontiguousarray(inputs[k])

    def col(v, nb):
        return np.ascontiguousarray(v.reshape(nb, 128).T)

    shared = {
        "W_in_T": np.ascontiguousarray(g("L0_attn_in_proj_w").T),
        "b_in": col(g("L0_attn_in_proj_b"), 32),
        "lmg": col(g("L0_attn_ln_mamba_g"), KB),
        "lmb": col(g("L0_attn_ln_mamba_b"), KB),
        "leg": col(g("L0_attn_ln_moe_g"), KB),
        "leb": col(g("L0_attn_ln_moe_b"), KB),
        "convw": np.ascontiguousarray(
            g("L0_attn_conv_w").reshape(IB, 128, KC).transpose(1, 0, 2).reshape(128, IB * KC)),
        "convb": col(g("L0_attn_conv_b"), IB),
        "W_x_T": np.ascontiguousarray(
            g("L0_attn_x_proj_w").T.reshape(IB, 128, 96).transpose(1, 0, 2).reshape(128, IB * 96)),
        "W_dt_T": np.ascontiguousarray(g("L0_attn_dt_proj_w").T),
        "dtb": col(g("L0_attn_dt_proj_b"), IB),
        "A_log": np.ascontiguousarray(
            g("L0_attn_A_log").reshape(IB, 128, NS).transpose(1, 0, 2).reshape(128, IB * NS)),
        "Dsk": col(g("L0_attn_D_skip"), IB),
        "W_out_T": np.ascontiguousarray(g("L0_attn_out_proj_w").T),
        "router_T": np.ascontiguousarray(
            g("L0_attn_router_w").T.reshape(KB, 128, E).transpose(1, 0, 2).reshape(128, KB * E)),
        "w1_T": np.ascontiguousarray(g("L0_attn_w1").transpose(0, 2, 1)),
        "b1": np.ascontiguousarray(g("L0_attn_b1").reshape(E * FB, 128).T),
        "w2_T": np.ascontiguousarray(g("L0_attn_w2").transpose(0, 2, 1)),
        "b2r": g("L0_attn_b2"),
        "W_r_T": np.ascontiguousarray(g("L0_ffn_W_r").T),
        "W_k_T": np.ascontiguousarray(g("L0_ffn_W_k").T),
        "W_v_T": np.ascontiguousarray(g("L0_ffn_W_v").T),
        "mur": col(g("L0_ffn_mu_r"), KB),
        "muk": col(g("L0_ffn_mu_k"), KB),
    }
    in_maps = []
    for c in range(8):
        b, h = c // 2, c % 2
        m = dict(shared)
        m["xTm"] = np.ascontiguousarray(x[b, 4096 * h:4096 * (h + 1), :].T)
        m["xTp"] = (np.ascontiguousarray(x[b, 0:4096, :].T) if h == 1
                    else np.zeros((D, T), np.float32))
        m["maskv"] = np.full((128, 1), float(h), np.float32)
        in_maps.append(m)
    nc = _get_nc()
    res = run_bass_kernel_spmd(nc, in_maps, core_ids=list(range(8)))
    out = np.empty((4, 8192, 1024), np.float32)
    for c in range(8):
        b, h = c // 2, c % 2
        out[b, 4096 * h:4096 * (h + 1), :] = res.results[c]["outT"].T
    return out


# version 08 (final)
# speedup vs baseline: 1.5620x; 1.0362x over previous
import sys
sys.path.insert(0, '/opt/trn_rl_repo')
import numpy as np
import concourse.bass as bass
import concourse.mybir as mybir
from concourse.bass_utils import run_bass_kernel_spmd

F32 = mybir.dt.float32
BF16 = mybir.dt.bfloat16
AF = mybir.ActivationFunctionType
ALU = mybir.AluOpType

D, DI, NS, DTR, KC, E, DFF = 1024, 2048, 16, 64, 4, 8, 2048
T = 4096
TC = 512
NCH = T // TC
KB, IB, FB = 8, 16, 16
TB = 2 * TC
W1 = TB + 1
TF = 256
EPS = 1e-5
ENGS = ("tensor", "vector", "scalar", "gpsimd", "sync")


class P:
    def __init__(self, nc, block):
        self.nc, self.block = nc, block
        self.q = {e: [] for e in ENGS}
        self.sem, self.cnt = {}, {}

    def new_sem(self, name):
        self.sem[name] = self.nc.alloc_semaphore(name=name)
        self.cnt[name] = 0

    def flush(self):
        for e in ENGS:
            fns = self.q[e]
            if fns:
                def runner(ex, fns=tuple(fns)):
                    for f in fns:
                        f(ex)
                getattr(self.block, e)(runner)
                self.q[e] = []

    def op(self, eng, sname, fn):
        by = 16 if sname.startswith("q") else 1
        self.cnt[sname] += by
        s = self.sem[sname]

        def g(ex, fn=fn, s=s, by=by):
            fn(ex).then_inc(s, by)
        self.q[eng].append(g)
        return self.cnt[sname]

    def val(self, name):
        return self.cnt[name]

    def wait(self, eng, sname, val=None):
        v = self.cnt[sname] if val is None else val
        if v > 0:
            s = self.sem[sname]
            self.q[eng].append(lambda ex, s=s, v=v: ex.wait_ge(s, v))


class Pref:

    def __init__(self, p, bufs):
        self.p, self.bufs = p, bufs
        self.loads, self.q, self.wts = [], [], []
        self.ni = self.ci = 0

    def add(self, fn):
        self.loads.append(fn)

    def _issue(self):
        i = self.ni
        b = self.bufs[i % len(self.bufs)]
        j = i - len(self.bufs)
        if j >= 0:
            self.p.wait("gpsimd", "T", self.wts[j])
        fn = self.loads[i]
        self.p.op("gpsimd", "qc", lambda g, fn=fn, b=b: fn(g, b))
        self.q.append(self.p.val("qc"))
        self.ni += 1

    def consume(self):
        while self.ni <= self.ci + 1 and self.ni < len(self.loads):
            j = self.ni - len(self.bufs)
            if j >= 0 and j >= len(self.wts):
                if self.ni <= self.ci:
                    raise RuntimeError("Pref underflow: done() missing")
                break
            self._issue()
        self.p.wait("tensor", "qc", self.q[self.ci])
        b = self.bufs[self.ci % len(self.bufs)]
        self.ci += 1
        return b

    def done(self):
        self.wts.append(self.p.val("T"))


def build(debug=False):
    nc = bass.Bass("TRN2", target_bir_lowering=False)
    dr, sb, ps = {}, {}, {}

    def din(name, shape, dt=F32):
        dr[name] = nc.dram_tensor(name, shape, dt, kind="ExternalInput").ap()

    def dscr(name, shape, dt):
        dr[name] = nc.dram_tensor(name, shape, dt, kind="Internal").ap()

    din("xTm", [D, T]); din("xTp", [D, T]); din("maskv", [128, 1])
    din("W_in_T", [D, 2 * DI]); din("b_in", [128, 32])
    din("lmg", [128, KB]); din("lmb", [128, KB])
    din("leg", [128, KB]); din("leb", [128, KB])
    din("convw", [128, IB * KC]); din("convb", [128, IB])
    din("W_x_T", [128, IB * 96])
    din("W_dt_T", [DTR, DI]); din("dtb", [128, IB])
    din("A_log", [128, IB * NS]); din("Dsk", [128, IB])
    din("W_out_T", [DI, D])
    din("router_T", [128, KB * E])
    din("w1_T", [E, D, DFF]); din("b1", [128, E * FB])
    din("w2_T", [E, DFF, D]); din("b2r", [8, D])
    din("W_r_T", [D, D]); din("W_k_T", [D, D]); din("W_v_T", [D, D])
    din("mur", [128, KB]); din("muk", [128, KB])
    dr["outT"] = nc.dram_tensor("outT", [D, T], F32, kind="ExternalOutput").ap()

    dscr("sWin", [32, 128, KB * 128], BF16)
    dscr("sWout", [KB, 128, IB * 128], BF16)
    dscr("sW1", [E, FB, 128, KB * 128], BF16)
    dscr("sW2", [KB, E, 128, FB * 128], BF16)
    dscr("sWr", [2, KB, 128, KB * 128], BF16)
    dscr("sWk", [2, KB, 128, KB * 128], BF16)
    dscr("sWv", [KB, 128, KB * 128], BF16)
    scrkind = "ExternalOutput" if debug else "Internal"
    dr["s_yg"] = nc.dram_tensor("s_yg", [IB, 128, 2 * T], BF16, kind=scrkind).ap()
    dr["s_sz"] = nc.dram_tensor("s_sz", [IB, 128, 2 * T], BF16, kind=scrkind).ap()
    dr["s_xcD"] = nc.dram_tensor("s_xcD", [IB, 128, 2 * T], BF16, kind=scrkind).ap()
    if debug:
        dr["s_dbgA"] = nc.dram_tensor("s_dbgA", [128, 2 * T], F32, kind="ExternalOutput").ap()
        dr["s_dbgP"] = nc.dram_tensor("s_dbgP", [128, 2 * T], F32, kind="ExternalOutput").ap()
        dr["s_dbgD"] = nc.dram_tensor("s_dbgD", [128, 2 * T], F32, kind="ExternalOutput").ap()
        dr["s_dbgU"] = nc.dram_tensor("s_dbgU", [128, 2 * T], F32, kind="ExternalOutput").ap()
        dr["s_dbgH"] = nc.dram_tensor("s_dbgH", [128, 2 * T], F32, kind="ExternalOutput").ap()
        for nm in ["s_dbgHA", "s_dbgXM", "s_dbgAC", "s_dbgX2", "s_dbgXS"]:
            dr[nm] = nc.dram_tensor(nm, [KB, 128, 2 * T], BF16, kind="ExternalOutput").ap()
        dr["s_dbgW"] = nc.dram_tensor("s_dbgW", [1, 8, 2 * T], F32, kind="ExternalOutput").ap()
        dr["s_dbgLG"] = nc.dram_tensor("s_dbgLG", [128, (2 * T) // 128 * 8], F32,
                                       kind="ExternalOutput").ap()
        dr["s_dbgRT"] = nc.dram_tensor("s_dbgRT", [128, KB * E], BF16,
                                       kind="ExternalOutput").ap()
        dr["s_dbgBR"] = nc.dram_tensor("s_dbgBR", [1, 8], BF16,
                                       kind="ExternalOutput").ap()
        dr["s_dbgT2"] = nc.dram_tensor("s_dbgT2", [128, (2 * T) // 128 * 24], F32,
                                       kind="ExternalOutput").ap()

    def sbuf(name, shape, dt):
        sb[name] = nc.alloc_sbuf_tensor(name, shape, dt).ap()

    sbuf("c_lmb", [128, KB], F32); sbuf("c_leg", [128, KB], F32)
    sbuf("c_leb", [128, KB], F32)
    sbuf("c_convw", [128, IB * KC], F32); sbuf("c_convb", [128, IB], F32)
    sbuf("c_dtb", [128, IB], F32); sbuf("c_A", [128, IB * NS], F32)
    sbuf("c_D", [128, IB], F32)
    sbuf("c_mur", [128, KB], F32); sbuf("c_muk", [128, KB], F32)
    sbuf("c_mask", [128, 1], F32)
    sbuf("c_Wx", [128, IB * 96], BF16)
    sbuf("c_Wdt", [DTR, DI], BF16)
    sbuf("c_rout", [128, KB * E], BF16)
    sbuf("c_ones", [128, 1], BF16); sbuf("c_ones32", [128, 1], F32)
    sbuf("c_eps", [128, 1], F32); sbuf("c_neg1", [128, 1], F32)
    sbuf("c_nhalf", [128, 1], F32)
    sbuf("c_onesr", [1, 128], BF16); sbuf("c_onesr32", [1, 128], F32)
    sbuf("c_sin", [1, 2 * DI], BF16)
    sbuf("c_swr", [1, 2 * D], BF16); sbuf("c_swk", [1, 2 * D], BF16)
    sbuf("c_bein", [128, 32], F32)
    sbuf("c_bew1", [128, E * FB], F32)
    sbuf("c_b2", [8, D], BF16)
    sbuf("c_ber", [1, 8], BF16)
    sbuf("st_h", [128, IB * NS], F32)
    sbuf("st_hal", [128, IB * (KC - 1)], F32)
    sbuf("st_x1p", [128, KB], BF16)
    sbuf("st_p2", [1, 1], F32)
    sbuf("wA", [128, IB * 128], BF16)
    sbuf("wB", [128, IB * 128], BF16)
    sbuf("k_yg", [128, 2 * TC], BF16)
    sbuf("k_x", [128, TB], F32)
    sbuf("k_rows", [1, 8 * TC], F32)
    sbuf("k_rb", [1, TB + 2], BF16)
    sbuf("k_w", [8, 2 * TC], F32)
    sbuf("k_wb", [8, 2 * TC], BF16)
    sbuf("k_tm", [128, 8], F32)
    sbuf("k_tc", [128, 8], F32)

    ARENA = 120 * 1024
    arena_t = nc.alloc_sbuf_tensor("arena", [128, ARENA], mybir.dt.uint8)
    arena_f32 = arena_t.ap().bitcast(F32)
    arena_bf = arena_t.ap().bitcast(BF16)
    _off = {"f": 0, "k": 0}

    def view(group, name, cols, dt):
        esz = mybir.dt.size(dt)
        off = (_off[group] + 63) // 64 * 64
        base = arena_f32 if dt == F32 else arena_bf
        sb[name] = base[:, off // esz: off // esz + cols]
        _off[group] = off + cols * esz
        assert _off[group] <= ARENA, (name, _off[group])

    view("f", "f_xk", 2 * TC, F32)
    view("f", "f_rows", 8 * TC, F32)
    view("f", "f_rb", TC, BF16)
    view("f", "f_xr1", KB * TC, BF16)
    view("f", "f_xi", TC + KC - 1, F32)
    view("f", "f_cva", TC, F32)
    view("f", "f_cvb", TC, F32)
    view("f", "f_xc", IB * TC, BF16)
    view("f", "f_proj", TC, F32)
    view("f", "f_dtr", TC, BF16)
    view("f", "f_dt", 2 * TC, F32)
    view("f", "f_dtx", 2 * TC, BF16)
    view("f", "f_Br", NS * TC, BF16)
    view("f", "f_Cr", NS * TC, BF16)
    view("f", "f_BC", TC, BF16)
    view("f", "f_dA", 2 * (TC + 2), F32)
    view("f", "f_u", 2 * (TC + 2), BF16)
    view("f", "f_h", 2 * (TC + 2), BF16)
    view("f", "f_ya", TC, F32)
    view("f", "f_t0", TC, BF16)
    view("f", "f_yg", TC, BF16)
    view("k", "k_ha", KB * TB, BF16)
    view("k", "k_xm", KB * TB, BF16)
    view("k", "k_x2a", KB * TB, BF16)
    view("k", "k_acc", KB * TB, BF16)
    view("k", "k_xs1", KB * W1, BF16)
    view("k", "k_h1", FB * TC, BF16)
    view("k", "k_r", KB * TF, BF16)
    view("k", "k_ka", KB * TF, BF16)
    view("k", "k_out", KB * TF, F32)
    view("k", "k_wr", TC, BF16)
    view("k", "k_dl", TC, F32)
    view("k", "k_t0", TC, BF16)
    view("k", "k_t1", TC, F32)
    view("k", "k_w2s", TF, F32)
    sb["f_rb"] = sb["f_rb"][0:1, :]
    sb["u_f32"] = sb["f_xc"].bitcast(F32)[:, 0:2 * DI]
    sb["u_bf"] = sb["f_Br"][:, 0:2 * DI]

    import ml_dtypes
    oh32 = np.zeros((32, 32 * 128), np.float32)
    for r in range(32):
        oh32[r, r * 128:(r + 1) * 128] = 1.0
    dr["i_oh32"] = nc.inline_tensor(oh32.astype(ml_dtypes.bfloat16), name="i_oh32").ap()
    dr["i_id"] = nc.inline_tensor(np.eye(128, dtype=ml_dtypes.bfloat16), name="i_id").ap()
    oh8 = np.zeros((8, 8 * 128), np.float32)
    for r in range(8):
        oh8[r, r * 128:(r + 1) * 128] = 1.0
    dr["i_oh8"] = nc.inline_tensor(oh8.astype(ml_dtypes.bfloat16), name="i_oh8").ap()
    sbuf("c_oh32", [32, 32 * 128], BF16)
    sbuf("c_id", [128, 128], BF16)
    sbuf("c_oh8", [8, 8 * 128], BF16)

    def psum(name, shape):
        ps[name] = nc.alloc_psum_tensor(name, shape, F32).ap()

    psum("pA", [128, 512]); psum("pB", [128, 512]); psum("pR", [128, 512])
    psum("pT", [128, 512]); psum("pU", [128, 512]); psum("pV", [128, 512])
    psum("pW", [128, 512]); psum("pX", [128, 512])

    with nc.Block() as block:
        p = P(nc, block)
        for s in ["qs", "qg", "qc", "T", "V", "S"]:
            p.new_sem(s)
        _setup(p, dr, sb, ps)
        for ch in range(NCH):
            _front(p, dr, sb, ps, ch, True)
            _scan(p, dr, sb, ps, ch, True)
        _back(p, dr, sb, ps, NCH - 2, True)
        for ch in range(0, NCH, 2):
            _front(p, dr, sb, ps, ch, False)
            _scan(p, dr, sb, ps, ch, False)
            _front(p, dr, sb, ps, ch + 1, False)
            _scan(p, dr, sb, ps, ch + 1, False)
            _back(p, dr, sb, ps, ch, False)
        p.wait("sync", "qs"); p.wait("sync", "qg"); p.wait("sync", "qc")
        p.flush()
    return nc


def _setup(p, dr, sb, ps):
    for dst, src in [("c_lmb", "lmb"), ("c_leg", "leg"), ("c_leb", "leb"),
                     ("c_convw", "convw"), ("c_convb", "convb"),
                     ("c_dtb", "dtb"), ("c_D", "Dsk"), ("c_mur", "mur"),
                     ("c_muk", "muk"), ("c_mask", "maskv"),
                     ("c_bein", "b_in"), ("c_bew1", "b1")]:
        p.op("sync", "qs", lambda s, d=sb[dst], x=dr[src]: s.dma_start(out=d, in_=x))
    for dst, src in [("c_Wx", "W_x_T"), ("c_Wdt", "W_dt_T"),
                     ("c_rout", "router_T"), ("c_b2", "b2r")]:
        p.op("gpsimd", "qg", lambda g, d=sb[dst], x=dr[src]: g.dma_start(out=d, in_=x))
    p.op("sync", "qs", lambda s: s.dma_start(out=sb["c_A"], in_=dr["A_log"]))
    p.wait("scalar", "qs")
    p.op("scalar", "S", lambda s: s.activation(sb["c_A"], sb["c_A"], AF.Exp))
    p.op("vector", "V", lambda v: v.memset(sb["c_ones32"], 1.0))
    p.op("vector", "V", lambda v: v.memset(sb["c_eps"], EPS))
    p.op("vector", "V", lambda v: v.memset(sb["c_neg1"], -1.0))
    p.op("vector", "V", lambda v: v.memset(sb["c_nhalf"], -0.5))
    p.op("vector", "V", lambda v: v.memset(sb["c_onesr32"], 1.0))
    p.op("vector", "V", lambda v: v.tensor_copy(sb["c_ones"], sb["c_ones32"]))
    p.op("vector", "V", lambda v: v.tensor_copy(sb["c_onesr"], sb["c_onesr32"]))
    for dst, srcn in [("c_oh32", "i_oh32"), ("c_id", "i_id"), ("c_oh8", "i_oh8")]:
        p.op("sync", "qs", lambda s, d=sb[dst], x=dr[srcn]: s.dma_start(out=d, in_=x))
    for z in ["st_h", "st_hal", "st_x1p", "st_p2"]:
        p.op("vector", "V", lambda v, z=z: v.memset(sb[z], 0.0))
    p.op("vector", "V", lambda v: v.memset(sb["f_dA"][:, 0:2], 0.0))
    p.op("vector", "V", lambda v: v.memset(sb["f_dA"][:, TC + 2:TC + 4], 0.0))
    p.op("vector", "V", lambda v: v.memset(sb["f_u"][:, 0:1], 0.0))
    p.op("vector", "V", lambda v: v.memset(sb["f_u"][:, TC + 2:TC + 3], 0.0))
    wv = p.val("V")
    p.wait("scalar", "V", wv)
    p.op("scalar", "S", lambda s: s.activation(sb["c_A"], sb["c_A"], AF.Copy,
                                               scale=sb["c_neg1"][:, 0:1]))
    p.flush()

    p.wait("vector", "qg")
    p.wait("vector", "qs")
    for k in range(KB):
        p.op("vector", "V", lambda v, k=k:
             v.tensor_scalar(sb["c_rout"][:, k * E:(k + 1) * E],
                             sb["c_rout"][:, k * E:(k + 1) * E],
                             sb["c_leg"][:, k:k + 1], None, ALU.mult))
    p.op("sync", "qs", lambda s: s.dma_start(out=sb["u_f32"][:, 0:KB * E],
                                             in_=dr["router_T"]))
    p.wait("tensor", "qs")
    p.wait("tensor", "V")
    for k in range(KB):
        p.op("tensor", "T", lambda t, k=k:
             t.matmul(ps["pV"][0:1, 0:8], sb["c_leb"][:, k:k + 1],
                      sb["u_f32"][:, k * E:(k + 1) * E],
                      start=(k == 0), stop=(k == KB - 1)))
    wt = p.val("T")
    p.wait("vector", "T", wt)
    p.op("vector", "V", lambda v: v.tensor_copy(sb["c_ber"], ps["pV"][0:1, 0:8]))
    p.flush()

    FR = lambda i: sb["f_rows"][0:1, i * TC:(i + 1) * TC]
    KRr = lambda i: sb["k_rows"][0:1, i * TC:(i + 1) * TC]

    def family(src3, kb, M, nmb, gain, lnb, sumdst, befdst, store_fn):
        nseg = (M + 511) // 512
        for k in range(kb):
            fb, bb = sb["u_f32"][:, 0:M], sb["u_bf"][:, 0:M]
            p.wait("sync", "V"); p.wait("sync", "T")
            p.op("sync", "qs", lambda s, a=src3(k), fb=fb: s.dma_start(out=fb, in_=a))
            p.wait("vector", "qs")
            if gain is not None:
                p.op("vector", "V", lambda v, fb=fb, bb=bb, g=gain, k=k:
                     v.tensor_scalar(bb, fb, g[:, k:k + 1], None, ALU.mult))
            else:
                p.op("vector", "V", lambda v, fb=fb, bb=bb: v.tensor_copy(bb, fb))
            wv = p.val("V")
            p.wait("tensor", "V", wv)
            if lnb is not None:
                for r in range(nseg):
                    so, sw = r * 512, min(512, M - r * 512)
                    if k == 0:
                        p.op("vector", "V", lambda v, r=r, sw=sw:
                             v.memset(KRr(r)[:, 0:sw], 0.0))
                    wvz = p.val("V")
                    p.wait("tensor", "V", wvz)
                    p.op("tensor", "T", lambda t, fb=fb, so=so, sw=sw, lb=lnb, k=k:
                         t.matmul(ps["pV"][0:1, 0:sw], lb[:, k:k + 1],
                                  fb[:, so:so + sw], start=True, stop=True))
                    wt = p.val("T")
                    p.wait("vector", "T", wt)
                    p.op("vector", "V", lambda v, r=r, sw=sw:
                         v.tensor_tensor(KRr(r)[:, 0:sw], KRr(r)[:, 0:sw],
                                         ps["pV"][0:1, 0:sw], ALU.add))
            if sumdst is not None:
                for r in range(nseg):
                    so, sw = r * 512, min(512, M - r * 512)
                    if k == 0:
                        p.op("vector", "V", lambda v, r=r, sw=sw:
                             v.memset(FR(r)[:, 0:sw], 0.0))
                    wvz = p.val("V")
                    p.wait("tensor", "V", wvz)
                    p.op("tensor", "T", lambda t, bb=bb, so=so, sw=sw:
                         t.matmul(ps["pB"][0:1, 0:sw], sb["c_ones"],
                                  bb[:, so:so + sw], start=True, stop=True))
                    wt = p.val("T")
                    p.wait("vector", "T", wt)
                    p.op("vector", "V", lambda v, r=r, sw=sw:
                         v.tensor_tensor(FR(r)[:, 0:sw], FR(r)[:, 0:sw],
                                         ps["pB"][0:1, 0:sw], ALU.add))
                    if k == kb - 1:
                        wvv = p.val("V")
                        p.wait("scalar", "V", wvv)
                        p.op("scalar", "S", lambda s, so=so, sw=sw, r=r:
                             sumdst(s, so, sw, r))
            wv2 = p.val("V")
            p.wait("sync", "V", wv2)
            p.op("sync", "qs", lambda s, o=store_fn(k), bb=bb, nmb=nmb:
                 s.dma_start(out=o.rearrange("mb p f -> p mb f"),
                             in_=bb[:, 0:nmb * 128].rearrange(
                                 "p (mb f) -> p mb f", f=128)))
            p.wait("vector", "qs")
            p.wait("vector", "T")
            p.flush()
        if lnb is not None:
            wv = p.val("V")
            p.wait("tensor", "V", wv)
            nmb2 = M // 128
            for m in range(nmb2):
                r, c = divmod(m * 128, TC)
                p.op("tensor", "T", lambda t, r=r, c=c, m=m:
                     t.matmul(ps["pV"][:, m:m + 1],
                              KRr(r)[:, c:c + 128], sb["c_onesr32"][0:1, 0:1],
                              start=True, stop=True))
            wt = p.val("T")
            p.wait("vector", "T", wt)
            p.op("vector", "V", lambda v, nmb2=nmb2, befdst=befdst:
                 v.tensor_tensor(befdst[:, 0:nmb2], befdst[:, 0:nmb2],
                                 ps["pV"][:, 0:nmb2], ALU.add))
            p.flush()

    lmg = p.nc.alloc_sbuf_tensor("tmp_lmg", [128, KB], F32).ap()
    p.op("sync", "qs", lambda s: s.dma_start(out=lmg, in_=dr["lmg"]))
    p.wait("vector", "qs")
    p.flush()

    def sum_in(s, so, sw, r):
        return s.activation(sb["c_sin"][0:1, so:so + sw], FR(r)[:, 0:sw], AF.Copy)

    family(lambda k: dr["W_in_T"][k * 128:(k + 1) * 128, :], KB, 2 * DI, 32,
           lmg, sb["c_lmb"], sum_in, sb["c_bein"],
           lambda k: dr["sWin"][:, :, k * 128:(k + 1) * 128])
    family(lambda k: dr["W_out_T"][k * 128:(k + 1) * 128, :], IB, D, KB,
           None, None, None, None,
           lambda k: dr["sWout"][:, :, k * 128:(k + 1) * 128])
    for e in range(E):
        family(lambda k, e=e: dr["w1_T"][e, k * 128:(k + 1) * 128, :], KB, DFF, FB,
               sb["c_leg"], sb["c_leb"], None,
               sb["c_bew1"][:, e * FB:(e + 1) * FB],
               lambda k, e=e: dr["sW1"][e, :, :, k * 128:(k + 1) * 128])
    for e in range(E):
        family(lambda k, e=e: dr["w2_T"][e, k * 128:(k + 1) * 128, :], FB, D, KB,
               None, None, None, None,
               lambda k, e=e: dr["sW2"][:, e, :, k * 128:(k + 1) * 128])

    for wsrc, scr, mucol, sumr in [("W_r_T", "sWr", "c_mur", "c_swr"),
                                   ("W_k_T", "sWk", "c_muk", "c_swk")]:
        for var in range(2):
            for k in range(KB):
                fb, bb = sb["u_f32"][:, 0:D], sb["u_bf"][:, 0:D]
                tmp = sb["u_f32"][:, D:2 * D]
                p.wait("sync", "V"); p.wait("sync", "T")
                p.op("sync", "qs", lambda s, k=k, fb=fb, w=wsrc:
                     s.dma_start(out=fb, in_=dr[w][k * 128:(k + 1) * 128, :]))
                p.wait("vector", "qs")
                if var == 0:
                    p.op("vector", "V", lambda v, fb=fb, bb=bb, k=k, mc=mucol:
                         v.tensor_scalar(bb, fb, sb[mc][:, k:k + 1], None, ALU.mult))
                else:
                    p.op("vector", "V", lambda v, fb=fb, k=k, mc=mucol, tmp=tmp:
                         v.tensor_scalar(tmp, fb, sb[mc][:, k:k + 1], None, ALU.mult))
                    p.op("vector", "V", lambda v, fb=fb, bb=bb, tmp=tmp:
                         v.tensor_tensor(bb, fb, tmp, ALU.subtract))
                wv = p.val("V")
                p.wait("tensor", "V", wv)
                for r in range(2):
                    so = r * 512
                    if k == 0:
                        p.op("vector", "V", lambda v, r=r: v.memset(FR(r), 0.0))
                    wvz = p.val("V")
                    p.wait("tensor", "V", wvz)
                    p.op("tensor", "T", lambda t, bb=bb, so=so:
                         t.matmul(ps["pB"][0:1, 0:512], sb["c_ones"],
                                  bb[:, so:so + 512], start=True, stop=True))
                    wt = p.val("T")
                    p.wait("vector", "T", wt)
                    p.op("vector", "V", lambda v, r=r:
                         v.tensor_tensor(FR(r), FR(r), ps["pB"][0:1, 0:512], ALU.add))
                    if k == KB - 1:
                        wvv = p.val("V")
                        p.wait("scalar", "V", wvv)
                        p.op("scalar", "S", lambda s, so=so, r=r, sumr=sumr, var=var:
                             s.activation(sb[sumr][0:1, var * D + so:var * D + so + 512],
                                          FR(r), AF.Copy))
                wv2 = p.val("V")
                p.wait("sync", "V", wv2)
                p.op("sync", "qs", lambda s, scr=scr, var=var, k=k, bb=bb:
                     s.dma_start(out=dr[scr][var, :, :, k * 128:(k + 1) * 128]
                                 .rearrange("mb p f -> p mb f"),
                                 in_=bb[:, 0:KB * 128].rearrange(
                                     "p (mb f) -> p mb f", f=128)))
                p.wait("vector", "qs")
                p.wait("vector", "T")
                p.flush()
    family(lambda k: dr["W_v_T"][k * 128:(k + 1) * 128, :], KB, D, KB,
           None, None, None, None,
           lambda k: dr["sWv"][:, :, k * 128:(k + 1) * 128])

    for e in ENGS:
        p.wait(e, "qs"); p.wait(e, "qg"); p.wait(e, "V"); p.wait(e, "S"); p.wait(e, "T")
    p.flush()


def _front(p, dr, sb, ps, ch, prefix):
    t0 = ch * TC
    half = 0 if prefix else 1
    xsrc = dr["xTp"] if prefix else dr["xTm"]
    do_z = (not prefix) or ch >= NCH - 2
    FR = lambda i: sb["f_rows"][0:1, i * TC:(i + 1) * TC]

    for k in range(KB):
        par = k % 2
        xk = sb["f_xk"][:, par * TC:(par + 1) * TC]
        p.wait("sync", "V"); p.wait("sync", "T"); p.wait("sync", "S")
        p.op("sync", "qs", lambda s, xk=xk, k=k:
             s.dma_start(out=xk, in_=xsrc[k * 128:(k + 1) * 128, t0:t0 + TC]))
        p.wait("scalar", "qs")
        p.wait("scalar", "T")
        p.op("scalar", "S", lambda s, xk=xk: s.activation(sb["f_cva"], xk, AF.Square))
        ws = p.val("S")
        p.wait("tensor", "qs")
        p.wait("tensor", "S", ws)
        p.op("tensor", "T", lambda t, xk=xk, k=k:
             t.matmul(ps["pA"][0:1, 0:TC], sb["c_ones32"], xk,
                      start=(k == 0), stop=(k == KB - 1)))
        p.op("tensor", "T", lambda t, k=k:
             t.matmul(ps["pA"][32:33, 0:TC], sb["c_ones32"], sb["f_cva"],
                      start=(k == 0), stop=(k == KB - 1)))
    wt = p.val("T")
    p.wait("vector", "T", wt)
    p.op("vector", "V", lambda v: v.tensor_scalar(FR(0), ps["pA"][0:1, 0:TC],
                                                  1.0 / D, None, ALU.mult))
    p.op("vector", "V", lambda v: v.tensor_scalar(FR(1), ps["pA"][32:33, 0:TC],
                                                  1.0 / D, None, ALU.mult))
    p.op("vector", "V", lambda v: v.tensor_tensor(FR(2), FR(0), FR(0), ALU.mult))
    p.op("vector", "V", lambda v: v.tensor_tensor(FR(2), FR(1), FR(2), ALU.subtract))
    wv = p.val("V")
    p.wait("scalar", "V", wv)
    p.op("scalar", "S", lambda s: s.activation(FR(3), FR(2), AF.Ln, bias=sb["c_eps"][0:1, 0:1]))
    p.op("scalar", "S", lambda s: s.activation(FR(3), FR(3), AF.Exp, scale=sb["c_nhalf"][0:1, 0:1]))
    ws = p.val("S")
    p.wait("vector", "S", ws)
    p.op("vector", "V", lambda v: v.tensor_tensor(FR(4), FR(0), FR(3), ALU.mult))
    p.op("vector", "V", lambda v: v.tensor_scalar(FR(4), FR(4), -1.0, None, ALU.mult))
    p.op("vector", "V", lambda v: v.tensor_copy(sb["f_rb"], FR(4)))
    wv = p.val("V")
    p.wait("tensor", "V", wv)
    p.op("tensor", "T", lambda t: t.matmul(ps["pR"][:, 0:TC], sb["c_onesr32"], FR(3),
                                           start=True, stop=True))
    wt = p.val("T")
    for k in range(KB):
        par = k % 2
        xk = sb["f_xk"][:, par * TC:(par + 1) * TC]
        p.wait("sync", "V")
        p.op("sync", "qs", lambda s, xk=xk, k=k:
             s.dma_start(out=xk, in_=xsrc[k * 128:(k + 1) * 128, t0:t0 + TC]))
        p.wait("vector", "qs")
        p.wait("vector", "T", wt)
        p.op("vector", "V", lambda v, xk=xk, k=k:
             v.tensor_tensor(sb["f_xr1"][:, k * TC:(k + 1) * TC], xk,
                             ps["pR"][:, 0:TC], ALU.mult))
    p.flush()

    mlist = list(range(IB)) + (list(range(IB, 32)) if do_z else [])
    pfi = Pref(p, [sb["wA"], sb["wB"]])
    for m in mlist:
        pfi.add(lambda g, b, m=m:
                g.dma_start(out=b[:, 0:KB * 128], in_=dr["sWin"][m]))
    p.wait("gpsimd", "T")
    wv_xr1 = p.val("V")
    ws_ev = []
    for i, m in enumerate(mlist):
        wb = pfi.consume()
        if i >= 2:
            p.wait("tensor", "S", ws_ev[i - 2])
        else:
            p.wait("tensor", "V", wv_xr1)
            p.wait("tensor", "S")
        pp = ps["pA"] if i % 2 == 0 else ps["pB"]
        for k in range(KB):
            p.op("tensor", "T", lambda t, wb=wb, k=k, pp=pp:
                 t.matmul(pp[:, 0:TC], wb[:, k * 128:(k + 1) * 128],
                          sb["f_xr1"][:, k * TC:(k + 1) * TC],
                          start=(k == 0), stop=False))
        p.op("tensor", "T", lambda t, m=m, pp=pp:
             t.matmul(pp[:, 0:TC], sb["c_sin"][0:1, m * 128:(m + 1) * 128],
                      sb["f_rb"], start=False, stop=True))
        pfi.done()
        wt = p.val("T")
        if m < IB:
            p.wait("scalar", "T", wt)
            p.wait("scalar", "V")
            p.op("scalar", "S", lambda s, pp=pp, m=m:
                 s.activation(sb["f_xi"][:, KC - 1:], pp[:, 0:TC], AF.Identity,
                              bias=sb["c_bein"][:, m:m + 1]))
            ws = p.val("S")
            ws_ev.append(ws)
            p.wait("vector", "S", ws)
            p.op("vector", "V", lambda v, m=m:
                 v.tensor_copy(sb["f_xi"][:, 0:KC - 1],
                               sb["st_hal"][:, m * (KC - 1):(m + 1) * (KC - 1)]))
            p.op("vector", "V", lambda v, m=m:
                 v.tensor_scalar(sb["f_cva"], sb["f_xi"][:, 0:TC],
                                 sb["c_convw"][:, m * KC:m * KC + 1],
                                 sb["c_convb"][:, m:m + 1], ALU.mult, ALU.add))
        if m < IB:
            for kk in range(1, KC):
                srcb, dstb = ((sb["f_cva"], sb["f_cvb"]) if kk % 2 == 1
                              else (sb["f_cvb"], sb["f_cva"]))
                p.op("vector", "V", lambda v, m=m, kk=kk, srcb=srcb, dstb=dstb:
                     v.scalar_tensor_tensor(dstb, sb["f_xi"][:, kk:kk + TC],
                                            sb["c_convw"][:, m * KC + kk:m * KC + kk + 1],
                                            srcb, ALU.mult, ALU.add))
            p.op("vector", "V", lambda v, m=m:
                 v.tensor_copy(sb["st_hal"][:, m * (KC - 1):(m + 1) * (KC - 1)],
                               sb["f_xi"][:, TC:TC + KC - 1]))
            wv = p.val("V")
            p.wait("scalar", "V", wv)
            fin = sb["f_cvb"] if KC % 2 == 0 else sb["f_cva"]
            p.op("scalar", "S", lambda s, m=m, fin=fin:
                 s.activation(sb["f_xc"][:, m * TC:(m + 1) * TC], fin, AF.Silu))
        else:
            mb = m - IB
            p.wait("scalar", "T", wt)
            p.wait("scalar", "qs")
            p.op("scalar", "S", lambda s, pp=pp, m=m:
                 s.activation(sb["f_t0"], pp[:, 0:TC], AF.Silu,
                              bias=sb["c_bein"][:, m:m + 1]))
            ws = p.val("S")
            ws_ev.append(ws)
            p.wait("sync", "S", ws)
            p.op("sync", "qs", lambda s, mb=mb:
                 s.dma_start(out=dr["s_sz"][mb, :, half * T + t0:half * T + t0 + TC],
                             in_=sb["f_t0"]))
        p.flush()

    ws = p.val("S")
    p.wait("tensor", "S", ws)
    for k in range(IB):
        p.op("tensor", "T", lambda t, k=k:
             t.matmul(ps["pR"][0:96, 0:TC], sb["c_Wx"][:, k * 96:(k + 1) * 96],
                      sb["f_xc"][:, k * TC:(k + 1) * TC],
                      start=(k == 0), stop=(k == IB - 1)))
    wt = p.val("T")
    p.wait("scalar", "T", wt)
    p.wait("scalar", "V")
    p.op("scalar", "S", lambda s: s.activation(sb["f_proj"][0:96, :],
                                               ps["pR"][0:96, 0:TC], AF.Copy))
    p.op("scalar", "S", lambda s: s.activation(sb["f_dtr"][0:DTR, :],
                                               ps["pR"][0:DTR, 0:TC], AF.Copy))
    p.op("scalar", "S", lambda s: s.activation(sb["f_BC"][0:32, :],
                                               ps["pR"][64:96, 0:TC], AF.Copy))
    ws = p.val("S")
    if "s_dbgP" in dr:
        p.wait("sync", "S", ws)
        p.op("sync", "qs", lambda s: s.dma_start(
            out=dr["s_dbgP"][0:96, half * T + t0:half * T + t0 + TC], in_=sb["f_proj"][0:96, :]))
    for m in range(IB):
        p.wait("vector", "qs")
        p.wait("vector", "S", ws)
        p.op("vector", "V", lambda v, m=m:
             v.tensor_scalar(sb["f_t0"], sb["f_xc"][:, m * TC:(m + 1) * TC],
                             sb["c_D"][:, m:m + 1], None, ALU.mult))
        wv = p.val("V")
        p.wait("sync", "V", wv)
        p.op("sync", "qs", lambda s, m=m:
             s.dma_start(out=dr["s_xcD"][m, :, half * T + t0:half * T + t0 + TC],
                         in_=sb["f_t0"]))
    p.wait("tensor", "S", ws)
    p.wait("tensor", "V")
    p.wait("scalar", "V")
    wsl = []
    for n in range(2 * NS):
        dst = sb["f_Br"] if n < NS else sb["f_Cr"]
        nn = n % NS
        col = n if n < NS else 16 + nn
        bank = "pV" if n % 2 == 0 else "pW"
        if n >= 2:
            p.wait("tensor", "S", wsl[n - 2])
        p.op("tensor", "T", lambda t, col=col, bank=bank:
             t.matmul(ps[bank][:, 0:TC], sb["c_oh32"][:, col * 128:(col + 1) * 128],
                      sb["f_BC"][0:32, :], start=True, stop=True))
        wt = p.val("T")
        p.wait("scalar", "T", wt)
        p.op("scalar", "S", lambda s, dst=dst, nn=nn, bank=bank:
             s.activation(dst[:, nn * TC:(nn + 1) * TC], ps[bank][:, 0:TC], AF.Copy))
        wsl.append(p.val("S"))
    p.flush()


def _scan(p, dr, sb, ps, ch, prefix):
    t0 = ch * TC
    half = 0 if prefix else 1
    do_y = (not prefix) or ch >= NCH - 2

    if True:
        p.op("vector", "V", lambda v: v.memset(sb["f_dA"][:, 0:2], 0.0))
        p.op("vector", "V", lambda v: v.memset(sb["f_dA"][:, TC + 2:TC + 4], 0.0))
        p.op("vector", "V", lambda v: v.memset(sb["f_u"][:, 0:1], 0.0))
        p.op("vector", "V", lambda v: v.memset(sb["f_u"][:, TC + 2:TC + 3], 0.0))
        p.wait("vector", "V")
    ws_sp = []
    for m in range(IB):
        par = m % 2
        dt = sb["f_dt"][:, par * TC:(par + 1) * TC]
        dtx = sb["f_dtx"][:, par * TC:(par + 1) * TC]
        if m >= 2:
            p.wait("tensor", "S", ws_sp[m - 2])
        else:
            p.wait("tensor", "S")
            p.wait("tensor", "V")
        pp = ps["pA"] if par == 0 else ps["pB"]
        p.op("tensor", "T", lambda t, m=m, pp=pp:
             t.matmul(pp[:, 0:TC], sb["c_Wdt"][:, m * 128:(m + 1) * 128],
                      sb["f_dtr"][0:DTR, :], start=True, stop=True))
        wt = p.val("T")
        p.wait("scalar", "T", wt)
        p.wait("scalar", "V")
        p.op("scalar", "S", lambda s, m=m, pp=pp:
             s.activation(sb["f_cva"], pp[:, 0:TC], AF.Abs, bias=sb["c_dtb"][:, m:m + 1]))
        p.op("scalar", "S", lambda s: s.activation(sb["f_cva"], sb["f_cva"], AF.Exp,
                                                   scale=sb["c_neg1"][:, 0:1]))
        p.op("scalar", "S", lambda s: s.activation(sb["f_cva"], sb["f_cva"], AF.Ln,
                                                   bias=1.0))
        p.op("scalar", "S", lambda s, m=m, pp=pp:
             s.activation(sb["f_cvb"], pp[:, 0:TC], AF.Relu, bias=sb["c_dtb"][:, m:m + 1]))
        ws = p.val("S")
        ws_sp.append(ws)
        p.wait("vector", "S", ws)
        p.op("vector", "V", lambda v, dt=dt: v.tensor_tensor(dt, sb["f_cva"],
                                                             sb["f_cvb"], ALU.add))
        mask = sb["c_mask"][:, 0:1] if prefix else 1.0
        p.op("vector", "V", lambda v, dt=dt, dtx=dtx, m=m, mask=mask:
             v.scalar_tensor_tensor(dtx, dt, mask, sb["f_xc"][:, m * TC:(m + 1) * TC],
                                    ALU.mult, ALU.mult))
        if do_y:
            p.op("vector", "V", lambda v: v.memset(sb["f_ya"], 0.0))
        wv_dt = p.val("V")
        if m == 0 and "s_dbgD" in dr:
            p.wait("sync", "V", wv_dt)
            p.op("sync", "qs", lambda s, dt=dt: s.dma_start(
                out=dr["s_dbgD"][:, half * T + t0:half * T + t0 + TC], in_=dt))
        p.wait("scalar", "V", wv_dt)
        wv_scan, ws_cp = [], []
        fu = lambda par: sb["f_u"][:, par * (TC + 2):(par + 1) * (TC + 2)]
        fh = lambda par: sb["f_h"][:, par * (TC + 2):(par + 1) * (TC + 2)]

        def emit_u(n):
            par = n % 2
            p.op("vector", "V", lambda v, dtx=dtx, n=n, par=par:
                 v.tensor_tensor(fu(par)[:, 2:], dtx,
                                 sb["f_Br"][:, n * TC:(n + 1) * TC], ALU.mult))
            p.op("vector", "V", lambda v, m=m, n=n, par=par:
                 v.tensor_copy(fu(par)[:, 1:2],
                               sb["st_h"][:, m * NS + n:m * NS + n + 1]))

        def emit_y(n):
            par = n % 2
            p.op("vector", "V", lambda v, n=n, par=par:
                 v.tensor_tensor(sb["f_t0"], fh(par)[:, 2:],
                                 sb["f_Cr"][:, n * TC:(n + 1) * TC], ALU.mult))
            p.op("vector", "V", lambda v:
                 v.tensor_tensor(sb["f_ya"], sb["f_ya"], sb["f_t0"], ALU.add))

        emit_u(0)
        p.wait("vector", "V")
        for n in range(NS):
            npar = n % 2
            dA = sb["f_dA"][:, npar * (TC + 2):(npar + 1) * (TC + 2)]
            if n >= 2:
                p.wait("scalar", "V", wv_scan[n - 2])
            p.op("scalar", "S", lambda s, dA=dA, dt=dt, m=m, n=n:
                 s.activation(dA[:, 2:], dt, AF.Exp,
                              scale=sb["c_A"][:, m * NS + n:m * NS + n + 1]))
            ws = p.val("S")
            if n + 1 < NS:
                emit_u(n + 1)
            if do_y and n >= 1:
                emit_y(n - 1)
            if n >= 2:
                p.wait("vector", "S", ws_cp[n - 2])
            p.wait("vector", "S", ws)
            p.wait("vector", "V")
            p.op("vector", "V", lambda v, dA=dA, npar=npar:
                 v.tensor_tensor_scan(fh(npar), dA, fu(npar), 0.0,
                                      ALU.mult, ALU.add))
            wv_scan.append(p.val("V"))
            p.wait("scalar", "V", wv_scan[n])
            p.op("scalar", "S", lambda s, m=m, n=n, npar=npar:
                 s.activation(sb["st_h"][:, m * NS + n:m * NS + n + 1],
                              fh(npar)[:, TC + 1:TC + 2], AF.Copy))
            ws_cp.append(p.val("S"))
            if m == 0 and n == 0 and "s_dbgU" in dr:
                wvx = p.val("V")
                p.wait("gpsimd", "V", wvx)
                p.op("gpsimd", "qg", lambda g, dA=dA: g.dma_start(
                    out=dr["s_dbgA"][:, half * T + t0:half * T + t0 + TC], in_=dA[:, 2:]))
                p.op("gpsimd", "qg", lambda g: g.dma_start(
                    out=dr["s_dbgU"][:, half * T + t0:half * T + t0 + TC], in_=fu(0)[:, 2:]))
                p.op("gpsimd", "qg", lambda g: g.dma_start(
                    out=dr["s_dbgH"][:, half * T + t0:half * T + t0 + TC], in_=fh(0)[:, 2:]))
                p.wait("vector", "qg")
        if do_y:
            p.wait("vector", "V")
            emit_y(NS - 1)
        if do_y:
            p.wait("vector", "qs")
            p.wait("gpsimd", "qs")
            p.wait("gpsimd", "V")
            p.op("gpsimd", "qg", lambda g, m=m:
                 g.dma_start(out=sb["f_t0"],
                             in_=dr["s_xcD"][m, :, half * T + t0:half * T + t0 + TC]))
            p.op("gpsimd", "qg", lambda g, m=m:
                 g.dma_start(out=sb["f_yg"],
                             in_=dr["s_sz"][m, :, half * T + t0:half * T + t0 + TC]))
            p.wait("vector", "qg")
            p.op("vector", "V", lambda v:
                 v.tensor_tensor(sb["f_t0"], sb["f_ya"], sb["f_t0"], ALU.add))
            p.op("vector", "V", lambda v:
                 v.tensor_tensor(sb["f_yg"], sb["f_t0"], sb["f_yg"], ALU.mult))
            wv = p.val("V")
            p.wait("sync", "V", wv)
            p.op("sync", "qs", lambda s, m=m:
                 s.dma_start(out=dr["s_yg"][m, :, half * T + t0:half * T + t0 + TC],
                             in_=sb["f_yg"]))
        p.flush()


def _back(p, dr, sb, ps, ch, prefix):
    t0 = ch * TC
    half = 0 if prefix else 1
    xsrc = dr["xTp"] if prefix else dr["xTm"]
    KRr = lambda i: sb["k_rows"][0:1, i * TC:(i + 1) * TC]

    def dbg(key, srcfn, nblk=KB):
        if key not in dr:
            return
        wv = p.val("V")
        p.wait("sync", "V", wv)
        for m in range(nblk):
            p.op("sync", "qs", lambda s, m=m: s.dma_start(
                out=dr[key][m, :, half * T + t0:half * T + t0 + TB], in_=srcfn(m)))
        p.wait("vector", "qs")
        p.wait("scalar", "qs")
        p.wait("tensor", "qs")

    PBANK = [("pA", "pB"), ("pR", "pV"), ("pT", "pU"), ("pW", "pX")]
    w4 = sb["k_h1"]
    ygbufs = [sb["k_yg"][:, 0:TB], sb["k_r"][:, 0:TB]]
    xbufs = [sb["k_x"], sb["k_ka"].bitcast(F32)[:, 0:TB]]
    for g in range(2):
        p.wait("gpsimd", "T")
        p.wait("gpsimd", "V")
        for mi in range(4):
            p.op("gpsimd", "qc", lambda gps, g=g, mi=mi:
                 gps.dma_start(out=w4[:, mi * 2048:(mi + 1) * 2048],
                               in_=dr["sWout"][g * 4 + mi]))
        p.op("gpsimd", "qc", lambda gps: gps.dma_start(
            out=ygbufs[0], in_=dr["s_yg"][0, :, half * T + t0:half * T + t0 + TB]))
        ygq = [p.val("qc")]
        wts = []
        for k in range(IB):
            if k + 1 < IB:
                if k >= 1:
                    p.wait("gpsimd", "T", wts[k - 1])
                p.op("gpsimd", "qc", lambda gps, k=k: gps.dma_start(
                    out=ygbufs[(k + 1) % 2],
                    in_=dr["s_yg"][k + 1, :, half * T + t0:half * T + t0 + TB]))
                ygq.append(p.val("qc"))
            ygb = ygbufs[k % 2]
            p.wait("tensor", "qc", ygq[k])
            if k == 0:
                p.wait("tensor", "V")
                p.wait("tensor", "S")
            for mi in range(4):
                b0, b1 = PBANK[mi]
                wsl = w4[:, mi * 2048 + k * 128:mi * 2048 + (k + 1) * 128]
                p.op("tensor", "T", lambda t, wsl=wsl, ygb=ygb, b0=b0, k=k:
                     t.matmul(ps[b0][:, 0:TC], wsl, ygb[:, 0:TC],
                              start=(k == 0), stop=(k == IB - 1)))
                p.op("tensor", "T", lambda t, wsl=wsl, ygb=ygb, b1=b1, k=k:
                     t.matmul(ps[b1][:, 0:TC], wsl, ygb[:, TC:TB],
                              start=(k == 0), stop=(k == IB - 1)))
            wts.append(p.val("T"))
        wtall = p.val("T")
        xq = []
        for mi in range(2):
            p.op("gpsimd", "qc", lambda gps, g=g, mi=mi, b=xbufs[mi]:
                 gps.dma_start(out=b, in_=xsrc[(g * 4 + mi) * 128:(g * 4 + mi + 1) * 128,
                                               t0:t0 + TB]))
            xq.append(p.val("qc"))
        wv_ev = []
        for mi in range(4):
            m = g * 4 + mi
            b0, b1 = PBANK[mi]
            xb = xbufs[mi % 2]
            p.wait("vector", "qc", xq[mi])
            if mi == 0:
                p.wait("vector", "T", wtall)
            p.op("vector", "V", lambda v, m=m, xb=xb, b0=b0:
                 v.tensor_tensor(sb["k_ha"][:, m * TB:m * TB + TC],
                                 xb[:, 0:TC], ps[b0][:, 0:TC], ALU.add))
            p.op("vector", "V", lambda v, m=m, xb=xb, b1=b1:
                 v.tensor_tensor(sb["k_ha"][:, m * TB + TC:(m + 1) * TB],
                                 xb[:, TC:TB], ps[b1][:, 0:TC], ALU.add))
            wv_ev.append(p.val("V"))
            if mi + 2 < 4:
                p.wait("gpsimd", "V", wv_ev[mi])
                p.op("gpsimd", "qc", lambda gps, g=g, mi=mi, b=xbufs[mi % 2]:
                     gps.dma_start(out=b,
                                   in_=xsrc[(g * 4 + mi + 2) * 128:(g * 4 + mi + 3) * 128,
                                            t0:t0 + TB]))
                xq.append(p.val("qc"))
        p.flush()
    dbg("s_dbgHA", lambda m: sb["k_ha"][:, m * TB:(m + 1) * TB])

    def stats(srcfn):
        for m in range(KB):
            p.wait("scalar", "T")
            p.wait("scalar", "V")
            p.op("scalar", "S", lambda s, m=m:
                 s.activation(sb["k_dl"], srcfn(m), AF.Square))
            ws = p.val("S")
            p.wait("tensor", "S", ws)
            p.op("tensor", "T", lambda t, m=m:
                 t.matmul(ps["pA"][0:1, 0:TC], sb["c_ones"], srcfn(m),
                          start=(m == 0), stop=(m == KB - 1)))
            p.op("tensor", "T", lambda t, m=m:
                 t.matmul(ps["pA"][32:33, 0:TC], sb["c_ones32"], sb["k_dl"],
                          start=(m == 0), stop=(m == KB - 1)))

    def lnrows(o):
        wt = p.val("T")
        p.wait("vector", "T", wt)
        p.op("vector", "V", lambda v: v.tensor_scalar(KRr(o), ps["pA"][0:1, 0:TC],
                                                      1.0 / D, None, ALU.mult))
        p.op("vector", "V", lambda v: v.tensor_scalar(KRr(o + 1), ps["pA"][32:33, 0:TC],
                                                      1.0 / D, None, ALU.mult))
        p.op("vector", "V", lambda v: v.tensor_tensor(KRr(o + 2), KRr(o), KRr(o), ALU.mult))
        p.op("vector", "V", lambda v: v.tensor_tensor(KRr(o + 1), KRr(o + 1), KRr(o + 2),
                                                      ALU.subtract))
        wv = p.val("V")
        p.wait("scalar", "V", wv)
        p.op("scalar", "S", lambda s: s.activation(KRr(o + 1), KRr(o + 1), AF.Ln, bias=sb["c_eps"][0:1, 0:1]))
        p.op("scalar", "S", lambda s: s.activation(KRr(o + 1), KRr(o + 1), AF.Exp, scale=sb["c_nhalf"][0:1, 0:1]))
        ws = p.val("S")
        p.wait("vector", "S", ws)
        p.op("vector", "V", lambda v: v.tensor_tensor(KRr(o + 2), KRr(o), KRr(o + 1), ALU.mult))
        p.op("vector", "V", lambda v: v.tensor_scalar(KRr(o + 2), KRr(o + 2), -1.0,
                                                      None, ALU.mult))

    for hh in range(2):
        stats(lambda m, hh=hh: sb["k_ha"][:, m * TB + hh * TC:m * TB + (hh + 1) * TC])
        lnrows(hh * 4)
        wv = p.val("V")
        p.wait("tensor", "V", wv)
        p.op("tensor", "T", lambda t, o=hh * 4:
             t.matmul(ps["pR"][:, 0:TC], sb["c_onesr32"], KRr(o), start=True, stop=True))
        p.op("tensor", "T", lambda t, o=hh * 4:
             t.matmul(ps["pV"][:, 0:TC], sb["c_onesr32"], KRr(o + 1), start=True, stop=True))
        wt = p.val("T")
        p.wait("vector", "T", wt)
        for m in range(KB):
            p.op("vector", "V", lambda v, m=m, hh=hh:
                 v.tensor_tensor(sb["k_xm"][:, m * TB + hh * TC:m * TB + (hh + 1) * TC],
                                 sb["k_ha"][:, m * TB + hh * TC:m * TB + (hh + 1) * TC],
                                 ps["pR"][:, 0:TC], ALU.subtract))
            p.op("vector", "V", lambda v, m=m, hh=hh:
                 v.tensor_tensor(sb["k_xm"][:, m * TB + hh * TC:m * TB + (hh + 1) * TC],
                                 sb["k_xm"][:, m * TB + hh * TC:m * TB + (hh + 1) * TC],
                                 ps["pV"][:, 0:TC], ALU.mult))
        p.flush()
    dbg("s_dbgXM", lambda m: sb["k_xm"][:, m * TB:(m + 1) * TB])

    for hh in range(2):
        for tb in range(TC // 128):
            wv = p.val("V")
            p.wait("tensor", "V", wv)
            p.wait("tensor", "S")
            for k in range(KB):
                p.op("tensor", "T", lambda t, k=k, hh=hh, tb=tb:
                     t.matmul(ps["pW"][:, 0:8],
                              sb["k_xm"][:, k * TB + hh * TC + tb * 128:
                                         k * TB + hh * TC + (tb + 1) * 128],
                              sb["c_rout"][:, k * E:(k + 1) * E],
                              start=(k == 0), stop=False))
            p.op("tensor", "T", lambda t:
                 t.matmul(ps["pW"][:, 0:8], sb["c_onesr"], sb["c_ber"],
                          start=False, stop=True))
            wt = p.val("T")
            p.wait("vector", "T", wt)
            TM, TCC = sb["k_tm"], sb["k_tc"]
            if "s_dbgLG" in dr:
                gb = (half * T + t0 + hh * TC + tb * 128) // 128
                p.op("vector", "V", lambda v: v.tensor_copy(
                    sb["k_t1"][:, 8:16], ps["pW"][:, 0:8]))
                wv2 = p.val("V")
                p.wait("sync", "V", wv2)
                p.op("sync", "qs", lambda s, gb=gb: s.dma_start(
                    out=dr["s_dbgLG"][:, gb * 8:(gb + 1) * 8], in_=sb["k_t1"][:, 8:16]))
                p.wait("vector", "qs")
            p.op("vector", "V", lambda v: v.tensor_reduce(
                TCC[:, 0:1], ps["pW"][:, 0:8], axis=mybir.AxisListType.X, op=ALU.max))
            p.wait("vector", "V")
            p.op("vector", "V", lambda v: v.tensor_scalar(
                TM, ps["pW"][:, 0:8], TCC[:, 0:1], None, ALU.subtract))
            p.wait("vector", "V")
            GE = sb["k_dl"][:, 0:8]
            L2 = sb["k_dl"][:, 8:16]
            p.op("vector", "V", lambda v, GE=GE: v.tensor_scalar(
                GE, TM, 0.0, None, ALU.is_ge))
            p.wait("vector", "V")
            p.op("vector", "V", lambda v, GE=GE, L2=L2: v.tensor_scalar(
                L2, GE, -1e30, None, ALU.mult))
            p.wait("vector", "V")
            p.op("vector", "V", lambda v, L2=L2: v.tensor_tensor(L2, TM, L2, ALU.add))
            p.wait("vector", "V")
            p.op("vector", "V", lambda v, L2=L2: v.tensor_reduce(
                TCC[:, 1:2], L2, axis=mybir.AxisListType.X, op=ALU.max))
            wv = p.val("V")
            p.wait("scalar", "V", wv)
            p.op("scalar", "S", lambda s: s.activation(sb["k_t1"][:, 0:8], TM, AF.Exp))
            p.op("scalar", "S", lambda s: s.activation(TCC[:, 2:3], TCC[:, 1:2], AF.Exp))
            ws = p.val("S")
            p.wait("vector", "S", ws)
            p.op("vector", "V", lambda v, GE=GE: v.tensor_scalar(
                GE, TM, TCC[:, 1:2], None, ALU.is_ge))
            p.op("vector", "V", lambda v: v.tensor_scalar(
                TCC[:, 3:4], TCC[:, 2:3], 1.0, None, ALU.add))
            p.wait("vector", "V")
            p.op("vector", "V", lambda v: v.reciprocal(TCC[:, 3:4], TCC[:, 3:4]))
            p.wait("vector", "V")
            p.op("vector", "V", lambda v, GE=GE: v.tensor_tensor(
                TM, sb["k_t1"][:, 0:8], GE, ALU.mult))
            p.wait("vector", "V")
            p.op("vector", "V", lambda v: v.tensor_scalar(
                TM, TM, TCC[:, 3:4], None, ALU.mult))
            p.wait("vector", "V")
            p.op("vector", "V", lambda v: v.tensor_copy(sb["k_t0"][:, 0:8], TM))
            if "s_dbgT2" in dr:
                gb = (half * T + t0 + hh * TC + tb * 128) // 128
                wv2 = p.val("V")
                p.wait("sync", "V", wv2)
                p.op("sync", "qs", lambda s, gb=gb: s.dma_start(
                    out=dr["s_dbgT2"][:, gb * 24:gb * 24 + 4], in_=TCC[:, 0:4]))
                p.op("sync", "qs", lambda s, gb=gb, GE=GE: s.dma_start(
                    out=dr["s_dbgT2"][:, gb * 24 + 4:gb * 24 + 12], in_=GE))
                p.op("sync", "qs", lambda s, gb=gb: s.dma_start(
                    out=dr["s_dbgT2"][:, gb * 24 + 12:gb * 24 + 20], in_=TM))
                p.wait("vector", "qs")
            wv = p.val("V")
            p.wait("tensor", "V", wv)
            p.op("tensor", "T", lambda t:
                 t.matmul(ps["pX"][0:8, 0:128], sb["k_t0"][:, 0:8], sb["c_id"],
                          start=True, stop=True))
            wt = p.val("T")
            p.wait("vector", "T", wt)
            p.op("vector", "V", lambda v, hh=hh, tb=tb: v.tensor_copy(
                sb["k_w"][:, hh * TC + tb * 128:hh * TC + (tb + 1) * 128],
                ps["pX"][0:8, 0:128]))
            p.op("vector", "V", lambda v, hh=hh, tb=tb: v.tensor_copy(
                sb["k_wb"][:, hh * TC + tb * 128:hh * TC + (tb + 1) * 128],
                ps["pX"][0:8, 0:128]))
            p.flush()

    dbg("s_dbgW", lambda m: sb["k_w"][0:8, 0:TB], nblk=1)
    if "s_dbgRT" in dr:
        p.op("sync", "qs", lambda s: s.dma_start(out=dr["s_dbgRT"], in_=sb["c_rout"]))
        p.op("sync", "qs", lambda s: s.dma_start(out=dr["s_dbgBR"], in_=sb["c_ber"]))

    for hh in range(2):
        for m in range(KB):
            wv = p.val("V")
            p.wait("tensor", "V", wv)
            pp = ps["pT"] if m % 2 == 0 else ps["pU"]
            p.op("tensor", "T", lambda t, m=m, hh=hh, pp=pp:
                 t.matmul(pp[:, 0:TC], sb["c_b2"][:, m * 128:(m + 1) * 128],
                          sb["k_wb"][:, hh * TC:(hh + 1) * TC], start=True, stop=True))
            wt = p.val("T")
            p.wait("vector", "T", wt)
            p.op("vector", "V", lambda v, m=m, hh=hh, pp=pp:
                 v.tensor_copy(sb["k_acc"][:, m * TB + hh * TC:m * TB + (hh + 1) * TC],
                               pp[:, 0:TC]))
    p.flush()
    pf = Pref(p, [sb["wA"], sb["wB"]])
    for e in range(E):
        for hh in range(2):
            for mb in range(FB):
                pf.add(lambda g, b, e=e, mb=mb:
                       g.dma_start(out=b[:, 0:KB * 128], in_=dr["sW1"][e, mb]))
            for m in range(KB):
                pf.add(lambda g, b, e=e, m=m:
                       g.dma_start(out=b[:, 0:FB * 128], in_=dr["sW2"][m, e]))
    silb = [sb["k_t0"], sb["k_t1"].bitcast(BF16)[:, 0:TC]]
    p.wait("gpsimd", "T")
    for e in range(E):
        for hh in range(2):
            wv = p.val("V")
            p.wait("tensor", "V", wv)
            p.op("tensor", "T", lambda t, e=e, hh=hh:
                 t.matmul(ps["pR"][:, 0:TC], sb["c_oh8"][:, e * 128:(e + 1) * 128],
                          sb["k_wb"][:, hh * TC:(hh + 1) * TC],
                          start=True, stop=True))
            wt = p.val("T")
            p.wait("scalar", "T", wt)
            p.wait("scalar", "V")
            p.op("scalar", "S", lambda s: s.activation(sb["k_wr"][:, 0:TC],
                                                       ps["pR"][:, 0:TC], AF.Copy))
            ws = p.val("S")
            p.wait("tensor", "S", ws)
            ws_sil, wv_mul = [], []
            for mb in range(FB):
                wbuf = pf.consume()
                if mb >= 2:
                    p.wait("tensor", "S", ws_sil[mb - 2])
                pp = ps["pT"] if mb % 2 == 0 else ps["pU"]
                for k in range(KB):
                    p.op("tensor", "T", lambda t, k=k, wbuf=wbuf, pp=pp, hh=hh:
                         t.matmul(pp[:, 0:TC], wbuf[:, k * 128:(k + 1) * 128],
                                  sb["k_xm"][:, k * TB + hh * TC:k * TB + (hh + 1) * TC],
                                  start=(k == 0), stop=(k == KB - 1)))
                pf.done()
                wt = p.val("T")
                p.wait("scalar", "T", wt)
                if mb >= 2:
                    p.wait("scalar", "V", wv_mul[mb - 2])
                sdst = silb[mb % 2]
                p.op("scalar", "S", lambda s, pp=pp, e=e, mb=mb, sdst=sdst:
                     s.activation(sdst, pp[:, 0:TC], AF.Silu,
                                  bias=sb["c_bew1"][:, e * FB + mb:e * FB + mb + 1]))
                ws_sil.append(p.val("S"))
                p.wait("vector", "S", ws_sil[-1])
                p.op("vector", "V", lambda v, mb=mb, sdst=sdst:
                     v.tensor_tensor(sb["k_h1"][:, mb * TC:(mb + 1) * TC],
                                     sdst, sb["k_wr"][:, 0:TC], ALU.mult))
                wv_mul.append(p.val("V"))
            wv_acc = []
            for m in range(KB):
                wbuf2 = pf.consume()
                p.wait("tensor", "V", wv_acc[m - 2] if m >= 2 else wv_mul[-1])
                pp = ps["pT"] if m % 2 == 0 else ps["pU"]
                for k in range(FB):
                    p.op("tensor", "T", lambda t, k=k, wbuf2=wbuf2, pp=pp:
                         t.matmul(pp[:, 0:TC], wbuf2[:, k * 128:(k + 1) * 128],
                                  sb["k_h1"][:, k * TC:(k + 1) * TC],
                                  start=(k == 0), stop=(k == FB - 1)))
                pf.done()
                wt = p.val("T")
                p.wait("vector", "T", wt)
                p.op("vector", "V", lambda v, m=m, hh=hh, pp=pp:
                     v.tensor_tensor(sb["k_acc"][:, m * TB + hh * TC:m * TB + (hh + 1) * TC],
                                     sb["k_acc"][:, m * TB + hh * TC:m * TB + (hh + 1) * TC],
                                     pp[:, 0:TC], ALU.add))
                wv_acc.append(p.val("V"))
        p.flush()
    dbg("s_dbgAC", lambda m: sb["k_acc"][:, m * TB:(m + 1) * TB])

    xq2, wv_x2 = [], []
    for mi in range(2):
        p.op("gpsimd", "qc", lambda gps, mi=mi, b=xbufs[mi]:
             gps.dma_start(out=b, in_=xsrc[mi * 128:(mi + 1) * 128, t0:t0 + TB]))
        xq2.append(p.val("qc"))
    for m in range(KB):
        xb = xbufs[m % 2]
        p.wait("vector", "qc", xq2[m])
        p.op("vector", "V", lambda v, m=m, xb=xb:
             v.tensor_tensor(sb["k_x2a"][:, m * TB:(m + 1) * TB],
                             sb["k_ha"][:, m * TB:(m + 1) * TB], xb, ALU.add))
        p.op("vector", "V", lambda v, m=m:
             v.tensor_tensor(sb["k_x2a"][:, m * TB:(m + 1) * TB],
                             sb["k_x2a"][:, m * TB:(m + 1) * TB],
                             sb["k_acc"][:, m * TB:(m + 1) * TB], ALU.add))
        wv_x2.append(p.val("V"))
        if m + 2 < KB:
            p.wait("gpsimd", "V", wv_x2[m])
            p.op("gpsimd", "qc", lambda gps, m=m, b=xbufs[m % 2]:
                 gps.dma_start(out=b, in_=xsrc[(m + 2) * 128:(m + 3) * 128, t0:t0 + TB]))
            xq2.append(p.val("qc"))
    dbg("s_dbgX2", lambda m: sb["k_x2a"][:, m * TB:(m + 1) * TB])
    wv = p.val("V")
    p.wait("scalar", "V", wv)
    for hh in range(2):
        stats(lambda m, hh=hh: sb["k_x2a"][:, m * TB + hh * TC:m * TB + (hh + 1) * TC])
        lnrows(hh * 4)
    p.op("vector", "V", lambda v: v.tensor_copy(sb["k_rb"][0:1, 0:1], sb["st_p2"]))
    p.op("vector", "V", lambda v: v.tensor_copy(sb["k_rb"][0:1, 1:TC + 1], KRr(2)))
    p.op("vector", "V", lambda v: v.tensor_copy(sb["k_rb"][0:1, TC + 1:TB + 1], KRr(6)))
    for hh in range(2):
        o = hh * 4
        wv = p.val("V")
        p.wait("tensor", "V", wv)
        p.op("tensor", "T", lambda t, o=o:
             t.matmul(ps["pR"][:, 0:TC], sb["c_onesr32"], KRr(o + 1), start=True, stop=True))
        wt = p.val("T")
        p.wait("vector", "T", wt)
        for m in range(KB):
            p.op("vector", "V", lambda v, m=m, hh=hh:
                 v.tensor_tensor(sb["k_xs1"][:, m * W1 + 1 + hh * TC:m * W1 + 1 + (hh + 1) * TC],
                                 sb["k_x2a"][:, m * TB + hh * TC:m * TB + (hh + 1) * TC],
                                 ps["pR"][:, 0:TC], ALU.mult))
    p.op("vector", "V", lambda v:
         v.tensor_copy(sb["k_xs1"][:, 0:KB * W1:W1], sb["st_x1p"]))
    if prefix:
        p.op("vector", "V", lambda v:
             v.tensor_scalar(sb["st_x1p"], sb["k_xs1"][:, TB:KB * W1:W1],
                             sb["c_mask"][:, 0:1], None, ALU.mult))
        p.op("vector", "V", lambda v:
             v.tensor_scalar(sb["st_p2"], KRr(6)[:, TC - 1:TC], sb["c_mask"][0:1, 0:1],
                             None, ALU.mult))
        p.flush()
        return
    p.op("vector", "V", lambda v:
         v.tensor_copy(sb["st_x1p"], sb["k_xs1"][:, TB:KB * W1:W1]))
    p.op("vector", "V", lambda v: v.tensor_copy(sb["st_p2"], KRr(6)[:, TC - 1:TC]))
    dbg("s_dbgXS", lambda m: sb["k_xs1"][:, m * W1 + 1:m * W1 + 1 + TB])
    wv_xs1 = p.val("V")

    pfv = Pref(p, [sb["wA"], sb["wB"]])
    for sf in range(TB // TF):
        for m in range(KB):
            for wnm, var in (("sWr", 0), ("sWr", 1), ("sWk", 0), ("sWk", 1)):
                pfv.add(lambda g, b, wnm=wnm, var=var, m=m:
                        g.dma_start(out=b[:, 0:KB * 128], in_=dr[wnm][var, m]))
        for m in range(KB):
            pfv.add(lambda g, b, m=m:
                    g.dma_start(out=b[:, 0:KB * 128], in_=dr["sWv"][m]))
    p.wait("gpsimd", "T")
    for sf in range(TB // TF):
        c0 = sf * TF
        p.wait("vector", "qs")
        ws_r, ws_k = [], []
        for m in range(KB):
            bA = pfv.consume()
            bB = pfv.consume()
            if m >= 1:
                p.wait("tensor", "S", ws_r[m - 1])
            p.wait("tensor", "V", wv_xs1)
            for k in range(KB):
                base = k * W1
                p.op("tensor", "T", lambda t, k=k, base=base, c0=c0, bA=bA:
                     t.matmul(ps["pT"][:, 0:TF], bA[:, k * 128:(k + 1) * 128],
                              sb["k_xs1"][:, base + 1 + c0:base + 1 + c0 + TF],
                              start=(k == 0), stop=False))
                p.op("tensor", "T", lambda t, k=k, base=base, c0=c0, bB=bB:
                     t.matmul(ps["pT"][:, 0:TF], bB[:, k * 128:(k + 1) * 128],
                              sb["k_xs1"][:, base + c0:base + c0 + TF],
                              start=False, stop=False))
            p.op("tensor", "T", lambda t, m=m, c0=c0:
                 t.matmul(ps["pT"][:, 0:TF], sb["c_swr"][0:1, m * 128:(m + 1) * 128],
                          sb["k_rb"][0:1, 1 + c0:1 + c0 + TF], start=False, stop=False))
            p.op("tensor", "T", lambda t, m=m, c0=c0:
                 t.matmul(ps["pT"][:, 0:TF], sb["c_swr"][0:1, D + m * 128:D + (m + 1) * 128],
                          sb["k_rb"][0:1, c0:c0 + TF], start=False, stop=True))
            pfv.done()
            pfv.done()
            wt = p.val("T")
            p.wait("scalar", "T", wt)
            p.op("scalar", "S", lambda s, m=m:
                 s.activation(sb["k_r"][:, m * TF:(m + 1) * TF], ps["pT"][:, 0:TF],
                              AF.Sigmoid))
            ws_r.append(p.val("S"))
            bA = pfv.consume()
            bB = pfv.consume()
            if m >= 1:
                p.wait("tensor", "S", ws_k[m - 1])
            for k in range(KB):
                base = k * W1
                p.op("tensor", "T", lambda t, k=k, base=base, c0=c0, bA=bA:
                     t.matmul(ps["pU"][:, 0:TF], bA[:, k * 128:(k + 1) * 128],
                              sb["k_xs1"][:, base + 1 + c0:base + 1 + c0 + TF],
                              start=(k == 0), stop=False))
                p.op("tensor", "T", lambda t, k=k, base=base, c0=c0, bB=bB:
                     t.matmul(ps["pU"][:, 0:TF], bB[:, k * 128:(k + 1) * 128],
                              sb["k_xs1"][:, base + c0:base + c0 + TF],
                              start=False, stop=False))
            p.op("tensor", "T", lambda t, m=m, c0=c0:
                 t.matmul(ps["pU"][:, 0:TF], sb["c_swk"][0:1, m * 128:(m + 1) * 128],
                          sb["k_rb"][0:1, 1 + c0:1 + c0 + TF], start=False, stop=False))
            p.op("tensor", "T", lambda t, m=m, c0=c0:
                 t.matmul(ps["pU"][:, 0:TF], sb["c_swk"][0:1, D + m * 128:D + (m + 1) * 128],
                          sb["k_rb"][0:1, c0:c0 + TF], start=False, stop=True))
            pfv.done()
            pfv.done()
            wt = p.val("T")
            p.wait("scalar", "T", wt)
            p.op("scalar", "S", lambda s: s.activation(sb["k_t1"][:, 0:TF],
                                                       ps["pU"][:, 0:TF], AF.Relu))
            ws_k.append(p.val("S"))
            p.op("scalar", "S", lambda s, m=m:
                 s.activation(sb["k_ka"][:, m * TF:(m + 1) * TF], sb["k_t1"][:, 0:TF],
                              AF.Square))
        ws = p.val("S")
        p.wait("tensor", "S", ws)
        for m in range(KB):
            wbuf = pfv.consume()
            pp = ps["pT"] if m % 2 == 0 else ps["pU"]
            for k in range(KB):
                p.op("tensor", "T", lambda t, k=k, wbuf=wbuf, pp=pp:
                     t.matmul(pp[:, 0:TF], wbuf[:, k * 128:(k + 1) * 128],
                              sb["k_ka"][:, k * TF:(k + 1) * TF],
                              start=(k == 0), stop=(k == KB - 1)))
            pfv.done()
            wt = p.val("T")
            p.wait("vector", "T", wt)
            p.wait("vector", "S")
            p.op("vector", "V", lambda v, m=m, pp=pp:
                 v.tensor_tensor(sb["k_w2s"], pp[:, 0:TF],
                                 sb["k_r"][:, m * TF:(m + 1) * TF], ALU.mult))
            base = m * W1
            p.op("vector", "V", lambda v, base=base, c0=c0:
                 v.tensor_tensor(sb["k_w2s"], sb["k_w2s"],
                                 sb["k_xs1"][:, base + 1 + c0:base + 1 + c0 + TF], ALU.add))
            p.op("vector", "V", lambda v, m=m:
                 v.tensor_copy(sb["k_out"][:, m * TF:(m + 1) * TF], sb["k_w2s"]))
            wv = p.val("V")
            p.wait("scalar", "V", wv)
            p.op("scalar", "S", lambda s: s.activation(sb["k_t1"][:, 0:TF], sb["k_w2s"],
                                                       AF.Square))
            ws = p.val("S")
            p.wait("tensor", "S", ws)
            p.wait("tensor", "V", wv)
            p.op("tensor", "T", lambda t, m=m:
                 t.matmul(ps["pA"][0:1, 0:TF], sb["c_ones32"], sb["k_w2s"],
                          start=(m == 0), stop=(m == KB - 1)))
            p.op("tensor", "T", lambda t, m=m:
                 t.matmul(ps["pA"][32:33, 0:TF], sb["c_ones32"], sb["k_t1"][:, 0:TF],
                          start=(m == 0), stop=(m == KB - 1)))
        wt = p.val("T")
        p.wait("vector", "T", wt)
        p.op("vector", "V", lambda v: v.tensor_scalar(KRr(0)[:, 0:TF], ps["pA"][0:1, 0:TF],
                                                      1.0 / D, None, ALU.mult))
        p.op("vector", "V", lambda v: v.tensor_scalar(KRr(1)[:, 0:TF], ps["pA"][32:33, 0:TF],
                                                      1.0 / D, None, ALU.mult))
        p.op("vector", "V", lambda v: v.tensor_tensor(KRr(3)[:, 0:TF], KRr(0)[:, 0:TF],
                                                      KRr(0)[:, 0:TF], ALU.mult))
        p.op("vector", "V", lambda v: v.tensor_tensor(KRr(1)[:, 0:TF], KRr(1)[:, 0:TF],
                                                      KRr(3)[:, 0:TF], ALU.subtract))
        wv = p.val("V")
        p.wait("scalar", "V", wv)
        p.op("scalar", "S", lambda s: s.activation(KRr(1)[:, 0:TF], KRr(1)[:, 0:TF],
                                                   AF.Ln, bias=sb["c_eps"][0:1, 0:1]))
        p.op("scalar", "S", lambda s: s.activation(KRr(1)[:, 0:TF], KRr(1)[:, 0:TF],
                                                   AF.Exp, scale=sb["c_nhalf"][0:1, 0:1]))
        ws = p.val("S")
        p.wait("tensor", "S", ws)
        p.op("tensor", "T", lambda t: t.matmul(ps["pR"][:, 0:TF], sb["c_onesr32"],
                                               KRr(0)[:, 0:TF], start=True, stop=True))
        p.op("tensor", "T", lambda t: t.matmul(ps["pV"][:, 0:TF], sb["c_onesr32"],
                                               KRr(1)[:, 0:TF], start=True, stop=True))
        wt = p.val("T")
        p.wait("vector", "T", wt)
        for m in range(KB):
            p.op("vector", "V", lambda v, m=m:
                 v.tensor_tensor(sb["k_out"][:, m * TF:(m + 1) * TF],
                                 sb["k_out"][:, m * TF:(m + 1) * TF],
                                 ps["pR"][:, 0:TF], ALU.subtract))
            p.op("vector", "V", lambda v, m=m:
                 v.tensor_tensor(sb["k_out"][:, m * TF:(m + 1) * TF],
                                 sb["k_out"][:, m * TF:(m + 1) * TF],
                                 ps["pV"][:, 0:TF], ALU.mult))
            wv = p.val("V")
            p.wait("sync", "V", wv)
            p.op("sync", "qs", lambda s, m=m, c0=c0:
                 s.dma_start(out=dr["outT"][m * 128:(m + 1) * 128, t0 + c0:t0 + c0 + TF],
                             in_=sb["k_out"][:, m * TF:(m + 1) * TF]))
        p.flush()


_NC_CACHE = {}


def _get_nc():
    if "nc" not in _NC_CACHE:
        _NC_CACHE["nc"] = build()
    return _NC_CACHE["nc"]


def kernel(**inputs):
    x = np.ascontiguousarray(inputs["L0_attn_x"])

    def g(k):
        return np.ascontiguousarray(inputs[k])

    def col(v, nb):
        return np.ascontiguousarray(v.reshape(nb, 128).T)

    shared = {
        "W_in_T": np.ascontiguousarray(g("L0_attn_in_proj_w").T),
        "b_in": col(g("L0_attn_in_proj_b"), 32),
        "lmg": col(g("L0_attn_ln_mamba_g"), KB),
        "lmb": col(g("L0_attn_ln_mamba_b"), KB),
        "leg": col(g("L0_attn_ln_moe_g"), KB),
        "leb": col(g("L0_attn_ln_moe_b"), KB),
        "convw": np.ascontiguousarray(
            g("L0_attn_conv_w").reshape(IB, 128, KC).transpose(1, 0, 2).reshape(128, IB * KC)),
        "convb": col(g("L0_attn_conv_b"), IB),
        "W_x_T": np.ascontiguousarray(
            g("L0_attn_x_proj_w").T.reshape(IB, 128, 96).transpose(1, 0, 2).reshape(128, IB * 96)),
        "W_dt_T": np.ascontiguousarray(g("L0_attn_dt_proj_w").T),
        "dtb": col(g("L0_attn_dt_proj_b"), IB),
        "A_log": np.ascontiguousarray(
            g("L0_attn_A_log").reshape(IB, 128, NS).transpose(1, 0, 2).reshape(128, IB * NS)),
        "Dsk": col(g("L0_attn_D_skip"), IB),
        "W_out_T": np.ascontiguousarray(g("L0_attn_out_proj_w").T),
        "router_T": np.ascontiguousarray(
            g("L0_attn_router_w").T.reshape(KB, 128, E).transpose(1, 0, 2).reshape(128, KB * E)),
        "w1_T": np.ascontiguousarray(g("L0_attn_w1").transpose(0, 2, 1)),
        "b1": np.ascontiguousarray(g("L0_attn_b1").reshape(E * FB, 128).T),
        "w2_T": np.ascontiguousarray(g("L0_attn_w2").transpose(0, 2, 1)),
        "b2r": g("L0_attn_b2"),
        "W_r_T": np.ascontiguousarray(g("L0_ffn_W_r").T),
        "W_k_T": np.ascontiguousarray(g("L0_ffn_W_k").T),
        "W_v_T": np.ascontiguousarray(g("L0_ffn_W_v").T),
        "mur": col(g("L0_ffn_mu_r"), KB),
        "muk": col(g("L0_ffn_mu_k"), KB),
    }
    in_maps = []
    for c in range(8):
        b, h = c // 2, c % 2
        m = dict(shared)
        m["xTm"] = np.ascontiguousarray(x[b, 4096 * h:4096 * (h + 1), :].T)
        m["xTp"] = (np.ascontiguousarray(x[b, 0:4096, :].T) if h == 1
                    else np.zeros((D, T), np.float32))
        m["maskv"] = np.full((128, 1), float(h), np.float32)
        in_maps.append(m)
    nc = _get_nc()
    res = run_bass_kernel_spmd(nc, in_maps, core_ids=list(range(8)))
    out = np.empty((4, 8192, 1024), np.float32)
    for c in range(8):
        b, h = c // 2, c % 2
        out[b, 4096 * h:4096 * (h + 1), :] = res.results[c]["outT"].T
    return out
